# version 10
# baseline (speedup 1.0000x reference)
from dataclasses import dataclass
from contextlib import ExitStack
import numpy as np
import ml_dtypes

import concourse.bass as bass
import concourse.mybir as mybir

F32 = mybir.dt.float32
BF16 = mybir.dt.bfloat16
AF = mybir.ActivationFunctionType
ALU = mybir.AluOpType
AX = mybir.AxisListType

P = 128


@dataclass
class Cfg:
    T: int = 2048
    D: int = 2048
    NH: int = 16
    NG: int = 4
    E: int = 16
    DI: int = 1792
    N_HALVES: int = 2
    WQB: int = 256
    GIB: int = 128
    CAP: int = 0

    @property
    def TT(self): return self.T // P
    @property
    def HC(self): return self.D // P
    @property
    def GS(self): return self.NH // self.NG
    @property
    def DKV(self): return self.NG * P
    @property
    def IC(self): return self.DI // P
    @property
    def TH(self): return self.T // self.N_HALVES
    @property
    def TTH(self): return self.TH // P
    @property
    def CT(self): return self.CAP // P


def chunks(total, step):
    o, out = 0, []
    while o < total:
        out.append((o, min(step, total - o)))
        o += step
    return out


class Sem:
    __slots__ = ("name", "n", "handle")

    def __init__(self, name, handle):
        self.name, self.n, self.handle = name, 0, handle


class Sched:
    ENGINES = ("tensor", "vector", "scalar", "sync", "gpsimd")

    def __init__(self, nc, ctx):
        self.nc, self.ctx = nc, ctx
        self.ops = {e: [] for e in self.ENGINES}

    def sem(self, name):
        return Sem(name, self.ctx.enter_context(self.nc.semaphore(name)))

    def op(self, eng, fn, waits=(), inc=None, inc_by=1, defer=None):
        if inc is not None:
            inc.n += inc_by
        rec = (tuple((s, v) for (s, v) in waits if v > 0), fn, inc, inc_by)
        (self.ops[eng] if defer is None else defer).append(rec)
        return inc.n if inc else 0

    def splice(self, eng, deferred, k=10 ** 9):
        for _ in range(min(k, len(deferred))):
            self.ops[eng].append(deferred.pop(0))

    def playback(self, block):
        def make(name):
            def run(eng):
                last = {}
                for waits, fn, inc, inc_by in self.ops[name]:
                    for (s, v) in waits:
                        if last.get(id(s), -1) < v:
                            eng.wait_ge(s.handle, v)
                            last[id(s)] = v
                    ins = fn(eng)
                    if inc is not None:
                        ins.then_inc(inc.handle, inc_by)
            return run
        block.tensor(make("tensor"))
        block.vector(make("vector"))
        block.scalar(make("scalar"))
        block.sync(make("sync"))
        block.gpsimd(make("gpsimd"))


class Builder:
    def __init__(self, cfg: Cfg, stop_after="moe", qt_limit=None, head_limit=None, no_stage4=False):
        self.c = cfg
        self.stop_after = stop_after
        self.qt_limit = qt_limit
        self.head_limit = head_limit
        self.no_stage4 = no_stage4
        self.nc = bass.Bass("TRN2", target_bir_lowering=False)

    def build(self):
        c, nc = self.c, self.nc
        self.x_d = nc.dram_tensor("x", [c.T, c.D], F32, kind="ExternalInput").ap()
        self.wq_d = nc.dram_tensor("wq", [c.D, c.D], F32, kind="ExternalInput").ap()
        self.wk_d = nc.dram_tensor("wk", [c.D, c.DKV], F32, kind="ExternalInput").ap()
        self.wv_d = nc.dram_tensor("wv", [c.D, c.DKV], F32, kind="ExternalInput").ap()
        self.wo_d = nc.dram_tensor("wo", [c.D, c.D], F32, kind="ExternalInput").ap()
        self.bo_d = nc.dram_tensor("bo", [1, c.D], F32, kind="ExternalInput").ap()
        self.rw_d = nc.dram_tensor("rw", [c.E, c.D], F32, kind="ExternalInput").ap()
        self.eb_d = nc.dram_tensor("eb", [1, c.E], F32, kind="ExternalInput").ap()
        self.gate_d = nc.dram_tensor("gate", [c.E, c.D, c.DI], F32, kind="ExternalInput").ap()
        self.up_d = nc.dram_tensor("up", [c.E, c.D, c.DI], F32, kind="ExternalInput").ap()
        self.down_d = nc.dram_tensor("down", [c.E, c.DI, c.D], F32, kind="ExternalInput").ap()
        self.y_d = nc.dram_tensor("y", [c.T, c.D], F32, kind="ExternalOutput").ap()
        self.x2_d = nc.dram_tensor("x2s", [c.T, c.D], F32, kind="Internal").ap()
        self.qt_d = nc.dram_tensor("qts", [c.D, c.T], BF16, kind="Internal").ap()
        self.xn2_d = nc.dram_tensor("xn2s", [c.D, c.T], BF16, kind="Internal").ap()
        if c.CAP:
            self.xntm_d = nc.dram_tensor("xntm", [c.T, c.D], BF16, kind="Internal").ap()
            self.wr_d = nc.dram_tensor("wrd", [c.T, c.E], F32, kind="Internal").ap()
            self.idx_d = nc.dram_tensor("idxd", [c.E * c.CAP, 1], mybir.dt.int32, kind="Internal").ap()
            self.tot_d = nc.dram_tensor("totd", [c.TT, c.E], F32, kind="Internal").ap()
            self.off_d = nc.dram_tensor("offd", [1, c.TT * c.E], F32, kind="Internal").ap()

        self.ident_d = nc.inline_tensor(np.eye(P, dtype=ml_dtypes.bfloat16), "identc")
        self.identf_d = nc.inline_tensor(np.eye(P, dtype=np.float32), "identfc")
        m01 = np.triu(np.ones((P, P))).astype(ml_dtypes.bfloat16)
        self.mask_d = nc.inline_tensor(m01, "mask01c")
        mA = np.where(np.triu(np.ones((P, P), bool), 1), np.float32(-1e30), np.float32(0.0))
        self.maskA_d = nc.inline_tensor(mA.astype(np.float32), "maskAc")
        if self.c.CAP:
            self.m01f_d = nc.inline_tensor(np.triu(np.ones((P, P), np.float32)), "m01fc")
            self.s01f_d = nc.inline_tensor(np.triu(np.ones((P, P), np.float32), 1), "s01fc")
            iota = (np.arange(P)[:, None] + P * np.arange(self.c.TT)[None, :]).astype(np.int32)
            self.iota_d = nc.inline_tensor(iota, "iotac")
            ecr = (np.arange(self.c.E, dtype=np.float32) * self.c.CAP)[None, :]
            self.ecr_d = nc.inline_tensor(ecr, "ecrc")
            self.sent_d = nc.inline_tensor(np.full((P, self.c.E * self.c.CAP // P), self.c.T, np.int32), "sentc")
            thr = np.tile(((np.arange(self.c.E) + 1) * self.c.CAP).astype(np.float32), (P, 1))
            self.thr_d = nc.inline_tensor(thr, "thrc")

        self.ctx = ExitStack()
        with self.ctx:
            self.sc = Sched(nc, self.ctx)
            self.mk_sems()
            self.scope0()
            kv_guards = self.enter_kv_scope()
            ln_guards = self.enter_ln_scope()
            self.phase_consts()
            self.phase_ln1()
            if self.stop_after == "ln":
                self.sc.op("sync", lambda e: e.nop(),
                           waits=[(self.s_xn1, self.ln1_done), (self.s_mm, self.s_mm.n)])
            else:
                self.phase_qkv()
            for g in reversed(ln_guards):
                g.__exit__(None, None, None)
            if self.stop_after == "qkv":
                self.sc.op("sync", lambda e: e.nop(),
                           waits=[(self.s_ev, self.qkv_ev_done),
                                  (self.s_q0, self.qst_war[0]), (self.s_q1, self.qst_war[1])])
            if self.stop_after in ("attn", "moe"):
                at_guards = self.enter_attn_scope()
                self.phase_attention()
                for g in reversed(at_guards):
                    g.__exit__(None, None, None)
            for g in reversed(kv_guards):
                g.__exit__(None, None, None)
            if self.stop_after == "attn":
                self.sc.op("sync", lambda e: e.nop(),
                           waits=[(self.s_l2, self.w_done), (self.s_x2st, self.attn_x2),
                                  (self.s_xbst, self.attn_xb), (self.s_sc, self.s_sc.n),
                                  (self.s_tr, self.s_tr.n), (self.s_vc, self.s_vc.n)])
            if self.stop_after == "moe":
                if self.c.CAP:
                    moe_guards = self.enter_moe_scope_sparse()
                    self.phase_moe_sparse()
                    for g in reversed(moe_guards):
                        g.__exit__(None, None, None)
                    self.sc.op("sync", lambda e: e.nop(),
                               waits=[(self.s_ss0, self.moe_sst[0]),
                                      (self.s_ss1, self.moe_sst[1]),
                                      (self.s_x2st, self.attn_x2)])
                else:
                    moe_guards = self.enter_moe_scope()
                    self.phase_moe()
                    for g in reversed(moe_guards):
                        g.__exit__(None, None, None)
                    self.sc.op("sync", lambda e: e.nop(),
                               waits=[(self.s_yst, self.s_yst.n)])
            block = self.ctx.enter_context(nc.Block())
            self.sc.playback(block)
        return nc

    def mk_sems(self):
        s = self.sc
        for name in ("s_x s_xn1 s_wld s_mm s_ev s_vc s_sc s_p1 s_rm s_nr s_p2 "
                     "s_pe s_pb s_cs s_ri s_cx s_ce s_wo s_x2 s_l2 s_tr s_te "
                     "s_st s_qs s_wg s_gu s_ae s_si s_dn s_de s_yi s_xl s_ms "
                     "s_cst s_bo s_x0 s_x1 s_wol s_w0 s_w1 s_q0 s_q1 s_qs0 s_qs1 "
                     "s_tm s_kp s_rb s_sx s_ix s_gx s_tp s_sst s_ycb s_ps "
                     "s_bc s_wa s_wb s_yc0 s_yc1 s_gw0 s_gw1 s_gx0 s_gx1 s_ss0 s_ss1 s_sx0 s_sx1 s_yo0 s_yo1 "
                     "s_x2st s_xbst s_yst s_g0 s_g1 s_dw s_cw").split():
            setattr(self, name, s.sem(name))

    def scope0(self):
        nc, c, ctx = self.nc, self.c, self.ctx
        SB = lambda n, s, d: ctx.enter_context(nc.sbuf_tensor(n, s, d))
        self.w_all = SB("w_allsb", [P, c.TT * c.E], F32)
        self.ident = SB("identsb", [P, P], BF16)
        self.pw = ctx.enter_context(nc.psum_tensor("pw", [P, 512], F32))

    def phase_consts(self):
        sc, c = self.sc, self.c
        sc.op("sync", lambda e: e.dma_start(self.ident[:], self.ident_d.ap()), inc=self.s_cst, inc_by=16)
        sc.op("sync", lambda e: e.dma_start(self.identf[:], self.identf_d.ap()), inc=self.s_cst, inc_by=16)
        sc.op("sync", lambda e: e.dma_start(self.maskT[:], self.mask_d.ap()), inc=self.s_cst, inc_by=16)
        sc.op("sync", lambda e: e.dma_start(self.maskA[:], self.maskA_d.ap()), inc=self.s_cst, inc_by=16)
        sc.op("sync", lambda e: e.dma_start(self.eb_f[:], self.eb_d), inc=self.s_cst, inc_by=16)
        sc.op("sync", lambda e: e.dma_start(self.rw_s[:], self.rw_d), inc=self.s_cst, inc_by=16)
        self.w_const = self.s_cst.n
        sc.op("gpsimd", lambda e: e.dma_start(self.bo_bf[:], self.bo_d), inc=self.s_bo, inc_by=16)
        sc.op("gpsimd", lambda e: e.memset(self.ones_r1[:], 1.0))
        sc.op("gpsimd", lambda e: e.memset(self.ones_c[:], 1.0))
        sc.op("gpsimd", lambda e: e.memset(self.ones_cf[:], 1.0))
        sc.op("gpsimd", lambda e: e.memset(self.ones_rf[:], 1.0))
        sc.op("gpsimd", lambda e: e.memset(self.eps_c[:], 1e-5), inc=self.s_ms, inc_by=1)
        self.w_ones = self.s_ms.n
        for hc in range(c.HC):
            v = sc.op("tensor", lambda e, hc=hc: e.transpose(
                self.pw[:, 0:c.E], self.rw_s[:, hc * P:(hc + 1) * P],
                self.identf[0:c.E, 0:c.E]),
                waits=[(self.s_cst, self.w_const), (self.s_ms, self.w_ones),
                       (self.s_ev, self.s_ev.n)],
                inc=self.s_mm)
            sc.op("scalar", lambda e, hc=hc: e.copy(
                self.rwT[:, hc * c.E:(hc + 1) * c.E], self.pw[:, 0:c.E]),
                waits=[(self.s_mm, v)], inc=self.s_ev)
        self.rwT_done = self.s_ev.n

    def enter_kv_scope(self):
        nc, c = self.nc, self.c
        gs = [nc.sbuf_tensor("kT", [P, c.NG * c.T], BF16),
              nc.sbuf_tensor("vN", [P, c.TT * c.DKV], BF16),
              nc.sbuf_tensor("xring", [P, 2 * c.D], F32),
              nc.sbuf_tensor("stat", [P, 48], F32),
              nc.sbuf_tensor("xnorm", [P, c.D], F32),
              nc.sbuf_tensor("identfsb", [P, P], F32),
              nc.sbuf_tensor("masksb", [P, P], BF16),
              nc.sbuf_tensor("maskAsb", [P, P], F32),
              nc.sbuf_tensor("ones_r1", [1, P], BF16),
              nc.sbuf_tensor("ones_c", [P, 1], BF16),
              nc.sbuf_tensor("ones_cf", [P, 1], F32),
              nc.sbuf_tensor("ones_rf", [1, P], F32),
              nc.sbuf_tensor("eps_c", [P, 1], F32),
              nc.sbuf_tensor("bo_bf", [1, c.D], BF16),
              nc.sbuf_tensor("eb_f", [1, c.E], F32),
              nc.sbuf_tensor("rwT", [P, c.HC * c.E], F32)]
        (self.kT, self.vN, self.xring, self.stat, self.xnorm,
         self.identf, self.maskT, self.maskA, self.ones_r1, self.ones_c,
         self.ones_cf, self.ones_rf, self.eps_c, self.bo_bf, self.eb_f,
         self.rwT) = [g.__enter__() for g in gs]
        return gs

    def enter_ln_scope(self):
        nc, c = self.nc, self.c
        gs = [nc.sbuf_tensor("rw_s", [c.E, c.D], F32),
              nc.sbuf_tensor("xn1T", [P, c.HC * c.T], BF16),
              nc.sbuf_tensor("wqb", [P, 2 * c.HC * c.WQB], BF16),
              nc.sbuf_tensor("xnbf", [P, c.D], BF16),
              nc.sbuf_tensor("qstage", [P, 2 * 512], BF16),
              nc.psum_tensor("tpA", [P, 512], BF16),
              nc.psum_tensor("tpB", [P, 512], BF16),
              nc.psum_tensor("qk0", [P, 512], F32),
              nc.psum_tensor("qk1", [P, 512], F32),
              nc.psum_tensor("qk2", [P, 512], F32)]
        (self.rw_s, self.xn1T, self.wqb, self.xnbf, self.qstage,
         tpA, tpB, qk0, qk1, qk2) = [g.__enter__() for g in gs]
        self.tp = [tpA, tpB]
        self.qk = [self.pw, qk0, qk1, qk2]
        return gs

    def enter_attn_scope(self):
        nc, c = self.nc, self.c
        gs = [nc.sbuf_tensor("qs", [P, 2 * c.HC * P], BF16),
              nc.sbuf_tensor("probs", [P, 4 * c.T], BF16),
              nc.sbuf_tensor("mxs", [P, 4 * 8], F32),
              nc.sbuf_tensor("esc", [P, 6 * c.T], BF16),
              nc.sbuf_tensor("ssc", [P, 96], F32),
              nc.sbuf_tensor("ctxT", [P, 2 * c.HC * P], BF16),
              nc.sbuf_tensor("x2t", [P, c.D], F32),
              nc.sbuf_tensor("xn2f", [P, c.HC * P], F32),
              nc.sbuf_tensor("xn2b", [P, (1 if c.CAP else c.HC) * P], BF16),
              nc.sbuf_tensor("rc", [P, 64], F32),
              nc.sbuf_tensor("xtmb", [P, c.D], BF16),
              nc.sbuf_tensor("wo_sb", [P, c.HC * c.D], BF16),
              nc.psum_tensor("S1a", [P, 512], F32),
              nc.psum_tensor("S1b", [P, 512], F32),
              nc.psum_tensor("S2a", [P, 512], BF16),
              nc.psum_tensor("S2b", [P, 512], BF16),
              nc.psum_tensor("C", [P, 512], F32),
              nc.psum_tensor("DR", [P, 512], F32),
              nc.psum_tensor("LTL", [P, 512], F32)]
        (self.qs, self.probs, self.mxs, self.esc,
         self.ssc, self.ctxT, self.x2t, self.xn2f, self.xn2b, self.rc,
         self.xtmb, self.wo_sb, S1a, S1b, S2a, S2b, self.C, self.DR, self.LTL) = \
            [g.__enter__() for g in gs]
        self.S1 = [S1a, S1b]
        self.S2b = [S2a, S2b]
        c = self.c
        for hcb in range(0, c.HC, 4):
            w = min(4, c.HC - hcb)
            self.sc.op("gpsimd", lambda e, hcb=hcb, w=w: e.dma_start(
                self.wo_sb[:].rearrange("p (cc d) -> p cc d", cc=c.HC)[:, hcb:hcb + w, :],
                self.wo_d[hcb * P:(hcb + w) * P, :].rearrange("(cc p) d -> p cc d", p=P)),
                waits=[(self.s_mm, self.qkv_mm_done), (self.s_ev, self.qkv_ev_done),
                       (self.s_q0, self.qst_war[0]), (self.s_q1, self.qst_war[1])],
                inc=self.s_wol, inc_by=16)
        self.wo_ld = self.s_wol.n
        return gs

    def enter_moe_scope(self):
        nc, c = self.nc, self.c
        gub_w = 2 * 2 * c.HC * c.GIB
        gs = [nc.sbuf_tensor("xn2h", [P, c.HC * c.TH], BF16),
              nc.sbuf_tensor("acc", [P, c.TTH * c.D], F32),
              nc.sbuf_tensor("act", [P, c.IC * c.TH], BF16),
              nc.sbuf_tensor("gub", [P, gub_w], BF16),
              nc.sbuf_tensor("dwn", [P, c.IC * c.D], BF16),
              nc.sbuf_tensor("gssb", [P, 2 * 512], BF16),
              nc.psum_tensor("pg0", [P, 512], F32),
              nc.psum_tensor("pg1", [P, 512], F32),
              nc.psum_tensor("pu0", [P, 512], F32),
              nc.psum_tensor("pu1", [P, 512], F32),
              nc.psum_tensor("pd0", [P, 512], F32),
              nc.psum_tensor("pd1", [P, 512], F32),
              nc.psum_tensor("pd2", [P, 512], F32)]
        (self.xn2h, self.acc, self.act, self.gub, self.dwn, self.gsb,
         pg0, pg1, pu0, pu1, pd0, pd1, pd2) = [g.__enter__() for g in gs]
        self.pg = [pg0, pg1]
        self.pu = [pu0, pu1]
        self.pd = [pd0, pd1, pd2]
        return gs

    def phase_ln1(self):
        sc, c = self.sc, self.c
        xv = self.xn1T[:].rearrange("p (cc t) -> p cc t", cc=c.HC)
        ln_v = [0, 0]
        ln_s = [0, 0]
        tp_war = [0, 0]
        nbatch = (c.HC + 3) // 4
        for tt in range(c.TT):
            xo = (tt % 2) * c.D
            waits = ([(self.s_vc, ln_v[tt % 2]), (self.s_sc, ln_s[tt % 2])]
                     if tt >= 2 else [])
            sx = [self.s_x0, self.s_x1][tt % 2]
            xw = sc.op("sync", lambda e, tt=tt, xo=xo: e.dma_start(
                self.xring[:, xo:xo + c.D], self.x_d[tt * P:(tt + 1) * P, :]),
                waits=waits, inc=sx, inc_by=16)
            sc.op("vector", lambda e, xo=xo: e.reduce_sum(
                self.stat[:, 0:1], self.xring[:, xo:xo + c.D], axis=AX.X),
                waits=[(sx, xw), (self.s_vc, self.s_vc.n)], inc=self.s_vc)
            sq = sc.op("scalar", lambda e, xo=xo: e.activation(
                self.xnorm[:], self.xring[:, xo:xo + c.D], AF.Square),
                waits=[(sx, xw), (self.s_sc, self.s_sc.n)], inc=self.s_sc)
            sc.op("vector", lambda e: e.reduce_sum(self.stat[:, 1:2], self.xnorm[:], axis=AX.X),
                  waits=[(self.s_sc, sq), (self.s_vc, self.s_vc.n)], inc=self.s_vc)
            sc.op("vector", lambda e: e.tensor_scalar(
                self.stat[:, 2:3], self.stat[:, 0:1], 1.0 / c.D, None, ALU.mult),
                waits=[(self.s_vc, self.s_vc.n)], inc=self.s_vc)
            sc.op("vector", lambda e: e.tensor_scalar(
                self.stat[:, 3:4], self.stat[:, 2:3], self.stat[:, 2:3], None, ALU.mult),
                waits=[(self.s_vc, self.s_vc.n)], inc=self.s_vc)
            vv = sc.op("vector", lambda e: e.scalar_tensor_tensor(
                self.stat[:, 4:5], self.stat[:, 1:2], 1.0 / c.D, self.stat[:, 3:4],
                ALU.mult, ALU.subtract),
                waits=[(self.s_vc, self.s_vc.n)], inc=self.s_vc)
            sc.op("scalar", lambda e: e.activation(
                self.stat[:, 5:6], self.stat[:, 4:5], AF.Ln, bias=self.eps_c[:]),
                waits=[(self.s_vc, vv), (self.s_sc, self.s_sc.n)], inc=self.s_sc)
            rs = sc.op("scalar", lambda e: e.activation(
                self.stat[:, 6:7], self.stat[:, 5:6], AF.Exp, scale=-0.5),
                waits=[(self.s_sc, self.s_sc.n)], inc=self.s_sc)
            nv = sc.op("vector", lambda e, xo=xo: e.tensor_scalar(
                self.xnbf[:], self.xring[:, xo:xo + c.D], self.stat[:, 2:3],
                self.stat[:, 6:7], ALU.subtract, ALU.mult),
                waits=[(self.s_sc, rs), (self.s_vc, self.s_vc.n)], inc=self.s_vc)
            ln_v[tt % 2] = nv
            ln_s[tt % 2] = rs
            for bi in range(nbatch):
                b0 = bi * 4
                bn = min(4, c.HC - b0)
                bank = (tt * nbatch + bi) % 2
                mmv = 0
                for j in range(bn):
                    mmv = sc.op("tensor", lambda e, b0=b0, j=j, bank=bank: e.transpose(
                        self.tp[bank][:, j * P:(j + 1) * P],
                        self.xnbf[:, (b0 + j) * P:(b0 + j + 1) * P], self.ident[:]),
                        waits=[(self.s_vc, nv), (self.s_xn1, tp_war[bank])],
                        inc=self.s_mm)
                tp_war[bank] = sc.op("scalar", lambda e, b0=b0, bn=bn, tt=tt, bank=bank: e.copy(
                    xv[:, b0:b0 + bn, tt * P:(tt + 1) * P], self.tp[bank][:, 0:bn * P]),
                    waits=[(self.s_mm, mmv), (self.s_sc, self.s_sc.n)], inc=self.s_xn1)
        self.ln1_done = self.s_xn1.n

    def phase_qkv(self):
        sc, c = self.sc, self.c
        ln_all = self.ln1_done
        qkw = [0, 0, 0, 0]
        bank_i = 0
        wq_war = [0, 0]
        self.qst_war = [0, 0]
        kTv = self.kT[:].rearrange("p (g t) -> p g t", g=c.NG)
        vNv = self.vN[:].rearrange("p (t d) -> p t d", t=c.TT)
        wqbv = self.wqb[:].rearrange("p (s cc w) -> p s cc w", s=2, cc=c.HC)
        qsv = self.qstage[:].rearrange("p (s w) -> p s w", s=2)
        slot = 0
        qi = 0

        def wload(slot, src_ap, width, wait_mm):
            sw = [self.s_w0, self.s_w1][slot]
            return (sw, sc.op("gpsimd", lambda e, slot=slot, src_ap=src_ap, width=width: e.dma_start(
                wqbv[:, slot, :, 0:width], src_ap),
                waits=[(self.s_mm, wait_mm)], inc=sw, inc_by=16))

        for (dq0, dqw) in chunks(c.D, c.WQB):
            src = self.wq_d[:, dq0:dq0 + dqw].rearrange("(cc p) w -> p cc w", p=P)
            swl, wl = wload(slot, src, dqw, wq_war[slot])
            for (dqc0, dqcw) in chunks(dqw, P):
                for (t0, tw) in chunks(c.T, 512):
                    bank = bank_i % 4
                    bank_i += 1
                    pb = self.qk[bank]
                    mmv = 0
                    for hc in range(c.HC):
                        mmv = sc.op("tensor", lambda e, slot=slot, hc=hc, dqc0=dqc0, dqcw=dqcw, t0=t0, tw=tw, pb=pb: e.matmul(
                            pb[0:dqcw, 0:tw],
                            wqbv[:, slot, hc, dqc0:dqc0 + dqcw],
                            self.xn1T[:, hc * c.T + t0: hc * c.T + t0 + tw],
                            start=(hc == 0), stop=(hc == c.HC - 1)),
                            waits=([(self.s_xn1, ln_all), (swl, wl),
                                    (self.s_ev, qkw[bank])] if hc == 0 else []),
                            inc=self.s_mm)
                    qslot = qi % 2
                    qi += 1
                    sq_sem = [self.s_q0, self.s_q1][qslot]
                    ev = sc.op("scalar", lambda e, qslot=qslot, dqcw=dqcw, tw=tw, pb=pb: e.copy(
                        qsv[0:dqcw, qslot, 0:tw], pb[0:dqcw, 0:tw]),
                        waits=[(self.s_mm, mmv), (sq_sem, self.qst_war[qslot]),
                               (self.s_sc, self.s_sc.n)],
                        inc=self.s_ev)
                    qkw[bank] = ev
                    self.qst_war[qslot] = sc.op("sync", lambda e, qslot=qslot, dq0=dq0, dqc0=dqc0, dqcw=dqcw, t0=t0, tw=tw: e.dma_start(
                        self.qt_d[dq0 + dqc0:dq0 + dqc0 + dqcw, t0:t0 + tw],
                        qsv[0:dqcw, qslot, 0:tw]),
                        waits=[(self.s_ev, ev)], inc=sq_sem, inc_by=16)
            wq_war[slot] = self.s_mm.n
            slot = 1 - slot

        for (dk0, dkw) in chunks(c.DKV, c.WQB):
            src = self.wk_d[:, dk0:dk0 + dkw].rearrange("(cc p) w -> p cc w", p=P)
            swl, wl = wload(slot, src, dkw, wq_war[slot])
            for (dkc0, dkcw) in chunks(dkw, P):
                g = (dk0 + dkc0) // P
                for (t0, tw) in chunks(c.T, 512):
                    bank = bank_i % 4
                    bank_i += 1
                    pb = self.qk[bank]
                    mmv = 0
                    for hc in range(c.HC):
                        mmv = sc.op("tensor", lambda e, slot=slot, hc=hc, dkc0=dkc0, dkcw=dkcw, t0=t0, tw=tw, pb=pb: e.matmul(
                            pb[0:dkcw, 0:tw],
                            wqbv[:, slot, hc, dkc0:dkc0 + dkcw],
                            self.xn1T[:, hc * c.T + t0: hc * c.T + t0 + tw],
                            start=(hc == 0), stop=(hc == c.HC - 1)),
                            waits=([(self.s_xn1, ln_all), (swl, wl),
                                    (self.s_ev, qkw[bank])] if hc == 0 else []),
                            inc=self.s_mm)
                    qkw[bank] = sc.op("scalar", lambda e, g=g, t0=t0, tw=tw, pb=pb, dkcw=dkcw: e.copy(
                        kTv[0:dkcw, g, t0:t0 + tw], pb[0:dkcw, 0:tw]),
                        waits=[(self.s_mm, mmv), (self.s_sc, self.s_sc.n)], inc=self.s_ev)
            wq_war[slot] = self.s_mm.n
            slot = 1 - slot

        for (vb0, vbw) in chunks(c.DKV, c.WQB):
            src = self.wv_d[:, vb0:vb0 + vbw].rearrange("(cc p) w -> p cc w", p=P)
            swl, wl = wload(slot, src, vbw, wq_war[slot])
            for tt in range(c.TT):
                bank = bank_i % 4
                bank_i += 1
                pb = self.qk[bank]
                mmv = 0
                for hc in range(c.HC):
                    mmv = sc.op("tensor", lambda e, slot=slot, hc=hc, tt=tt, vbw=vbw, pb=pb: e.matmul(
                        pb[:, 0:vbw],
                        self.xn1T[:, hc * c.T + tt * P: hc * c.T + (tt + 1) * P],
                        wqbv[:, slot, hc, 0:vbw],
                        start=(hc == 0), stop=(hc == c.HC - 1)),
                        waits=([(self.s_xn1, ln_all), (swl, wl),
                                (self.s_ev, qkw[bank])] if hc == 0 else []),
                        inc=self.s_mm)
                qkw[bank] = sc.op("scalar", lambda e, tt=tt, vb0=vb0, vbw=vbw, pb=pb: e.copy(
                    vNv[:, tt, vb0:vb0 + vbw], pb[:, 0:vbw]),
                    waits=[(self.s_mm, mmv), (self.s_sc, self.s_sc.n)], inc=self.s_ev)
            wq_war[slot] = self.s_mm.n
            slot = 1 - slot

        self.qkv_mm_done = self.s_mm.n
        self.qkv_ev_done = self.s_ev.n

    def join_all(self, waits):
        for eng in ("tensor", "vector", "scalar", "sync", "gpsimd"):
            self.sc.op(eng, lambda e: e.nop(), waits=list(waits))

    def phase_attention(self):
        sc, c = self.sc, self.c
        SCALE = float(1.0 / np.sqrt(128.0).astype(np.float32))
        kTv = self.kT[:].rearrange("p (g t) -> p g t", g=c.NG)
        vNv = self.vN[:].rearrange("p (t d) -> p t d", t=c.TT)
        qv = self.qs[:].rearrange("p (s cc w) -> p s cc w", s=2, cc=c.HC)
        prv = self.probs[:].rearrange("p (s w) -> p s w", s=4)
        escv = self.esc[:].rearrange("p (s w) -> p s w", s=6)
        ctv = self.ctxT[:].rearrange("p (s cc w) -> p s cc w", s=2, cc=c.HC)
        wov = self.wo_sb[:].rearrange("p (cc d) -> p cc d", cc=c.HC)
        xnfv = self.xn2f[:].rearrange("p (cc w) -> p cc w", cc=c.HC)
        xnbv = self.xn2b[:].rearrange("p (cc w) -> p cc w", cc=c.HC)

        st = {
            "q_ld": [0, 0], "q_war": [0, 0],
            "s1_war": [0, 0], "s1_i": 0, "s2_pe": [0, 0], "s2_i": 0,
            "pr_war": [0, 0],
            "esc_war": [0, 0, 0],
            "ctx_war": [0, 0],
            "ctx_ev": [0, 0],
            "ltl_free": 0, "xn2f_rd": 0, "xn2b_st": 0, "x2_sq": 0, "x2_st": 0,
            "xr_war": [0, 0],
        }
        stage4_q = []
        self.ce_all = {}

        def stage_a1(qt, u, hx):
            h0 = 2 * u
            g = h0 // c.GS
            nk = (qt + 1) * P
            qslot = qt % 2
            us = u % 3
            kblocks = chunks(nk, 512)
            hx.update(g=g, nk=nk, us=us, us2=u % 2, kblocks=kblocks, h0=h0)
            nch = len(kblocks)
            for hi in range(2):
                h = h0 + hi
                sl = 2 * us + hi
                for ci, (k0, kw) in enumerate(kblocks):
                    bank = st["s1_i"] % len(self.S1)
                    st["s1_i"] += 1
                    mmv = sc.op("tensor", lambda e, h=h, g=g, qslot=qslot, k0=k0, kw=kw, bank=bank: e.matmul(
                        self.S1[bank][:, 0:kw], qv[:, qslot, h, :], kTv[:, g, k0:k0 + kw],
                        start=True, stop=True),
                        waits=[([self.s_qs0, self.s_qs1][qslot], st["q_ld"][qslot]),
                               (self.s_cs, st["s1_war"][bank]),
                               (self.s_ev, self.qkv_ev_done)],
                        inc=self.s_p1)
                    if ci == nch - 1:
                        mmv_m = sc.op("vector", lambda e, kw=kw, bank=bank: e.tensor_add(
                            self.S1[bank][:, kw - P:kw], self.S1[bank][:, kw - P:kw],
                            self.maskA[:]),
                            waits=[(self.s_p1, mmv), (self.s_vc, self.s_vc.n)], inc=self.s_vc)
                        st["s1_war"][bank] = sc.op("scalar", lambda e, k0=k0, kw=kw, bank=bank, sl=sl, ci=ci: e.activation(
                            escv[:, sl, k0:k0 + kw], self.S1[bank][:, 0:kw], AF.Exp,
                            scale=SCALE,
                            accum_out=self.ssc[:, sl * 12 + ci: sl * 12 + ci + 1]),
                            waits=[(self.s_vc, mmv_m), (self.s_sc, self.s_sc.n),
                                   (self.s_p2, st["esc_war"][us])],
                            inc=self.s_cs)
                    else:
                        st["s1_war"][bank] = sc.op("scalar", lambda e, k0=k0, kw=kw, bank=bank, sl=sl, ci=ci: e.activation(
                            escv[:, sl, k0:k0 + kw], self.S1[bank][:, 0:kw], AF.Exp,
                            scale=SCALE,
                            accum_out=self.ssc[:, sl * 12 + ci: sl * 12 + ci + 1]),
                            waits=[(self.s_p1, mmv), (self.s_sc, self.s_sc.n),
                                   (self.s_p2, st["esc_war"][us])],
                            inc=self.s_cs)
                tot = sc.op("vector", lambda e, nch=nch, sl=sl: e.reduce_sum(
                    self.stat[:, 36 + sl:37 + sl],
                    self.ssc[:, sl * 12: sl * 12 + nch], axis=AX.X),
                    waits=[(self.s_cs, self.s_cs.n), (self.s_vc, self.s_vc.n)],
                    inc=self.s_vc)
                sc.op("vector", lambda e, sl=sl: e.reciprocal(
                    self.stat[:, 42 + sl:43 + sl], self.stat[:, 36 + sl:37 + sl]),
                    waits=[(self.s_vc, tot)], inc=self.s_vc)
            hx["cv"] = self.s_vc.n

        def stage_b(qt, u, hx):
            us, us2, kblocks, nk = hx["us"], hx["us2"], hx["kblocks"], hx["nk"]
            rs = 0
            for hi in range(2):
                sl = 2 * us + hi
                rs = sc.op("vector", lambda e, nk=nk, sl=sl: e.tensor_scalar(
                    escv[:, sl, 0:nk], escv[:, sl, 0:nk],
                    self.stat[:, 42 + sl:43 + sl], None, ALU.mult),
                    waits=[(self.s_vc, hx["cv"] if hi == 0 else self.s_vc.n)],
                    inc=self.s_vc)
            pe_marks = [[], []]
            for hi in range(2):
                sl = 2 * us + hi
                psl = 2 * us2 + hi
                for gi, (kb0, kbw) in enumerate(kblocks):
                    bank = st["s2_i"] % 2
                    st["s2_i"] += 1
                    mmv = 0
                    for (kc0, kcw) in chunks(kbw, P):
                        j = kc0 // P
                        mmv = sc.op("tensor", lambda e, sl=sl, o=kb0 + kc0, j=j, bank=bank: e.transpose(
                            self.S2b[bank][:, j * P:(j + 1) * P],
                            escv[:, sl, o:o + P], self.ident[:]),
                            waits=[(self.s_vc, rs), (self.s_pe, st["s2_pe"][bank])],
                            inc=self.s_p2)
                    pe = sc.op("scalar", lambda e, bank=bank, kb0=kb0, kbw=kbw, psl=psl: e.copy(
                        prv[:, psl, kb0:kb0 + kbw], self.S2b[bank][:, 0:kbw]),
                        waits=[(self.s_p2, mmv), (self.s_cx, st["pr_war"][us2]),
                               (self.s_sc, self.s_sc.n)],
                        inc=self.s_pe)
                    st["s2_pe"][bank] = pe
                    pe_marks[hi].append(pe)
            st["esc_war"][us] = self.s_p2.n
            hx["pe_marks"] = pe_marks

        def stage_c(qt, u, hx):
            g, nk, us, kblocks = hx["g"], hx["nk"], hx["us2"], hx["kblocks"]
            h0 = hx["h0"]
            pe_marks = hx["pe_marks"]
            n_k = nk // P
            bank = self.C if u % 2 == 0 else self.DR
            cxv = 0
            ki = 0
            for bi, (kb0, kbw) in enumerate(kblocks):
                for (kc0, kcw) in chunks(kbw, P):
                    tt_k = (kb0 + kc0) // P
                    first = ki == 0
                    last = ki == n_k - 1
                    ki += 1
                    for hi in range(2):
                        sl = 2 * us + hi
                        w = [(self.s_pe, pe_marks[hi][bi])]
                        if first and hi == 0:
                            w += [(self.s_ce, st["ctx_ev"][us])]
                        cxv = sc.op("tensor", lambda e, tt_k=tt_k, g=g, sl=sl, hi=hi, o=kb0 + kc0, first=first, last=last, bank=bank: e.matmul(
                            bank[:, hi * P:(hi + 1) * P],
                            vNv[:, tt_k, g * P:(g + 1) * P],
                            prv[:, sl, o:o + P],
                            start=(first and hi == 0), stop=(last and hi == 1)),
                            waits=w, inc=self.s_cx)
            st["pr_war"][us] = cxv
            ce = sc.op("scalar", lambda e, h0=h0, qt=qt, bank=bank: e.copy(
                ctv[:, qt % 2, h0, :].tensor[0:0] if False else
                self.ctxT[:, (qt % 2) * c.HC * P + h0 * P: (qt % 2) * c.HC * P + (h0 + 2) * P],
                bank[:, 0:2 * P]),
                waits=[(self.s_cx, cxv), (self.s_wo, st["ctx_war"][qt % 2]),
                       (self.s_sc, self.s_sc.n)],
                inc=self.s_ce)
            st["ctx_ev"][us] = ce

        def stage4(qt):
            tt = qt
            cslot = qt % 2
            xslot = qt % 2
            xo = xslot * c.D
            sx = [self.s_x0, self.s_x1][xslot]
            xw = sc.op("sync", lambda e, tt=tt, xo=xo: e.dma_start(
                self.xring[:, xo:xo + c.D], self.x_d[tt * P:(tt + 1) * P, :]),
                waits=[(self.s_x2, st["xr_war"][xslot])], inc=sx, inc_by=16)
            x2_marks = []
            prev_ev = 0
            for (d0, dw) in chunks(c.D, 512):
                mmv = 0
                for hc in range(c.HC):
                    w = ([(self.s_ce, self.ce_all[qt]), (self.s_wol, self.wo_ld),
                          (self.s_x2, prev_ev if prev_ev else st["xr_war"][1 - xslot]),
                          (self.s_ev, self.qkv_ev_done)] if hc == 0 else [])
                    mmv = sc.op("tensor", lambda e, hc=hc, d0=d0, dw=dw, cslot=cslot: e.matmul(
                        self.pw[:, 0:dw], ctv[:, cslot, hc, :], wov[:, hc, d0:d0 + dw],
                        start=(hc == 0), stop=False),
                        waits=w, inc=self.s_wo, defer=stage4_q)
                mmv = sc.op("tensor", lambda e, d0=d0, dw=dw: e.matmul(
                    self.pw[:, 0:dw], self.ones_r1[:], self.bo_bf[0:1, d0:d0 + dw],
                    start=False, stop=True),
                    waits=[(self.s_ms, self.w_ones), (self.s_bo, 16)], inc=self.s_wo, defer=stage4_q)
                ev = sc.op("vector", lambda e, d0=d0, dw=dw, xo=xo: e.tensor_add(
                    self.x2t[:, d0:d0 + dw], self.pw[:, 0:dw],
                    self.xring[:, xo + d0:xo + d0 + dw]),
                    waits=[(self.s_wo, mmv), (sx, xw),
                           (self.s_x2st, st["x2_st"]), (self.s_sc, st["x2_sq"]),
                           (self.s_vc, self.s_vc.n)],
                    inc=self.s_x2)
                prev_ev = ev
                x2_marks.append(ev)
            st["ctx_war"][cslot] = self.s_wo.n
            x2v = x2_marks[-1]
            st["xr_war"][xslot] = x2v
            x2dst = self.y_d if self.c.CAP else self.x2_d
            st["x2_st"] = sc.op("sync", lambda e, tt=tt, x2dst=x2dst: e.dma_start(
                x2dst[tt * P:(tt + 1) * P, :], self.x2t[:]),
                waits=[(self.s_x2, x2v)], inc=self.s_x2st, inc_by=16)
            sc.op("vector", lambda e: e.reduce_sum(self.stat[:, 16:17], self.x2t[:], axis=AX.X),
                  waits=[(self.s_x2, x2v), (self.s_vc, self.s_vc.n)], inc=self.s_vc)
            sq = sc.op("scalar", lambda e: e.activation(self.xnorm[:], self.x2t[:], AF.Square),
                       waits=[(self.s_x2, x2v), (self.s_sc, self.s_sc.n),
                              (self.s_tr, st["xn2f_rd"])], inc=self.s_sc)
            st["x2_sq"] = sq
            sc.op("vector", lambda e: e.reduce_sum(self.stat[:, 17:18], self.xnorm[:], axis=AX.X),
                  waits=[(self.s_sc, sq), (self.s_vc, self.s_vc.n)], inc=self.s_vc)
            sc.op("vector", lambda e: e.tensor_scalar(
                self.stat[:, 18:19], self.stat[:, 16:17], 1.0 / c.D, None, ALU.mult),
                waits=[(self.s_vc, self.s_vc.n)], inc=self.s_vc)
            sc.op("vector", lambda e: e.tensor_scalar(
                self.stat[:, 19:20], self.stat[:, 18:19], self.stat[:, 18:19], None, ALU.mult),
                waits=[(self.s_vc, self.s_vc.n)], inc=self.s_vc)
            vv = sc.op("vector", lambda e: e.scalar_tensor_tensor(
                self.stat[:, 20:21], self.stat[:, 17:18], 1.0 / c.D, self.stat[:, 19:20],
                ALU.mult, ALU.subtract),
                waits=[(self.s_vc, self.s_vc.n)], inc=self.s_vc)
            sc.op("scalar", lambda e: e.activation(
                self.stat[:, 21:22], self.stat[:, 20:21], AF.Ln, bias=self.eps_c[:]),
                waits=[(self.s_vc, vv), (self.s_sc, self.s_sc.n)], inc=self.s_sc)
            rs = sc.op("scalar", lambda e: e.activation(
                self.stat[:, 22:23], self.stat[:, 21:22], AF.Exp, scale=-0.5),
                waits=[(self.s_sc, self.s_sc.n)], inc=self.s_sc)
            nv = sc.op("vector", lambda e: e.tensor_scalar(
                self.xnorm[:], self.x2t[:], self.stat[:, 18:19], self.stat[:, 22:23],
                ALU.subtract, ALU.mult),
                waits=[(self.s_sc, rs), (self.s_vc, self.s_vc.n)], inc=self.s_vc)
            st["x2_nv"] = nv
            prev_t2 = 0
            for hc in range(c.HC):
                w = [(self.s_vc, nv), (self.s_te, self.s_te.n),
                     (self.s_l2, st["ltl_free"]), (self.s_tr, st["xn2f_rd"])]
                tvv = sc.op("tensor", lambda e, hc=hc: e.transpose(
                    self.LTL[:, 0:P], self.xnorm[:, hc * P:(hc + 1) * P], self.identf[:]),
                    waits=w, inc=self.s_tr, defer=stage4_q)
                t1 = sc.op("scalar", lambda e, hc=hc: e.copy(xnfv[:, hc, :], self.LTL[:, 0:P]),
                      waits=[(self.s_tr, tvv), (self.s_sc, self.s_sc.n)], inc=self.s_te)
                if not c.CAP:
                    sc.op("vector", lambda e, hc=hc: e.tensor_copy(xnbv[:, hc, :], xnfv[:, hc, :]),
                          waits=[(self.s_te, t1), (self.s_xbst, st["xn2b_st"]),
                                 (self.s_vc, self.s_vc.n)],
                          inc=self.s_te)
            te_all = self.s_te.n
            if not c.CAP:
                st["xn2b_st"] = sc.op("sync", lambda e, qt=qt: e.dma_start(
                    self.xn2_d[:, qt * P:(qt + 1) * P].rearrange("(cc p) w -> p cc w", p=P),
                    xnbv[:]),
                    waits=[(self.s_te, te_all)], inc=self.s_xbst, inc_by=16)
            if c.CAP:
                tmc = sc.op("vector", lambda e: e.tensor_copy(self.xtmb[:], self.xnorm[:]),
                            waits=[(self.s_vc, nv), (self.s_tm, st.get("xtm_st", 0)),
                                   (self.s_vc, self.s_vc.n)],
                            inc=self.s_vc)
                st["xtm_st"] = sc.op("sync", lambda e, qt=qt: e.dma_start(
                    self.xntm_d[qt * P:(qt + 1) * P, :], self.xtmb[:]),
                    waits=[(self.s_vc, tmc)], inc=self.s_tm, inc_by=16)
            rv = 0
            for hc in range(c.HC):
                w = ([(self.s_te, te_all), (self.s_ev, self.rwT_done)] if hc == 0 else [])
                rv = sc.op("tensor", lambda e, hc=hc: e.matmul(
                    self.LTL[:, P:P + c.E], xnfv[:, hc, :],
                    self.rwT[:, hc * c.E:(hc + 1) * c.E],
                    start=(hc == 0), stop=False),
                    waits=w, inc=self.s_tr, defer=stage4_q)
            rv = sc.op("tensor", lambda e: e.matmul(
                self.LTL[:, P:P + c.E], self.ones_rf[:], self.eb_f[:],
                start=False, stop=True),
                waits=[(self.s_cst, self.w_const)], inc=self.s_tr, defer=stage4_q)
            st["xn2f_rd"] = rv
            lg = sc.op("vector", lambda e: e.tensor_copy(self.rc[:, 0:c.E], self.LTL[:, P:P + c.E]),
                       waits=[(self.s_tr, rv), (self.s_vc, self.s_vc.n)], inc=self.s_l2)
            st["ltl_free"] = lg
            m8 = sc.op("vector", lambda e: e.max(out=self.rc[:, 16:24], in_=self.rc[:, 0:c.E]),
                       waits=[(self.s_l2, lg), (self.s_vc, self.s_vc.n)], inc=self.s_vc)
            ng = sc.op("vector", lambda e: e.tensor_scalar(
                self.rc[:, 24:25], self.rc[:, 16:17], -1.0, None, ALU.mult),
                waits=[(self.s_vc, m8)], inc=self.s_vc)
            ex = sc.op("scalar", lambda e: e.activation(
                self.rc[:, 0:c.E],
                self.rc[:, 0:c.E], AF.Exp, bias=self.rc[:, 24:25]),
                waits=[(self.s_vc, ng), (self.s_sc, self.s_sc.n)], inc=self.s_sc)
            e4 = sc.op("scalar", lambda e: e.activation(
                self.rc[:, 25:29], self.rc[:, 16:20], AF.Exp, bias=self.rc[:, 24:25]),
                waits=[(self.s_sc, ex)], inc=self.s_sc)
            sm = sc.op("vector", lambda e: e.reduce_sum(self.rc[:, 29:30], self.rc[:, 25:29], axis=AX.X),
                       waits=[(self.s_sc, e4), (self.s_vc, self.s_vc.n)], inc=self.s_vc)
            rcp = sc.op("vector", lambda e: e.reciprocal(self.rc[:, 30:31], self.rc[:, 29:30]),
                        waits=[(self.s_vc, sm)], inc=self.s_vc)
            kp = sc.op("vector", lambda e: e.tensor_scalar(
                self.rc[:, 32:32 + c.E], self.rc[:, 0:c.E], self.rc[:, 28:29], None, ALU.is_ge),
                waits=[(self.s_vc, rcp)], inc=self.s_vc)
            if c.CAP:
                tl = sc.op("vector", lambda e: e.tensor_scalar(
                    self.rc[:, 31:32], self.rc[:, 28:29], 0.99601, None, ALU.mult),
                    waits=[(self.s_vc, kp)], inc=self.s_vc)
                kl = sc.op("vector", lambda e: e.tensor_scalar(
                    self.rc[:, 48:48 + c.E], self.rc[:, 0:c.E], self.rc[:, 31:32],
                    None, ALU.is_ge),
                    waits=[(self.s_vc, tl)], inc=self.s_vc)
                wv_ = sc.op("vector", lambda e: e.tensor_scalar(
                    self.rc[:, 0:c.E], self.rc[:, 0:c.E], self.rc[:, 30:31], None, ALU.mult),
                    waits=[(self.s_vc, kl)], inc=self.s_vc)
                wv_ = sc.op("vector", lambda e: e.tensor_mul(
                    self.rc[:, 0:c.E], self.rc[:, 0:c.E], self.rc[:, 32:32 + c.E]),
                    waits=[(self.s_vc, wv_)], inc=self.s_vc)
                sc.op("vector", lambda e, tt=tt: e.scalar_tensor_tensor(
                    self.w_all[:, tt * c.E:(tt + 1) * c.E], self.rc[:, 48:48 + c.E],
                    1e-30, self.rc[:, 0:c.E], ALU.mult, ALU.add),
                    waits=[(self.s_vc, wv_)], inc=self.s_l2)
            else:
                wv_ = sc.op("vector", lambda e: e.tensor_scalar(
                    self.rc[:, 0:c.E], self.rc[:, 0:c.E], self.rc[:, 30:31], None, ALU.mult),
                    waits=[(self.s_vc, kp)], inc=self.s_vc)
                sc.op("vector", lambda e, tt=tt: e.tensor_mul(
                    self.w_all[:, tt * c.E:(tt + 1) * c.E], self.rc[:, 0:c.E],
                    self.rc[:, 32:32 + c.E]),
                    waits=[(self.s_vc, wv_)], inc=self.s_l2)

        self.join_all([(self.s_mm, self.qkv_mm_done), (self.s_ev, self.qkv_ev_done),
                       (self.s_xn1, self.ln1_done),
                       (self.s_q0, self.qst_war[0]), (self.s_q1, self.qst_war[1]),
                       (self.s_w0, self.s_w0.n), (self.s_w1, self.s_w1.n),
                       (self.s_vc, self.s_vc.n), (self.s_sc, self.s_sc.n)])
        for qt in range(c.TT if self.qt_limit is None else self.qt_limit):
            sc.splice("tensor", stage4_q)
            qslot = qt % 2
            qwaits = ([(self.s_p2, st["q_war"][qslot])] if qt >= 2 else
                      [(self.s_q0, self.qst_war[0]), (self.s_q1, self.qst_war[1]),
                       (self.s_mm, self.qkv_mm_done), (self.s_ev, self.qkv_ev_done)])
            sqs = [self.s_qs0, self.s_qs1][qslot]
            st["q_ld"][qslot] = sc.op("sync", lambda e, qt=qt, qslot=qslot: e.dma_start(
                qv[:, qslot, :, :],
                self.qt_d[:, qt * P:(qt + 1) * P].rearrange("(cc p) w -> p cc w", p=P)),
                waits=qwaits, inc=sqs, inc_by=16)
            st["q_sem"] = sqs
            nu = (c.NH if self.head_limit is None else self.head_limit) // 2
            hctx = [dict() for _ in range(nu)]
            for step in range(nu + 3):
                if step >= 3:
                    stage_c(qt, step - 3, hctx[step - 3])
                if 2 <= step < nu + 2:
                    stage_b(qt, step - 2, hctx[step - 2])
                if step < nu:
                    stage_a1(qt, step, hctx[step])
            self.ce_all[qt] = self.s_ce.n
            st["q_war"][qslot] = self.s_p2.n
            if not self.no_stage4:
                stage4(qt)
        sc.splice("tensor", stage4_q)
        self.w_done = self.s_l2.n
        self.attn_xb = self.s_xbst.n
        self.attn_x2 = self.s_x2st.n

    def phase_moe(self):
        sc, c = self.sc, self.c
        xnv = self.xn2h[:].rearrange("p (cc t) -> p cc t", cc=c.HC)
        accv = self.acc[:].rearrange("p (t d) -> p t d", t=c.TTH)
        actv = self.act[:].rearrange("p (ic t) -> p ic t", ic=c.IC)
        gubv = self.gub[:].rearrange("p (s cc w) -> p s cc w", s=4, cc=c.HC)
        dwv = self.dwn[:].rearrange("p (ic d) -> p ic d", ic=c.IC)
        gsv = self.gsb[:].rearrange("p (s w) -> p s w", s=2)

        gu_war = [0, 0]
        pg_war = [0, 0]
        pu_war = [0, 0]
        pd_war = [0, 0, 0]
        gs_war = [0, 0]
        act_rd = 0
        dwn_rd = 0
        y_total = 0
        xl_prev = 0

        self.join_all([(self.s_l2, self.w_done), (self.s_sc, self.s_sc.n),
                       (self.s_tr, self.s_tr.n), (self.s_te, self.s_te.n),
                       (self.s_vc, self.s_vc.n), (self.s_ce, self.s_ce.n),
                       (self.s_x2, self.s_x2.n), (self.s_x2st, self.attn_x2),
                       (self.s_xbst, self.attn_xb),
                       (self.s_qs0, self.s_qs0.n), (self.s_qs1, self.s_qs1.n),
                       (self.s_x0, self.s_x0.n), (self.s_x1, self.s_x1.n),
                       (self.s_wol, self.wo_ld)]
                      + ([(self.s_tm, self.s_tm.n)] if self.c.CAP else []))
        for half in range(c.N_HALVES):
            t_off = half * c.TH
            xl = sc.op("sync", lambda e, t_off=t_off: e.dma_start(
                xnv[:],
                self.xn2_d[:, t_off:t_off + c.TH].rearrange("(cc p) w -> p cc w", p=P)),
                waits=[(self.s_gu, xl_prev), (self.s_l2, self.w_done),
                       (self.s_xbst, self.attn_xb)],
                inc=self.s_xl, inc_by=16)
            for tt in range(c.TTH):
                sc.op("sync", lambda e, tt=tt, t_off=t_off: e.dma_start(
                    accv[:, tt, :], self.x2_d[t_off + tt * P: t_off + (tt + 1) * P, :]),
                    waits=[(self.s_yst, y_total), (self.s_x2st, self.attn_x2)],
                    inc=self.s_yi, inc_by=16)
            yi_all = self.s_yi.n
            for ei in range(c.E):
                ib_list = chunks(c.DI, c.GIB)
                gl_marks = {}

                def gu_load(bi, wait_mark):
                    i0, iw = ib_list[bi]
                    slot = bi % 2
                    sg = [self.s_g0, self.s_g1][slot]
                    sc.op("gpsimd", lambda e, ei=ei, i0=i0, iw=iw, slot=slot: e.dma_start(
                        gubv[:, 2 * slot, :, 0:iw],
                        self.gate_d[ei][:, i0:i0 + iw].rearrange("(cc p) w -> p cc w", p=P)),
                        waits=[(self.s_gu, wait_mark)], inc=sg, inc_by=16)
                    gl = sc.op("gpsimd", lambda e, ei=ei, i0=i0, iw=iw, slot=slot: e.dma_start(
                        gubv[:, 2 * slot + 1, :, 0:iw],
                        self.up_d[ei][:, i0:i0 + iw].rearrange("(cc p) w -> p cc w", p=P)),
                        waits=[], inc=sg, inc_by=16)
                    gl_marks[bi] = (sg, gl)

                for bi in range(min(2, len(ib_list))):
                    gu_load(bi, gu_war[bi % 2])
                dl_next = [0]
                dl_last = [0]

                def down_load_some(k):
                    while dl_next[0] < c.IC and k > 0:
                        ic = dl_next[0]
                        dl_next[0] += 1
                        k -= 1
                        dl_last[0] = sc.op("gpsimd", lambda e, ei=ei, ic=ic: e.dma_start(
                            dwv[:, ic, :], self.down_d[ei][ic * P:(ic + 1) * P, :]),
                            waits=[(self.s_dn, dwn_rd)], inc=self.s_dw, inc_by=16)
                pi = 0
                for bi, (i0, iw) in enumerate(ib_list):
                    slot = bi % 2
                    for (ic0, icw) in chunks(iw, P):
                        ic = (i0 + ic0) // P
                        for (t0, tw) in chunks(c.TH, 512):
                            bank = pi % 2
                            gslot = pi % 2
                            pi += 1
                            gm = um = 0
                            for hc in range(c.HC):
                                gm = sc.op("tensor", lambda e, slot=slot, hc=hc, ic0=ic0, t0=t0, tw=tw, bank=bank: e.matmul(
                                    self.pg[bank][:, 0:tw],
                                    gubv[:, 2 * slot, hc, ic0:ic0 + P],
                                    xnv[:, hc, t0:t0 + tw],
                                    start=(hc == 0), stop=(hc == c.HC - 1)),
                                    waits=([gl_marks[bi], (self.s_xl, xl),
                                            (self.s_si, pg_war[bank])] if hc == 0 else []),
                                    inc=self.s_gu)
                            for hc in range(c.HC):
                                um = sc.op("tensor", lambda e, slot=slot, hc=hc, ic0=ic0, t0=t0, tw=tw, bank=bank: e.matmul(
                                    self.pu[bank][:, 0:tw],
                                    gubv[:, 2 * slot + 1, hc, ic0:ic0 + P],
                                    xnv[:, hc, t0:t0 + tw],
                                    start=(hc == 0), stop=(hc == c.HC - 1)),
                                    waits=([(self.s_ae, pu_war[bank])] if hc == 0 else []),
                                    inc=self.s_gu)
                            sv = sc.op("scalar", lambda e, bank=bank, tw=tw, gslot=gslot: e.activation(
                                gsv[:, gslot, 0:tw], self.pg[bank][:, 0:tw], AF.Silu),
                                waits=[(self.s_gu, gm), (self.s_ae, gs_war[gslot]),
                                       (self.s_sc, self.s_sc.n)],
                                inc=self.s_si)
                            pg_war[bank] = sv
                            av = sc.op("vector", lambda e, bank=bank, tw=tw, gslot=gslot, ic=ic, t0=t0: e.tensor_mul(
                                actv[:, ic, t0:t0 + tw], gsv[:, gslot, 0:tw],
                                self.pu[bank][:, 0:tw]),
                                waits=[(self.s_gu, um), (self.s_si, sv),
                                       (self.s_dn, act_rd), (self.s_vc, self.s_vc.n)],
                                inc=self.s_ae)
                            pu_war[bank] = av
                            gs_war[gslot] = av
                    gu_war[slot] = self.s_gu.n
                    if bi + 2 < len(ib_list):
                        gu_load(bi + 2, gu_war[slot])
                    if bi >= 1:
                        down_load_some(2)
                down_load_some(c.IC)
                dl_all = dl_last[0]
                act_done = self.s_ae.n
                bank_i = 0
                for tt in range(c.TTH):
                    for (d0, dw) in chunks(c.D, 512):
                        bank = bank_i % 3
                        bank_i += 1
                        dm = 0
                        for ic in range(c.IC):
                            w = ([(self.s_ae, act_done), (self.s_de, pd_war[bank]),
                                  (self.s_dw, dl_all)] if ic == 0 else [])
                            dm = sc.op("tensor", lambda e, ic=ic, tt=tt, d0=d0, dw=dw, bank=bank: e.matmul(
                                self.pd[bank][:, 0:dw],
                                actv[:, ic, tt * P:(tt + 1) * P],
                                dwv[:, ic, d0:d0 + dw],
                                start=(ic == 0), stop=(ic == c.IC - 1)),
                                waits=w, inc=self.s_dn)
                        wt = half * c.TTH + tt
                        pd_war[bank] = sc.op("vector", lambda e, tt=tt, d0=d0, dw=dw, bank=bank, wt=wt, ei=ei: e.scalar_tensor_tensor(
                            accv[:, tt, d0:d0 + dw], self.pd[bank][:, 0:dw],
                            self.w_all[:, wt * c.E + ei: wt * c.E + ei + 1],
                            accv[:, tt, d0:d0 + dw], ALU.mult, ALU.add),
                            waits=[(self.s_dn, dm), (self.s_yi, yi_all),
                                   (self.s_l2, self.w_done), (self.s_vc, self.s_vc.n)],
                            inc=self.s_de)
                act_rd = self.s_dn.n
                dwn_rd = self.s_dn.n
            xl_prev = self.s_gu.n
            for tt in range(c.TTH):
                sc.op("sync", lambda e, tt=tt, t_off=t_off: e.dma_start(
                    self.y_d[t_off + tt * P: t_off + (tt + 1) * P, :], accv[:, tt, :]),
                    waits=[(self.s_de, self.s_de.n)], inc=self.s_yst, inc_by=16)
            y_total = self.s_yst.n


    def enter_moe_scope_sparse(self):
        nc, c = self.nc, self.c
        gub_w = 2 * 2 * c.HC * c.GIB
        gs = [nc.sbuf_tensor("xngT", [P, c.HC * c.CAP], BF16),
              nc.sbuf_tensor("acts", [P, c.IC * c.CAP], BF16),
              nc.sbuf_tensor("gub", [P, gub_w], BF16),
              nc.sbuf_tensor("dwn", [P, c.IC * c.D], BF16),
              nc.sbuf_tensor("gssb", [P, 2 * 512], BF16),
              nc.sbuf_tensor("xstg", [P, 2 * c.D], BF16),
              nc.sbuf_tensor("sst", [P, 2 * min(1024, c.D)], F32),
              nc.sbuf_tensor("kpall", [P, c.TT * c.E], F32),
              nc.sbuf_tensor("rksc", [P, 2 * c.E], F32),
              nc.sbuf_tensor("peib", [P, 2 * c.E], mybir.dt.int32),
              nc.sbuf_tensor("wrtb", [P, 2 * c.E], F32),
              nc.sbuf_tensor("idxc", [P, 2 * c.CT], mybir.dt.int32),
              nc.sbuf_tensor("wrg", [P, 2 * c.CT * c.E], F32),
              nc.sbuf_tensor("m01f_sb", [P, P], F32),
              nc.sbuf_tensor("s01f_sb", [P, P], F32),
              nc.sbuf_tensor("iota_sb", [P, c.TT], mybir.dt.int32),
              nc.sbuf_tensor("ecr_sb", [1, c.E], F32),
              nc.sbuf_tensor("thr_sb", [P, c.E], F32),
              nc.sbuf_tensor("trow", [1, c.TT * c.E], F32),
              nc.sbuf_tensor("orow", [1, c.TT * c.E], F32),
              nc.sbuf_tensor("tot16", [c.TT, c.E], F32),
              nc.psum_tensor("pg0", [P, 512], F32),
              nc.psum_tensor("pg1", [P, 512], F32),
              nc.psum_tensor("pu0", [P, 512], F32),
              nc.psum_tensor("pu1", [P, 512], F32),
              nc.psum_tensor("ptp", [P, 512], BF16),
              nc.psum_tensor("ptq", [P, 512], BF16)]
        (self.xngT, self.acts, self.gub, self.dwn, self.gsb, self.xstg,
         self.sst, self.kpall, self.rksc, self.peib, self.wrtb, self.idxc,
         self.wrg, self.m01f_sb, self.s01f_sb,
         self.iota_sb, self.ecr_sb, self.thr_sb, self.trow, self.orow, self.tot16,
         pg0, pg1, pu0, pu1, ptp, ptq) = [g.__enter__() for g in gs]
        self.pg = [pg0, pg1]
        self.pu = [pu0, pu1]
        self.pd = self.pg
        self.ptp = [ptp, ptq]
        self.tp_grp = 0
        self.tp_bank_war = [0, 0]
        return gs

    def phase_router_build(self):
        sc, c = self.sc, self.c
        BIG = 1.0e7
        kv = self.kpall[:].rearrange("p (t e) -> p t e", t=c.TT)

        def _mkregs(e):
            self.bnd_ix = e.alloc_register("bnd_ix")
            e.reg_mov(self.bnd_ix, c.E * c.CAP - 1)
            self.bnd_t = e.alloc_register("bnd_t")
            return e.reg_mov(self.bnd_t, c.T - 1)
        sc.op("gpsimd", _mkregs)
        sc.op("sync", lambda e: e.dma_start(self.m01f_sb[:], self.m01f_d.ap()), inc=self.s_rb, inc_by=16)
        sc.op("sync", lambda e: e.dma_start(self.s01f_sb[:], self.s01f_d.ap()), inc=self.s_rb, inc_by=16)
        sc.op("sync", lambda e: e.dma_start(self.iota_sb[:], self.iota_d.ap()), inc=self.s_rb, inc_by=16)
        sc.op("sync", lambda e: e.dma_start(self.ecr_sb[:], self.ecr_d.ap()), inc=self.s_rb, inc_by=16)
        sc.op("sync", lambda e: e.dma_start(self.thr_sb[:], self.thr_d.ap()), inc=self.s_rb, inc_by=16)
        sc.op("sync", lambda e: e.dma_start(
            self.idx_d.tensor[:].rearrange("(p w) one -> p (w one)", p=P),
            self.sent_d.ap()), inc=self.s_rb, inc_by=16)
        self.sent_mark = cst = self.s_rb.n
        kmarks = []
        for tt in range(c.TT):
            kmarks.append(sc.op("vector", lambda e, tt=tt: e.tensor_scalar(
                kv[:, tt, :], self.w_all[:, tt * c.E:(tt + 1) * c.E], 0.0, None, ALU.is_gt),
                waits=[(self.s_l2, self.w_done), (self.s_vc, self.s_vc.n)], inc=self.s_kp))
        tmm = 0
        for tt in range(c.TT):
            tmm = sc.op("tensor", lambda e, tt=tt: e.matmul(
                self.pw[0:1, tt * c.E:(tt + 1) * c.E], self.ones_cf[:], kv[:, tt, :],
                start=(tt == 0), stop=(tt == c.TT - 1)),
                waits=[(self.s_kp, kmarks[tt])], inc=self.s_ps)
        tev = sc.op("scalar", lambda e: e.copy(self.trow[:], self.pw[0:1, 0:c.TT * c.E]),
                    waits=[(self.s_ps, tmm), (self.s_sc, self.s_sc.n)], inc=self.s_ps)
        t1 = sc.op("sync", lambda e: e.dma_start(self.tot_d, self.trow[:]),
                   waits=[(self.s_ps, tev)], inc=self.s_bc, inc_by=16)
        t2 = sc.op("sync", lambda e: e.dma_start(self.tot16[:], self.tot_d),
                   waits=[(self.s_bc, t1)], inc=self.s_bc, inc_by=16)
        omm = sc.op("tensor", lambda e: e.matmul(
            self.pw[0:c.TT, 256:256 + c.E], self.s01f_sb[0:c.TT, 0:c.TT], self.tot16[:],
            start=True, stop=False),
            waits=[(self.s_bc, t2), (self.s_rb, cst), (self.s_ps, self.s_ps.n)],
            inc=self.s_ps)
        omm = sc.op("tensor", lambda e: e.matmul(
            self.pw[0:c.TT, 256:256 + c.E], self.ones_rf[0:1, 0:c.TT], self.ecr_sb[:],
            start=False, stop=True),
            waits=[(self.s_rb, cst)], inc=self.s_ps)
        oev = sc.op("scalar", lambda e: e.copy(self.tot16[:], self.pw[0:c.TT, 256:256 + c.E]),
                    waits=[(self.s_ps, omm), (self.s_sc, self.s_sc.n)], inc=self.s_ps)
        o1 = sc.op("sync", lambda e: e.dma_start(self.off_d, self.tot16[:]),
                   waits=[(self.s_ps, oev)], inc=self.s_bc, inc_by=16)
        o2 = sc.op("sync", lambda e: e.dma_start(self.orow[:], self.off_d),
                   waits=[(self.s_bc, o1)], inc=self.s_bc, inc_by=16)
        wrt_war = [0, 0]
        pd_rw = [0, 0]
        pei_sx = [0, 0]
        for tt in range(c.TT):
            bank = self.pd[tt % 2]
            cmm = sc.op("tensor", lambda e, tt=tt, bank=bank: e.matmul(
                bank[:, 0:c.E], self.m01f_sb[:], kv[:, tt, :],
                start=True, stop=False),
                waits=[(self.s_kp, kmarks[tt]), (self.s_bc, o2),
                       (self.s_kp, pd_rw[tt % 2])],
                inc=self.s_ps)
            cmm = sc.op("tensor", lambda e, tt=tt, bank=bank: e.matmul(
                bank[:, 0:c.E], self.ones_rf[:], self.orow[0:1, tt * c.E:(tt + 1) * c.E],
                start=False, stop=True),
                waits=[], inc=self.s_ps)
            px = sc.op("vector", lambda e, tt=tt, bank=bank: e.tensor_sub(
                self.rksc[:, 0:c.E], bank[:, 0:c.E], kv[:, tt, :]),
                waits=[(self.s_ps, cmm), (self.s_vc, self.s_vc.n),
                       (self.s_kp, self.s_kp.n)], inc=self.s_kp)
            pd_rw[tt % 2] = px
            ws = tt % 2
            wv2 = sc.op("vector", lambda e, tt=tt, ws=ws: e.tensor_copy(
                self.wrtb[:, ws * c.E:(ws + 1) * c.E],
                self.w_all[:, tt * c.E:(tt + 1) * c.E]),
                waits=[([self.s_wa, self.s_wb][ws], wrt_war[ws]),
                       (self.s_vc, self.s_vc.n)], inc=self.s_vc)
            wrt_war[ws] = sc.op("sync", lambda e, tt=tt, ws=ws: e.dma_start(
                self.wr_d[tt * P:(tt + 1) * P, :],
                self.wrtb[:, ws * c.E:(ws + 1) * c.E]),
                waits=[(self.s_vc, wv2)], inc=[self.s_wa, self.s_wb][ws], inc_by=16)
            p1 = sc.op("vector", lambda e, tt=tt: e.tensor_scalar(
                self.rksc[:, a:a + c.E] if False else self.rksc[:, 1 * c.E:2 * c.E],
                kv[:, tt, :], -BIG, BIG, ALU.mult, ALU.add),
                waits=[(self.s_vc, self.s_vc.n)], inc=self.s_vc)
            p2 = sc.op("vector", lambda e: e.tensor_add(
                self.rksc[:, 0:c.E], self.rksc[:, 0:c.E], self.rksc[:, 1 * c.E:2 * c.E]),
                waits=[(self.s_vc, p1)], inc=self.s_vc)
            ov1 = sc.op("vector", lambda e: e.tensor_tensor(
                self.rksc[:, 1 * c.E:2 * c.E], self.rksc[:, 0:c.E], self.thr_sb[:],
                ALU.is_ge),
                waits=[(self.s_vc, p2), (self.s_rb, self.sent_mark)], inc=self.s_vc)
            p3 = sc.op("vector", lambda e: e.scalar_tensor_tensor(
                self.rksc[:, 0:c.E], self.rksc[:, 1 * c.E:2 * c.E], BIG,
                self.rksc[:, 0:c.E], ALU.mult, ALU.add),
                waits=[(self.s_vc, ov1)], inc=self.s_vc)
            ps = tt % 2
            pcast = sc.op("vector", lambda e, ps=ps: e.tensor_copy(
                self.peib[:, ps * c.E:(ps + 1) * c.E], self.rksc[:, 0:c.E]),
                waits=[(self.s_vc, p3), ([self.s_sx0, self.s_sx1][ps], pei_sx[ps]),
                       (self.s_vc, self.s_vc.n)],
                inc=self.s_kp)
            last_sx = 0
            for ei in range(c.E):
                last_sx = sc.op("gpsimd", lambda e, tt=tt, ei=ei, ps=ps: e.indirect_dma_start(
                    self.idx_d,
                    bass.IndirectOffsetOnAxis(self.peib[:, ps * c.E + ei: ps * c.E + ei + 1], 0),
                    self.iota_sb[:, tt:tt + 1], None,
                    bounds_check=self.bnd_ix, oob_is_err=False),
                    waits=[(self.s_kp, pcast), (self.s_rb, self.sent_mark)],
                    inc=[self.s_sx0, self.s_sx1][ps], inc_by=16)
            pei_sx[ps] = last_sx
        self.rt_done = (self.s_sx0.n, self.s_sx1.n)
        self.wr_done = (self.s_wa.n, self.s_wb.n)

    def phase_moe_sparse(self):
        sc, c = self.sc, self.c
        xnv = self.xngT[:].rearrange("p (cc t) -> p cc t", cc=c.HC)
        actv = self.acts[:].rearrange("p (ic t) -> p ic t", ic=c.IC)
        gubv = self.gub[:].rearrange("p (s cc w) -> p s cc w", s=4, cc=c.HC)
        dwv = self.dwn[:].rearrange("p (ic d) -> p ic d", ic=c.IC)
        gsv = self.gsb[:].rearrange("p (s w) -> p s w", s=2)
        xsv = self.xstg[:].rearrange("p (s w) -> p s w", s=2)
        ssv = self.sst[:].rearrange("p (s w) -> p s w", s=2)
        wgv = self.wrg[:].rearrange("p (s ct w) -> p s ct w", s=2, ct=c.CT)

        self.join_all([(self.s_l2, self.w_done), (self.s_sc, self.s_sc.n),
                       (self.s_tr, self.s_tr.n), (self.s_te, self.s_te.n),
                       (self.s_vc, self.s_vc.n), (self.s_ce, self.s_ce.n),
                       (self.s_x2, self.s_x2.n), (self.s_x2st, self.attn_x2),
                       (self.s_xbst, self.attn_xb), (self.s_tm, self.s_tm.n),
                       (self.s_qs0, self.s_qs0.n), (self.s_qs1, self.s_qs1.n),
                       (self.s_x0, self.s_x0.n), (self.s_x1, self.s_x1.n),
                       (self.s_wol, self.wo_ld)])
        self.phase_router_build()

        sc.op("gpsimd", lambda e: e.memset(self.wrg[:], 0.0), inc=self.s_ms)

        gu_war = [0, 0]
        pg_war = [0, 0]
        pu_war = [0, 0]
        pd_war = [0, 0]
        gs_war = [0, 0]
        xstg_war = [0, 0]
        sst_war = [0, 0]
        idx_war = [[], []]
        wrg_war = [0, 0]
        act_rd = 0
        dwn_rd = 0
        prep_done = {}
        self.y_marks = {}

        def prep(ei, agate):
            es = ei % 2
            il = 0
            for ct in range(c.CT):
                il = sc.op("gpsimd", lambda e, ei=ei, es=es, ct=ct: e.dma_start(
                    self.idxc[:, es * c.CT + ct: es * c.CT + ct + 1],
                    self.idx_d.tensor[ei * c.CAP + ct * P: ei * c.CAP + (ct + 1) * P, :]),
                    waits=([(self.s_sx0, self.rt_done[0]), (self.s_sx1, self.rt_done[1])]
                           + idx_war[es] if ct == 0 else []),
                    inc=self.s_ix, inc_by=16)
            tps = []
            gx_last = 0
            for ct in range(c.CT):
                xs = (ei * c.CT + ct) % 2
                sgx = [self.s_gx0, self.s_gx1][xs]
                gx = sc.op("gpsimd", lambda e, ct=ct, es=es, xs=xs: e.indirect_dma_start(
                    xsv[:, xs, :], None, self.xntm_d,
                    bass.IndirectOffsetOnAxis(self.idxc[:, es * c.CT + ct: es * c.CT + ct + 1], 0),
                    bounds_check=self.bnd_t, oob_is_err=False),
                    waits=[(self.s_ix, il), (self.s_tp, xstg_war[xs])],
                    inc=sgx, inc_by=16)
                sgw = [self.s_gw0, self.s_gw1][es]
                wg = sc.op("gpsimd", lambda e, ct=ct, es=es: e.indirect_dma_start(
                    wgv[:, es, ct, :], None, self.wr_d,
                    bass.IndirectOffsetOnAxis(self.idxc[:, es * c.CT + ct: es * c.CT + ct + 1], 0),
                    bounds_check=self.bnd_t, oob_is_err=False),
                    waits=[(self.s_wa, self.wr_done[0]), (self.s_wb, self.wr_done[1]),
                           (self.s_de, wrg_war[es])],
                    inc=sgw, inc_by=16)
                gx_last = wg
                mmv = 0
                for hc in range(c.HC):
                    tb = self.tp_grp % 2
                    mmv = sc.op("tensor", lambda e, hc=hc, xs=xs, tb=tb: e.transpose(
                        self.ptp[tb][:, (hc % 4) * P:(hc % 4 + 1) * P],
                        xsv[:, xs, hc * P:(hc + 1) * P], self.ident[:]),
                        waits=(([(sgx, gx), agate] if hc == 0 else []) +
                               ([(self.s_tp, self.tp_bank_war[tb])] if hc % 4 == 0 else [])),
                        inc=self.s_ps)
                    if hc % 4 == 3 or hc == c.HC - 1:
                        nb = hc % 4 + 1
                        self.tp_bank_war[tb] = sc.op(
                            "scalar", lambda e, hc=hc, ct=ct, nb=nb, tb=tb: e.copy(
                                xnv[:, hc - nb + 1:hc + 1, ct * P:(ct + 1) * P],
                                self.ptp[tb][:, 0:nb * P].rearrange("p (f w) -> p f w", f=nb)),
                            waits=[(self.s_ps, mmv), (self.s_sc, self.s_sc.n)],
                            inc=self.s_tp)
                        self.tp_grp += 1
                xstg_war[xs] = self.s_tp.n
            prep_done[ei] = (self.s_tp.n, il, ([self.s_gw0, self.s_gw1][es], gx_last))

        prep(0, (self.s_sx0, self.rt_done[0]))
        ib_list = chunks(c.DI, c.GIB)
        gl_marks = {}

        def gu_load(lei, bi, wait_mark):
            i0, iw = ib_list[bi]
            slot = bi % 2
            sg = [self.s_g0, self.s_g1][slot]
            sc.op("gpsimd", lambda e, lei=lei, i0=i0, iw=iw, slot=slot: e.dma_start(
                gubv[:, 2 * slot, :, 0:iw],
                self.gate_d[lei][:, i0:i0 + iw].rearrange("(cc p) w -> p cc w", p=P)),
                waits=[(self.s_gu, wait_mark)], inc=sg, inc_by=16)
            gl = sc.op("gpsimd", lambda e, lei=lei, i0=i0, iw=iw, slot=slot: e.dma_start(
                gubv[:, 2 * slot + 1, :, 0:iw],
                self.up_d[lei][:, i0:i0 + iw].rearrange("(cc p) w -> p cc w", p=P)),
                waits=[], inc=sg, inc_by=16)
            gl_marks[(lei, bi)] = (sg, gl)

        for bi in range(min(2, len(ib_list))):
            gu_load(0, bi, 0)
        prev_ss = (0, 0)
        for ei in range(c.E):
            es = ei % 2
            tp_mark, il_mark, gw_mark = prep_done[ei]
            dl_next = [0]
            dl_last = [0]

            def down_load_some(k):
                while dl_next[0] < c.IC and k > 0:
                    ic = dl_next[0]
                    dl_next[0] += 1
                    k -= 1
                    dl_last[0] = sc.op("gpsimd", lambda e, ei=ei, ic=ic: e.dma_start(
                        dwv[:, ic, :], self.down_d[ei][ic * P:(ic + 1) * P, :]),
                        waits=[(self.s_dn, dwn_rd)], inc=self.s_dw, inc_by=16)

            pi = 0
            for bi, (i0, iw) in enumerate(ib_list):
                slot = bi % 2
                for (ic0, icw) in chunks(iw, P):
                    ic = (i0 + ic0) // P
                    for (t0, tw) in chunks(c.CAP, 512):
                        bank = pi % 2
                        gslot = pi % 2
                        pi += 1
                        gm = um = 0
                        for hc in range(c.HC):
                            gm = sc.op("tensor", lambda e, slot=slot, hc=hc, ic0=ic0, t0=t0, tw=tw, bank=bank: e.matmul(
                                self.pg[bank][:, 0:tw],
                                gubv[:, 2 * slot, hc, ic0:ic0 + P],
                                xnv[:, hc, t0:t0 + tw],
                                start=(hc == 0), stop=(hc == c.HC - 1)),
                                waits=([gl_marks[(ei, bi)], (self.s_tp, tp_mark),
                                        (self.s_si, pg_war[bank]),
                                        (self.s_de, pd_war[bank])] if hc == 0 else []),
                                inc=self.s_gu)
                        for hc in range(c.HC):
                            um = sc.op("tensor", lambda e, slot=slot, hc=hc, ic0=ic0, t0=t0, tw=tw, bank=bank: e.matmul(
                                self.pu[bank][:, 0:tw],
                                gubv[:, 2 * slot + 1, hc, ic0:ic0 + P],
                                xnv[:, hc, t0:t0 + tw],
                                start=(hc == 0), stop=(hc == c.HC - 1)),
                                waits=([(self.s_ae, pu_war[bank])] if hc == 0 else []),
                                inc=self.s_gu)
                        sv = sc.op("scalar", lambda e, bank=bank, tw=tw, gslot=gslot: e.activation(
                            gsv[:, gslot, 0:tw], self.pg[bank][:, 0:tw], AF.Silu),
                            waits=[(self.s_gu, gm), (self.s_ae, gs_war[gslot]),
                                   (self.s_sc, self.s_sc.n)],
                            inc=self.s_si)
                        pg_war[bank] = sv
                        av = sc.op("vector", lambda e, bank=bank, tw=tw, gslot=gslot, ic=ic, t0=t0: e.tensor_mul(
                            actv[:, ic, t0:t0 + tw], gsv[:, gslot, 0:tw],
                            self.pu[bank][:, 0:tw]),
                            waits=[(self.s_gu, um), (self.s_si, sv),
                                   (self.s_dn, act_rd), (self.s_vc, self.s_vc.n)],
                            inc=self.s_ae)
                        pu_war[bank] = av
                        gs_war[gslot] = av
                gu_war[slot] = self.s_gu.n
                nbi = bi + 2
                if nbi < len(ib_list):
                    gu_load(ei, nbi, gu_war[slot])
                elif ei + 1 < c.E:
                    gu_load(ei + 1, nbi - len(ib_list), gu_war[slot])
                if bi >= 1:
                    down_load_some(2)
            down_load_some(c.IC)
            dl_all = dl_last[0]
            act_done = self.s_ae.n
            a_end = self.s_gu.n
            if ei + 1 < c.E:
                prep(ei + 1, (self.s_gu, a_end))
            bank_i = 0
            DHW = min(1024, c.D)
            for stt_i in range(c.CT):
                for dh, (d0, dhw) in enumerate(chunks(c.D, DHW)):
                    dmm = 0
                    for (dc0, dcw) in chunks(dhw, 512):
                        bank = bank_i % 2
                        bank_i += 1
                        dm = 0
                        for ic in range(c.IC):
                            w = ([(self.s_ae, act_done), (self.s_de, pd_war[bank]),
                                  (self.s_dw, dl_all)] if ic == 0 else [])
                            dm = sc.op("tensor", lambda e, ic=ic, stt_i=stt_i, d0=d0, dc0=dc0, dcw=dcw, bank=bank: e.matmul(
                                self.pd[bank][:, 0:dcw],
                                actv[:, ic, stt_i * P:(stt_i + 1) * P],
                                dwv[:, ic, d0 + dc0:d0 + dc0 + dcw],
                                start=(ic == 0), stop=(ic == c.IC - 1)),
                                waits=w, inc=self.s_dn)
                        ss = stt_i % 2
                        pd_war[bank] = sc.op("vector", lambda e, ss=ss, dc0=dc0, dcw=dcw, bank=bank, es=es, stt_i=stt_i, ei=ei: e.tensor_scalar(
                            ssv[:, ss, dc0:dc0 + dcw], self.pd[bank][:, 0:dcw],
                            wgv[:, es, stt_i, ei:ei + 1], None, ALU.mult),
                            waits=[(self.s_dn, dm), ([self.s_ss0, self.s_ss1][ss], sst_war[ss]),
                                   gw_mark, (self.s_vc, self.s_vc.n)],
                            inc=self.s_de)
                        dmm = self.s_de.n
                    ss = stt_i % 2
                    first_sc = stt_i == 0 and dh == 0
                    sst_war[ss] = sc.op("gpsimd", lambda e, ss=ss, d0=d0, dhw=dhw, es=es, stt_i=stt_i: e.indirect_dma_start(
                        self.y_d,
                        bass.IndirectOffsetOnAxis(self.idxc[:, es * c.CT + stt_i: es * c.CT + stt_i + 1], 0),
                        ssv[:, ss, 0:dhw], None,
                        element_offset=d0,
                        bounds_check=self.bnd_t, oob_is_err=False,
                        compute_op=ALU.add),
                        waits=[(self.s_de, dmm), (self.s_ix, il_mark)] +
                              ([(self.s_ss0, prev_ss[0]), (self.s_ss1, prev_ss[1])]
                               if first_sc else []),
                        inc=[self.s_ss0, self.s_ss1][ss], inc_by=16)
            act_rd = self.s_dn.n
            dwn_rd = self.s_dn.n
            wrg_war[es] = self.s_de.n
            prev_ss = (self.s_ss0.n, self.s_ss1.n)
            idx_war[es] = [(self.s_ss0, prev_ss[0]), (self.s_ss1, prev_ss[1])]
        self.moe_sst = (self.s_ss0.n, self.s_ss1.n)
        self.moe_gu = self.s_gu.n


def build(cfg: Cfg, stop_after="moe", qt_limit=None, head_limit=None, no_stage4=False):
    return Builder(cfg, stop_after, qt_limit, head_limit, no_stage4).build()


_CFG = Cfg(T=2048, D=2048, NH=16, NG=4, E=16, DI=1792, N_HALVES=2, WQB=256,
           GIB=128, CAP=1408)
_NCORES = 8


def kernel(L0_attn_x, L0_attn_Wq, L0_attn_Wk, L0_attn_Wv, L0_attn_Wo, L0_attn_bo,
           L0_ffn_router_w, L0_ffn_expert_bias, L0_ffn_gate_proj, L0_ffn_up_proj,
           L0_ffn_down_proj):
    from concourse.bass_utils import run_bass_kernel_spmd
    c = _CFG
    nc = build(c)
    x = np.ascontiguousarray(np.asarray(L0_attn_x, dtype=np.float32))
    shared = {
        "wq": np.ascontiguousarray(np.asarray(L0_attn_Wq, dtype=np.float32)),
        "wk": np.ascontiguousarray(np.asarray(L0_attn_Wk, dtype=np.float32)),
        "wv": np.ascontiguousarray(np.asarray(L0_attn_Wv, dtype=np.float32)),
        "wo": np.ascontiguousarray(np.asarray(L0_attn_Wo, dtype=np.float32)),
        "bo": np.ascontiguousarray(np.asarray(L0_attn_bo, dtype=np.float32).reshape(1, c.D)),
        "rw": np.ascontiguousarray(np.asarray(L0_ffn_router_w, dtype=np.float32)),
        "eb": np.ascontiguousarray(np.asarray(L0_ffn_expert_bias, dtype=np.float32).reshape(1, c.E)),
        "gate": np.ascontiguousarray(np.asarray(L0_ffn_gate_proj, dtype=np.float32)),
        "up": np.ascontiguousarray(np.asarray(L0_ffn_up_proj, dtype=np.float32)),
        "down": np.ascontiguousarray(np.asarray(L0_ffn_down_proj, dtype=np.float32)),
    }
    in_maps = [dict(shared, x=np.ascontiguousarray(x[i])) for i in range(_NCORES)]
    res = run_bass_kernel_spmd(nc, in_maps, core_ids=list(range(_NCORES)))
    return np.stack([r["y"] for r in res.results]).astype(np.float32)


# version 11
# speedup vs baseline: 1.0775x; 1.0775x over previous
from dataclasses import dataclass
from contextlib import ExitStack
import numpy as np
import ml_dtypes

import concourse.bass as bass
import concourse.mybir as mybir

F32 = mybir.dt.float32
BF16 = mybir.dt.bfloat16
AF = mybir.ActivationFunctionType
ALU = mybir.AluOpType
AX = mybir.AxisListType

P = 128


@dataclass
class Cfg:
    T: int = 2048
    D: int = 2048
    NH: int = 16
    NG: int = 4
    E: int = 16
    DI: int = 1792
    N_HALVES: int = 2
    WQB: int = 256
    GIB: int = 128
    CAP: int = 0

    @property
    def TT(self): return self.T // P
    @property
    def HC(self): return self.D // P
    @property
    def GS(self): return self.NH // self.NG
    @property
    def DKV(self): return self.NG * P
    @property
    def IC(self): return self.DI // P
    @property
    def TH(self): return self.T // self.N_HALVES
    @property
    def TTH(self): return self.TH // P
    @property
    def CT(self): return self.CAP // P


def chunks(total, step):
    o, out = 0, []
    while o < total:
        out.append((o, min(step, total - o)))
        o += step
    return out


class Sem:
    __slots__ = ("name", "n", "handle")

    def __init__(self, name, handle):
        self.name, self.n, self.handle = name, 0, handle


class Sched:
    ENGINES = ("tensor", "vector", "scalar", "sync", "gpsimd")

    def __init__(self, nc, ctx):
        self.nc, self.ctx = nc, ctx
        self.ops = {e: [] for e in self.ENGINES}

    def sem(self, name):
        return Sem(name, self.ctx.enter_context(self.nc.semaphore(name)))

    def op(self, eng, fn, waits=(), inc=None, inc_by=1, defer=None):
        if inc is not None:
            inc.n += inc_by
        rec = (tuple((s, v) for (s, v) in waits if v > 0), fn, inc, inc_by)
        (self.ops[eng] if defer is None else defer).append(rec)
        return inc.n if inc else 0

    def splice(self, eng, deferred, k=10 ** 9):
        for _ in range(min(k, len(deferred))):
            self.ops[eng].append(deferred.pop(0))

    def playback(self, block):
        def make(name):
            def run(eng):
                last = {}
                for waits, fn, inc, inc_by in self.ops[name]:
                    for (s, v) in waits:
                        if last.get(id(s), -1) < v:
                            eng.wait_ge(s.handle, v)
                            last[id(s)] = v
                    ins = fn(eng)
                    if inc is not None:
                        ins.then_inc(inc.handle, inc_by)
            return run
        block.tensor(make("tensor"))
        block.vector(make("vector"))
        block.scalar(make("scalar"))
        block.sync(make("sync"))
        block.gpsimd(make("gpsimd"))


class Builder:
    def __init__(self, cfg: Cfg, stop_after="moe", qt_limit=None, head_limit=None, no_stage4=False):
        self.c = cfg
        self.stop_after = stop_after
        self.qt_limit = qt_limit
        self.head_limit = head_limit
        self.no_stage4 = no_stage4
        self.nc = bass.Bass("TRN2", target_bir_lowering=False)

    def build(self):
        c, nc = self.c, self.nc
        self.x_d = nc.dram_tensor("x", [c.T, c.D], F32, kind="ExternalInput").ap()
        self.wq_d = nc.dram_tensor("wq", [c.D, c.D], F32, kind="ExternalInput").ap()
        self.wk_d = nc.dram_tensor("wk", [c.D, c.DKV], F32, kind="ExternalInput").ap()
        self.wv_d = nc.dram_tensor("wv", [c.D, c.DKV], F32, kind="ExternalInput").ap()
        self.wo_d = nc.dram_tensor("wo", [c.D, c.D], F32, kind="ExternalInput").ap()
        self.bo_d = nc.dram_tensor("bo", [1, c.D], F32, kind="ExternalInput").ap()
        self.rw_d = nc.dram_tensor("rw", [c.E, c.D], F32, kind="ExternalInput").ap()
        self.eb_d = nc.dram_tensor("eb", [1, c.E], F32, kind="ExternalInput").ap()
        self.gate_d = nc.dram_tensor("gate", [c.E, c.D, c.DI], F32, kind="ExternalInput").ap()
        self.up_d = nc.dram_tensor("up", [c.E, c.D, c.DI], F32, kind="ExternalInput").ap()
        self.down_d = nc.dram_tensor("down", [c.E, c.DI, c.D], F32, kind="ExternalInput").ap()
        self.y_d = nc.dram_tensor("y", [c.T, c.D], F32, kind="ExternalOutput").ap()
        self.x2_d = nc.dram_tensor("x2s", [c.T, c.D], F32, kind="Internal").ap()
        self.qt_d = nc.dram_tensor("qts", [c.D, c.T], BF16, kind="Internal").ap()
        self.xn2_d = nc.dram_tensor("xn2s", [c.D, c.T], BF16, kind="Internal").ap()
        if c.CAP:
            self.xntm_d = nc.dram_tensor("xntm", [c.T, c.D], BF16, kind="Internal").ap()
            self.wr_d = nc.dram_tensor("wrd", [c.T, c.E], F32, kind="Internal").ap()
            self.idx_d = nc.dram_tensor("idxd", [c.E * c.CAP, 1], mybir.dt.int32, kind="Internal").ap()
            self.tot_d = nc.dram_tensor("totd", [c.TT, c.E], F32, kind="Internal").ap()
            self.off_d = nc.dram_tensor("offd", [1, c.TT * c.E], F32, kind="Internal").ap()

        self.ident_d = nc.inline_tensor(np.eye(P, dtype=ml_dtypes.bfloat16), "identc")
        self.identf_d = nc.inline_tensor(np.eye(P, dtype=np.float32), "identfc")
        m01 = np.triu(np.ones((P, P))).astype(ml_dtypes.bfloat16)
        self.mask_d = nc.inline_tensor(m01, "mask01c")
        mA = np.where(np.triu(np.ones((P, P), bool), 1), np.float32(-1e30), np.float32(0.0))
        self.maskA_d = nc.inline_tensor(mA.astype(np.float32), "maskAc")
        if self.c.CAP:
            self.m01f_d = nc.inline_tensor(np.triu(np.ones((P, P), np.float32)), "m01fc")
            self.s01f_d = nc.inline_tensor(np.triu(np.ones((P, P), np.float32), 1), "s01fc")
            iota = (np.arange(P)[:, None] + P * np.arange(self.c.TT)[None, :]).astype(np.int32)
            self.iota_d = nc.inline_tensor(iota, "iotac")
            ecr = (np.arange(self.c.E, dtype=np.float32) * self.c.CAP)[None, :]
            self.ecr_d = nc.inline_tensor(ecr, "ecrc")
            self.sent_d = nc.inline_tensor(np.full((P, self.c.E * self.c.CAP // P), self.c.T, np.int32), "sentc")
            thr = np.tile(((np.arange(self.c.E) + 1) * self.c.CAP).astype(np.float32), (P, 1))
            self.thr_d = nc.inline_tensor(thr, "thrc")

        self.ctx = ExitStack()
        with self.ctx:
            self.sc = Sched(nc, self.ctx)
            self.mk_sems()
            self.scope0()
            kv_guards = self.enter_kv_scope()
            ln_guards = self.enter_ln_scope()
            self.phase_consts()
            self.phase_ln1()
            if self.stop_after == "ln":
                self.sc.op("sync", lambda e: e.nop(),
                           waits=[(self.s_xn1, self.ln1_done), (self.s_mm, self.s_mm.n)])
            else:
                self.phase_qkv()
            for g in reversed(ln_guards):
                g.__exit__(None, None, None)
            if self.stop_after == "qkv":
                self.sc.op("sync", lambda e: e.nop(),
                           waits=[(self.s_ev, self.qkv_ev_done),
                                  (self.s_q0, self.qst_war[0]), (self.s_q1, self.qst_war[1])])
            if self.stop_after in ("attn", "moe"):
                at_guards = self.enter_attn_scope()
                self.phase_attention()
                for g in reversed(at_guards):
                    g.__exit__(None, None, None)
            for g in reversed(kv_guards):
                g.__exit__(None, None, None)
            if self.stop_after == "attn":
                self.sc.op("sync", lambda e: e.nop(),
                           waits=[(self.s_l2, self.w_done), (self.s_x2st, self.attn_x2),
                                  (self.s_xbst, self.attn_xb), (self.s_sc, self.s_sc.n),
                                  (self.s_tr, self.s_tr.n), (self.s_vc, self.s_vc.n)])
            if self.stop_after == "moe":
                if self.c.CAP:
                    moe_guards = self.enter_moe_scope_sparse()
                    self.phase_moe_sparse()
                    for g in reversed(moe_guards):
                        g.__exit__(None, None, None)
                    self.sc.op("sync", lambda e: e.nop(),
                               waits=[(self.s_ss0, self.moe_sst[0]),
                                      (self.s_ss1, self.moe_sst[1]),
                                      (self.s_x2st, self.attn_x2)])
                else:
                    moe_guards = self.enter_moe_scope()
                    self.phase_moe()
                    for g in reversed(moe_guards):
                        g.__exit__(None, None, None)
                    self.sc.op("sync", lambda e: e.nop(),
                               waits=[(self.s_yst, self.s_yst.n)])
            block = self.ctx.enter_context(nc.Block())
            self.sc.playback(block)
        return nc

    def mk_sems(self):
        s = self.sc
        for name in ("s_x s_xn1 s_wld s_mm s_ev s_vc s_sc s_p1 s_rm s_nr s_p2 "
                     "s_pe s_pb s_cs s_ri s_cx s_ce s_wo s_x2 s_l2 s_tr s_te "
                     "s_st s_qs s_wg s_gu s_ae s_si s_dn s_de s_yi s_xl s_ms "
                     "s_cst s_bo s_x0 s_x1 s_wol s_w0 s_w1 s_q0 s_q1 s_qs0 s_qs1 "
                     "s_tm s_kp s_rb s_sx s_ix s_gx s_tp s_sst s_ycb s_ps "
                     "s_bc s_wa s_wb s_yc0 s_yc1 s_gw0 s_gw1 s_gx0 s_gx1 s_ss0 s_ss1 s_sx0 s_sx1 s_yo0 s_yo1 "
                     "s_x2st s_xbst s_yst s_g0 s_g1 s_dw s_cw").split():
            setattr(self, name, s.sem(name))

    def scope0(self):
        nc, c, ctx = self.nc, self.c, self.ctx
        SB = lambda n, s, d: ctx.enter_context(nc.sbuf_tensor(n, s, d))
        self.w_all = SB("w_allsb", [P, c.TT * c.E], F32)
        self.ident = SB("identsb", [P, P], BF16)
        self.pw = ctx.enter_context(nc.psum_tensor("pw", [P, 512], F32))

    def phase_consts(self):
        sc, c = self.sc, self.c
        sc.op("sync", lambda e: e.dma_start(self.ident[:], self.ident_d.ap()), inc=self.s_cst, inc_by=16)
        sc.op("sync", lambda e: e.dma_start(self.identf[:], self.identf_d.ap()), inc=self.s_cst, inc_by=16)
        sc.op("sync", lambda e: e.dma_start(self.maskT[:], self.mask_d.ap()), inc=self.s_cst, inc_by=16)
        sc.op("sync", lambda e: e.dma_start(self.maskA[:], self.maskA_d.ap()), inc=self.s_cst, inc_by=16)
        sc.op("sync", lambda e: e.dma_start(self.eb_f[:], self.eb_d), inc=self.s_cst, inc_by=16)
        sc.op("sync", lambda e: e.dma_start(self.rw_s[:], self.rw_d), inc=self.s_cst, inc_by=16)
        self.w_const = self.s_cst.n
        sc.op("gpsimd", lambda e: e.dma_start(self.bo_bf[:], self.bo_d), inc=self.s_bo, inc_by=16)
        sc.op("gpsimd", lambda e: e.memset(self.ones_r1[:], 1.0))
        sc.op("gpsimd", lambda e: e.memset(self.ones_c[:], 1.0))
        sc.op("gpsimd", lambda e: e.memset(self.ones_cf[:], 1.0))
        sc.op("gpsimd", lambda e: e.memset(self.ones_rf[:], 1.0))
        sc.op("gpsimd", lambda e: e.memset(self.eps_c[:], 1e-5), inc=self.s_ms, inc_by=1)
        self.w_ones = self.s_ms.n
        for hc in range(c.HC):
            v = sc.op("tensor", lambda e, hc=hc: e.transpose(
                self.pw[:, 0:c.E], self.rw_s[:, hc * P:(hc + 1) * P],
                self.identf[0:c.E, 0:c.E]),
                waits=[(self.s_cst, self.w_const), (self.s_ms, self.w_ones),
                       (self.s_ev, self.s_ev.n)],
                inc=self.s_mm)
            sc.op("scalar", lambda e, hc=hc: e.copy(
                self.rwT[:, hc * c.E:(hc + 1) * c.E], self.pw[:, 0:c.E]),
                waits=[(self.s_mm, v)], inc=self.s_ev)
        self.rwT_done = self.s_ev.n

    def enter_kv_scope(self):
        nc, c = self.nc, self.c
        gs = [nc.sbuf_tensor("kT", [P, c.NG * c.T], BF16),
              nc.sbuf_tensor("vN", [P, c.TT * c.DKV], BF16),
              nc.sbuf_tensor("xring", [P, 2 * c.D], F32),
              nc.sbuf_tensor("stat", [P, 48], F32),
              nc.sbuf_tensor("xnorm", [P, c.D], F32),
              nc.sbuf_tensor("identfsb", [P, P], F32),
              nc.sbuf_tensor("masksb", [P, P], BF16),
              nc.sbuf_tensor("maskAsb", [P, P], F32),
              nc.sbuf_tensor("ones_r1", [1, P], BF16),
              nc.sbuf_tensor("ones_c", [P, 1], BF16),
              nc.sbuf_tensor("ones_cf", [P, 1], F32),
              nc.sbuf_tensor("ones_rf", [1, P], F32),
              nc.sbuf_tensor("eps_c", [P, 1], F32),
              nc.sbuf_tensor("bo_bf", [1, c.D], BF16),
              nc.sbuf_tensor("eb_f", [1, c.E], F32),
              nc.sbuf_tensor("rwT", [P, c.HC * c.E], F32)]
        (self.kT, self.vN, self.xring, self.stat, self.xnorm,
         self.identf, self.maskT, self.maskA, self.ones_r1, self.ones_c,
         self.ones_cf, self.ones_rf, self.eps_c, self.bo_bf, self.eb_f,
         self.rwT) = [g.__enter__() for g in gs]
        return gs

    def enter_ln_scope(self):
        nc, c = self.nc, self.c
        gs = [nc.sbuf_tensor("rw_s", [c.E, c.D], F32),
              nc.sbuf_tensor("xn1T", [P, c.HC * c.T], BF16),
              nc.sbuf_tensor("wqb", [P, 2 * c.HC * c.WQB], BF16),
              nc.sbuf_tensor("xnbf", [P, c.D], BF16),
              nc.sbuf_tensor("qstage", [P, 2 * 512], BF16),
              nc.psum_tensor("tpA", [P, 512], BF16),
              nc.psum_tensor("tpB", [P, 512], BF16),
              nc.psum_tensor("qk0", [P, 512], F32),
              nc.psum_tensor("qk1", [P, 512], F32),
              nc.psum_tensor("qk2", [P, 512], F32)]
        (self.rw_s, self.xn1T, self.wqb, self.xnbf, self.qstage,
         tpA, tpB, qk0, qk1, qk2) = [g.__enter__() for g in gs]
        self.tp = [tpA, tpB]
        self.qk = [self.pw, qk0, qk1, qk2]
        return gs

    def enter_attn_scope(self):
        nc, c = self.nc, self.c
        gs = [nc.sbuf_tensor("qs", [P, 2 * c.HC * P], BF16),
              nc.sbuf_tensor("probs", [P, 4 * c.T], BF16),
              nc.sbuf_tensor("mxs", [P, 4 * 8], F32),
              nc.sbuf_tensor("esc", [P, 6 * c.T], BF16),
              nc.sbuf_tensor("ssc", [P, 96], F32),
              nc.sbuf_tensor("ctxT", [P, 2 * c.HC * P], BF16),
              nc.sbuf_tensor("x2t", [P, c.D], F32),
              nc.sbuf_tensor("xn2f", [P, c.HC * P], F32),
              nc.sbuf_tensor("xn2b", [P, (1 if c.CAP else c.HC) * P], BF16),
              nc.sbuf_tensor("rc", [P, 64], F32),
              nc.sbuf_tensor("xtmb", [P, c.D], BF16),
              nc.sbuf_tensor("wo_sb", [P, c.HC * c.D], BF16),
              nc.psum_tensor("S1a", [P, 512], F32),
              nc.psum_tensor("S1b", [P, 512], F32),
              nc.psum_tensor("S2a", [P, 512], BF16),
              nc.psum_tensor("S2b", [P, 512], BF16),
              nc.psum_tensor("C", [P, 512], F32),
              nc.psum_tensor("DR", [P, 512], F32),
              nc.psum_tensor("LTL", [P, 512], F32)]
        (self.qs, self.probs, self.mxs, self.esc,
         self.ssc, self.ctxT, self.x2t, self.xn2f, self.xn2b, self.rc,
         self.xtmb, self.wo_sb, S1a, S1b, S2a, S2b, self.C, self.DR, self.LTL) = \
            [g.__enter__() for g in gs]
        self.S1 = [S1a, S1b]
        self.S2b = [S2a, S2b]
        c = self.c
        for hcb in range(0, c.HC, 4):
            w = min(4, c.HC - hcb)
            self.sc.op("gpsimd", lambda e, hcb=hcb, w=w: e.dma_start(
                self.wo_sb[:].rearrange("p (cc d) -> p cc d", cc=c.HC)[:, hcb:hcb + w, :],
                self.wo_d[hcb * P:(hcb + w) * P, :].rearrange("(cc p) d -> p cc d", p=P)),
                waits=[(self.s_mm, self.qkv_mm_done), (self.s_ev, self.qkv_ev_done),
                       (self.s_q0, self.qst_war[0]), (self.s_q1, self.qst_war[1])],
                inc=self.s_wol, inc_by=16)
        self.wo_ld = self.s_wol.n
        return gs

    def enter_moe_scope(self):
        nc, c = self.nc, self.c
        gub_w = 2 * 2 * c.HC * c.GIB
        gs = [nc.sbuf_tensor("xn2h", [P, c.HC * c.TH], BF16),
              nc.sbuf_tensor("acc", [P, c.TTH * c.D], F32),
              nc.sbuf_tensor("act", [P, c.IC * c.TH], BF16),
              nc.sbuf_tensor("gub", [P, gub_w], BF16),
              nc.sbuf_tensor("dwn", [P, c.IC * c.D], BF16),
              nc.sbuf_tensor("gssb", [P, 2 * 512], BF16),
              nc.psum_tensor("pg0", [P, 512], F32),
              nc.psum_tensor("pg1", [P, 512], F32),
              nc.psum_tensor("pu0", [P, 512], F32),
              nc.psum_tensor("pu1", [P, 512], F32),
              nc.psum_tensor("pd0", [P, 512], F32),
              nc.psum_tensor("pd1", [P, 512], F32),
              nc.psum_tensor("pd2", [P, 512], F32)]
        (self.xn2h, self.acc, self.act, self.gub, self.dwn, self.gsb,
         pg0, pg1, pu0, pu1, pd0, pd1, pd2) = [g.__enter__() for g in gs]
        self.pg = [pg0, pg1]
        self.pu = [pu0, pu1]
        self.pd = [pd0, pd1, pd2]
        return gs

    def phase_ln1(self):
        sc, c = self.sc, self.c
        xv = self.xn1T[:].rearrange("p (cc t) -> p cc t", cc=c.HC)
        ln_v = [0, 0]
        ln_s = [0, 0]
        tp_war = [0, 0]
        nbatch = (c.HC + 3) // 4
        for tt in range(c.TT):
            xo = (tt % 2) * c.D
            waits = ([(self.s_vc, ln_v[tt % 2]), (self.s_sc, ln_s[tt % 2])]
                     if tt >= 2 else [])
            sx = [self.s_x0, self.s_x1][tt % 2]
            xw = sc.op("sync", lambda e, tt=tt, xo=xo: e.dma_start(
                self.xring[:, xo:xo + c.D], self.x_d[tt * P:(tt + 1) * P, :]),
                waits=waits, inc=sx, inc_by=16)
            sc.op("vector", lambda e, xo=xo: e.reduce_sum(
                self.stat[:, 0:1], self.xring[:, xo:xo + c.D], axis=AX.X),
                waits=[(sx, xw), (self.s_vc, self.s_vc.n)], inc=self.s_vc)
            sq = sc.op("scalar", lambda e, xo=xo: e.activation(
                self.xnorm[:], self.xring[:, xo:xo + c.D], AF.Square),
                waits=[(sx, xw), (self.s_sc, self.s_sc.n)], inc=self.s_sc)
            sc.op("vector", lambda e: e.reduce_sum(self.stat[:, 1:2], self.xnorm[:], axis=AX.X),
                  waits=[(self.s_sc, sq), (self.s_vc, self.s_vc.n)], inc=self.s_vc)
            sc.op("vector", lambda e: e.tensor_scalar(
                self.stat[:, 2:3], self.stat[:, 0:1], 1.0 / c.D, None, ALU.mult),
                waits=[(self.s_vc, self.s_vc.n)], inc=self.s_vc)
            sc.op("vector", lambda e: e.tensor_scalar(
                self.stat[:, 3:4], self.stat[:, 2:3], self.stat[:, 2:3], None, ALU.mult),
                waits=[(self.s_vc, self.s_vc.n)], inc=self.s_vc)
            vv = sc.op("vector", lambda e: e.scalar_tensor_tensor(
                self.stat[:, 4:5], self.stat[:, 1:2], 1.0 / c.D, self.stat[:, 3:4],
                ALU.mult, ALU.subtract),
                waits=[(self.s_vc, self.s_vc.n)], inc=self.s_vc)
            sc.op("scalar", lambda e: e.activation(
                self.stat[:, 5:6], self.stat[:, 4:5], AF.Ln, bias=self.eps_c[:]),
                waits=[(self.s_vc, vv), (self.s_sc, self.s_sc.n)], inc=self.s_sc)
            rs = sc.op("scalar", lambda e: e.activation(
                self.stat[:, 6:7], self.stat[:, 5:6], AF.Exp, scale=-0.5),
                waits=[(self.s_sc, self.s_sc.n)], inc=self.s_sc)
            nv = sc.op("vector", lambda e, xo=xo: e.tensor_scalar(
                self.xnbf[:], self.xring[:, xo:xo + c.D], self.stat[:, 2:3],
                self.stat[:, 6:7], ALU.subtract, ALU.mult),
                waits=[(self.s_sc, rs), (self.s_vc, self.s_vc.n)], inc=self.s_vc)
            ln_v[tt % 2] = nv
            ln_s[tt % 2] = rs
            for bi in range(nbatch):
                b0 = bi * 4
                bn = min(4, c.HC - b0)
                bank = (tt * nbatch + bi) % 2
                mmv = 0
                for j in range(bn):
                    mmv = sc.op("tensor", lambda e, b0=b0, j=j, bank=bank: e.transpose(
                        self.tp[bank][:, j * P:(j + 1) * P],
                        self.xnbf[:, (b0 + j) * P:(b0 + j + 1) * P], self.ident[:]),
                        waits=[(self.s_vc, nv), (self.s_xn1, tp_war[bank])],
                        inc=self.s_mm)
                tp_war[bank] = sc.op("scalar", lambda e, b0=b0, bn=bn, tt=tt, bank=bank: e.copy(
                    xv[:, b0:b0 + bn, tt * P:(tt + 1) * P], self.tp[bank][:, 0:bn * P]),
                    waits=[(self.s_mm, mmv), (self.s_sc, self.s_sc.n)], inc=self.s_xn1)
        self.ln1_done = self.s_xn1.n

    def phase_qkv(self):
        sc, c = self.sc, self.c
        ln_all = self.ln1_done
        qkw = [0, 0, 0, 0]
        bank_i = 0
        wq_war = [0, 0]
        self.qst_war = [0, 0]
        kTv = self.kT[:].rearrange("p (g t) -> p g t", g=c.NG)
        vNv = self.vN[:].rearrange("p (t d) -> p t d", t=c.TT)
        wqbv = self.wqb[:].rearrange("p (s cc w) -> p s cc w", s=2, cc=c.HC)
        qsv = self.qstage[:].rearrange("p (s w) -> p s w", s=2)
        slot = 0
        qi = 0

        def wload(slot, src_ap, width, wait_mm):
            sw = [self.s_w0, self.s_w1][slot]
            return (sw, sc.op("gpsimd", lambda e, slot=slot, src_ap=src_ap, width=width: e.dma_start(
                wqbv[:, slot, :, 0:width], src_ap),
                waits=[(self.s_mm, wait_mm)], inc=sw, inc_by=16))

        for (dq0, dqw) in chunks(c.D, c.WQB):
            src = self.wq_d[:, dq0:dq0 + dqw].rearrange("(cc p) w -> p cc w", p=P)
            swl, wl = wload(slot, src, dqw, wq_war[slot])
            for (dqc0, dqcw) in chunks(dqw, P):
                for (t0, tw) in chunks(c.T, 512):
                    bank = bank_i % 4
                    bank_i += 1
                    pb = self.qk[bank]
                    mmv = 0
                    for hc in range(c.HC):
                        mmv = sc.op("tensor", lambda e, slot=slot, hc=hc, dqc0=dqc0, dqcw=dqcw, t0=t0, tw=tw, pb=pb: e.matmul(
                            pb[0:dqcw, 0:tw],
                            wqbv[:, slot, hc, dqc0:dqc0 + dqcw],
                            self.xn1T[:, hc * c.T + t0: hc * c.T + t0 + tw],
                            start=(hc == 0), stop=(hc == c.HC - 1)),
                            waits=([(self.s_xn1, ln_all), (swl, wl),
                                    (self.s_ev, qkw[bank])] if hc == 0 else []),
                            inc=self.s_mm)
                    qslot = qi % 2
                    qi += 1
                    sq_sem = [self.s_q0, self.s_q1][qslot]
                    ev = sc.op("scalar", lambda e, qslot=qslot, dqcw=dqcw, tw=tw, pb=pb: e.copy(
                        qsv[0:dqcw, qslot, 0:tw], pb[0:dqcw, 0:tw]),
                        waits=[(self.s_mm, mmv), (sq_sem, self.qst_war[qslot]),
                               (self.s_sc, self.s_sc.n)],
                        inc=self.s_ev)
                    qkw[bank] = ev
                    self.qst_war[qslot] = sc.op("sync", lambda e, qslot=qslot, dq0=dq0, dqc0=dqc0, dqcw=dqcw, t0=t0, tw=tw: e.dma_start(
                        self.qt_d[dq0 + dqc0:dq0 + dqc0 + dqcw, t0:t0 + tw],
                        qsv[0:dqcw, qslot, 0:tw]),
                        waits=[(self.s_ev, ev)], inc=sq_sem, inc_by=16)
            wq_war[slot] = self.s_mm.n
            slot = 1 - slot

        for (dk0, dkw) in chunks(c.DKV, c.WQB):
            src = self.wk_d[:, dk0:dk0 + dkw].rearrange("(cc p) w -> p cc w", p=P)
            swl, wl = wload(slot, src, dkw, wq_war[slot])
            for (dkc0, dkcw) in chunks(dkw, P):
                g = (dk0 + dkc0) // P
                for (t0, tw) in chunks(c.T, 512):
                    bank = bank_i % 4
                    bank_i += 1
                    pb = self.qk[bank]
                    mmv = 0
                    for hc in range(c.HC):
                        mmv = sc.op("tensor", lambda e, slot=slot, hc=hc, dkc0=dkc0, dkcw=dkcw, t0=t0, tw=tw, pb=pb: e.matmul(
                            pb[0:dkcw, 0:tw],
                            wqbv[:, slot, hc, dkc0:dkc0 + dkcw],
                            self.xn1T[:, hc * c.T + t0: hc * c.T + t0 + tw],
                            start=(hc == 0), stop=(hc == c.HC - 1)),
                            waits=([(self.s_xn1, ln_all), (swl, wl),
                                    (self.s_ev, qkw[bank])] if hc == 0 else []),
                            inc=self.s_mm)
                    qkw[bank] = sc.op("scalar", lambda e, g=g, t0=t0, tw=tw, pb=pb, dkcw=dkcw: e.copy(
                        kTv[0:dkcw, g, t0:t0 + tw], pb[0:dkcw, 0:tw]),
                        waits=[(self.s_mm, mmv), (self.s_sc, self.s_sc.n)], inc=self.s_ev)
            wq_war[slot] = self.s_mm.n
            slot = 1 - slot

        for (vb0, vbw) in chunks(c.DKV, c.WQB):
            src = self.wv_d[:, vb0:vb0 + vbw].rearrange("(cc p) w -> p cc w", p=P)
            swl, wl = wload(slot, src, vbw, wq_war[slot])
            for tt in range(c.TT):
                bank = bank_i % 4
                bank_i += 1
                pb = self.qk[bank]
                mmv = 0
                for hc in range(c.HC):
                    mmv = sc.op("tensor", lambda e, slot=slot, hc=hc, tt=tt, vbw=vbw, pb=pb: e.matmul(
                        pb[:, 0:vbw],
                        self.xn1T[:, hc * c.T + tt * P: hc * c.T + (tt + 1) * P],
                        wqbv[:, slot, hc, 0:vbw],
                        start=(hc == 0), stop=(hc == c.HC - 1)),
                        waits=([(self.s_xn1, ln_all), (swl, wl),
                                (self.s_ev, qkw[bank])] if hc == 0 else []),
                        inc=self.s_mm)
                qkw[bank] = sc.op("scalar", lambda e, tt=tt, vb0=vb0, vbw=vbw, pb=pb: e.copy(
                    vNv[:, tt, vb0:vb0 + vbw], pb[:, 0:vbw]),
                    waits=[(self.s_mm, mmv), (self.s_sc, self.s_sc.n)], inc=self.s_ev)
            wq_war[slot] = self.s_mm.n
            slot = 1 - slot

        self.qkv_mm_done = self.s_mm.n
        self.qkv_ev_done = self.s_ev.n

    def join_all(self, waits):
        for eng in ("tensor", "vector", "scalar", "sync", "gpsimd"):
            self.sc.op(eng, lambda e: e.nop(), waits=list(waits))

    def phase_attention(self):
        sc, c = self.sc, self.c
        SCALE = float(1.0 / np.sqrt(128.0).astype(np.float32))
        kTv = self.kT[:].rearrange("p (g t) -> p g t", g=c.NG)
        vNv = self.vN[:].rearrange("p (t d) -> p t d", t=c.TT)
        qv = self.qs[:].rearrange("p (s cc w) -> p s cc w", s=2, cc=c.HC)
        prv = self.probs[:].rearrange("p (s w) -> p s w", s=4)
        escv = self.esc[:].rearrange("p (s w) -> p s w", s=6)
        ctv = self.ctxT[:].rearrange("p (s cc w) -> p s cc w", s=2, cc=c.HC)
        wov = self.wo_sb[:].rearrange("p (cc d) -> p cc d", cc=c.HC)
        xnfv = self.xn2f[:].rearrange("p (cc w) -> p cc w", cc=c.HC)
        xnbv = self.xn2b[:].rearrange("p (cc w) -> p cc w", cc=c.HC)

        st = {
            "q_ld": [0, 0], "q_war": [0, 0],
            "s1_war": [0, 0], "s1_i": 0, "s2_pe": [0, 0], "s2_i": 0,
            "pr_war": [0, 0],
            "esc_war": [0, 0, 0],
            "ctx_war": [0, 0],
            "ctx_ev": [0, 0],
            "ltl_free": 0, "xn2f_rd": 0, "xn2b_st": 0, "x2_sq": 0, "x2_st": 0,
            "xr_war": [0, 0],
        }
        stage4_q = []
        self.ce_all = {}

        def stage_a1(qt, u, hx):
            h0 = 2 * u
            g = h0 // c.GS
            nk = (qt + 1) * P
            qslot = qt % 2
            us = u % 3
            kblocks = chunks(nk, 512)
            hx.update(g=g, nk=nk, us=us, us2=u % 2, kblocks=kblocks, h0=h0)
            nch = len(kblocks)
            for hi in range(2):
                h = h0 + hi
                sl = 2 * us + hi
                for ci, (k0, kw) in enumerate(kblocks):
                    bank = st["s1_i"] % len(self.S1)
                    st["s1_i"] += 1
                    mmv = sc.op("tensor", lambda e, h=h, g=g, qslot=qslot, k0=k0, kw=kw, bank=bank: e.matmul(
                        self.S1[bank][:, 0:kw], qv[:, qslot, h, :], kTv[:, g, k0:k0 + kw],
                        start=True, stop=True),
                        waits=[([self.s_qs0, self.s_qs1][qslot], st["q_ld"][qslot]),
                               (self.s_cs, st["s1_war"][bank]),
                               (self.s_ev, self.qkv_ev_done)],
                        inc=self.s_p1)
                    if ci == nch - 1:
                        mmv_m = sc.op("vector", lambda e, kw=kw, bank=bank: e.tensor_add(
                            self.S1[bank][:, kw - P:kw], self.S1[bank][:, kw - P:kw],
                            self.maskA[:]),
                            waits=[(self.s_p1, mmv), (self.s_vc, self.s_vc.n)], inc=self.s_vc)
                        st["s1_war"][bank] = sc.op("scalar", lambda e, k0=k0, kw=kw, bank=bank, sl=sl, ci=ci: e.activation(
                            escv[:, sl, k0:k0 + kw], self.S1[bank][:, 0:kw], AF.Exp,
                            scale=SCALE,
                            accum_out=self.ssc[:, sl * 12 + ci: sl * 12 + ci + 1]),
                            waits=[(self.s_vc, mmv_m), (self.s_sc, self.s_sc.n),
                                   (self.s_p2, st["esc_war"][us])],
                            inc=self.s_cs)
                    else:
                        st["s1_war"][bank] = sc.op("scalar", lambda e, k0=k0, kw=kw, bank=bank, sl=sl, ci=ci: e.activation(
                            escv[:, sl, k0:k0 + kw], self.S1[bank][:, 0:kw], AF.Exp,
                            scale=SCALE,
                            accum_out=self.ssc[:, sl * 12 + ci: sl * 12 + ci + 1]),
                            waits=[(self.s_p1, mmv), (self.s_sc, self.s_sc.n),
                                   (self.s_p2, st["esc_war"][us])],
                            inc=self.s_cs)
                tot = sc.op("vector", lambda e, nch=nch, sl=sl: e.reduce_sum(
                    self.stat[:, 36 + sl:37 + sl],
                    self.ssc[:, sl * 12: sl * 12 + nch], axis=AX.X),
                    waits=[(self.s_cs, self.s_cs.n), (self.s_vc, self.s_vc.n)],
                    inc=self.s_vc)
                sc.op("vector", lambda e, sl=sl: e.reciprocal(
                    self.stat[:, 42 + sl:43 + sl], self.stat[:, 36 + sl:37 + sl]),
                    waits=[(self.s_vc, tot)], inc=self.s_vc)
            hx["cv"] = self.s_vc.n

        def stage_b(qt, u, hx):
            us, us2, kblocks, nk = hx["us"], hx["us2"], hx["kblocks"], hx["nk"]
            rs = 0
            for hi in range(2):
                sl = 2 * us + hi
                rs = sc.op("vector", lambda e, nk=nk, sl=sl: e.tensor_scalar(
                    escv[:, sl, 0:nk], escv[:, sl, 0:nk],
                    self.stat[:, 42 + sl:43 + sl], None, ALU.mult),
                    waits=[(self.s_vc, hx["cv"] if hi == 0 else self.s_vc.n)],
                    inc=self.s_vc)
            pe_marks = [[], []]
            for hi in range(2):
                sl = 2 * us + hi
                psl = 2 * us2 + hi
                for gi, (kb0, kbw) in enumerate(kblocks):
                    bank = st["s2_i"] % 2
                    st["s2_i"] += 1
                    mmv = 0
                    for (kc0, kcw) in chunks(kbw, P):
                        j = kc0 // P
                        mmv = sc.op("tensor", lambda e, sl=sl, o=kb0 + kc0, j=j, bank=bank: e.transpose(
                            self.S2b[bank][:, j * P:(j + 1) * P],
                            escv[:, sl, o:o + P], self.ident[:]),
                            waits=[(self.s_vc, rs), (self.s_pe, st["s2_pe"][bank])],
                            inc=self.s_p2)
                    pe = sc.op("scalar", lambda e, bank=bank, kb0=kb0, kbw=kbw, psl=psl: e.copy(
                        prv[:, psl, kb0:kb0 + kbw], self.S2b[bank][:, 0:kbw]),
                        waits=[(self.s_p2, mmv), (self.s_cx, st["pr_war"][us2]),
                               (self.s_sc, self.s_sc.n)],
                        inc=self.s_pe)
                    st["s2_pe"][bank] = pe
                    pe_marks[hi].append(pe)
            st["esc_war"][us] = self.s_p2.n
            hx["pe_marks"] = pe_marks

        def stage_c(qt, u, hx):
            g, nk, us, kblocks = hx["g"], hx["nk"], hx["us2"], hx["kblocks"]
            h0 = hx["h0"]
            pe_marks = hx["pe_marks"]
            n_k = nk // P
            bank = self.C if u % 2 == 0 else self.DR
            cxv = 0
            ki = 0
            for bi, (kb0, kbw) in enumerate(kblocks):
                for (kc0, kcw) in chunks(kbw, P):
                    tt_k = (kb0 + kc0) // P
                    first = ki == 0
                    last = ki == n_k - 1
                    ki += 1
                    for hi in range(2):
                        sl = 2 * us + hi
                        w = [(self.s_pe, pe_marks[hi][bi])]
                        if first and hi == 0:
                            w += [(self.s_ce, st["ctx_ev"][us])]
                        cxv = sc.op("tensor", lambda e, tt_k=tt_k, g=g, sl=sl, hi=hi, o=kb0 + kc0, first=first, last=last, bank=bank: e.matmul(
                            bank[:, hi * P:(hi + 1) * P],
                            vNv[:, tt_k, g * P:(g + 1) * P],
                            prv[:, sl, o:o + P],
                            start=(first and hi == 0), stop=(last and hi == 1)),
                            waits=w, inc=self.s_cx)
            st["pr_war"][us] = cxv
            ce = sc.op("scalar", lambda e, h0=h0, qt=qt, bank=bank: e.copy(
                ctv[:, qt % 2, h0, :].tensor[0:0] if False else
                self.ctxT[:, (qt % 2) * c.HC * P + h0 * P: (qt % 2) * c.HC * P + (h0 + 2) * P],
                bank[:, 0:2 * P]),
                waits=[(self.s_cx, cxv), (self.s_wo, st["ctx_war"][qt % 2]),
                       (self.s_sc, self.s_sc.n)],
                inc=self.s_ce)
            st["ctx_ev"][us] = ce

        def stage4(qt):
            tt = qt
            cslot = qt % 2
            xslot = qt % 2
            xo = xslot * c.D
            sx = [self.s_x0, self.s_x1][xslot]
            xw = sc.op("sync", lambda e, tt=tt, xo=xo: e.dma_start(
                self.xring[:, xo:xo + c.D], self.x_d[tt * P:(tt + 1) * P, :]),
                waits=[(self.s_x2, st["xr_war"][xslot])], inc=sx, inc_by=16)
            x2_marks = []
            prev_ev = 0
            for (d0, dw) in chunks(c.D, 512):
                mmv = 0
                for hc in range(c.HC):
                    w = ([(self.s_ce, self.ce_all[qt]), (self.s_wol, self.wo_ld),
                          (self.s_x2, prev_ev if prev_ev else st["xr_war"][1 - xslot]),
                          (self.s_ev, self.qkv_ev_done)] if hc == 0 else [])
                    mmv = sc.op("tensor", lambda e, hc=hc, d0=d0, dw=dw, cslot=cslot: e.matmul(
                        self.pw[:, 0:dw], ctv[:, cslot, hc, :], wov[:, hc, d0:d0 + dw],
                        start=(hc == 0), stop=False),
                        waits=w, inc=self.s_wo, defer=stage4_q)
                mmv = sc.op("tensor", lambda e, d0=d0, dw=dw: e.matmul(
                    self.pw[:, 0:dw], self.ones_r1[:], self.bo_bf[0:1, d0:d0 + dw],
                    start=False, stop=True),
                    waits=[(self.s_ms, self.w_ones), (self.s_bo, 16)], inc=self.s_wo, defer=stage4_q)
                ev = sc.op("vector", lambda e, d0=d0, dw=dw, xo=xo: e.tensor_add(
                    self.x2t[:, d0:d0 + dw], self.pw[:, 0:dw],
                    self.xring[:, xo + d0:xo + d0 + dw]),
                    waits=[(self.s_wo, mmv), (sx, xw),
                           (self.s_x2st, st["x2_st"]), (self.s_sc, st["x2_sq"]),
                           (self.s_vc, self.s_vc.n)],
                    inc=self.s_x2)
                prev_ev = ev
                x2_marks.append(ev)
            st["ctx_war"][cslot] = self.s_wo.n
            x2v = x2_marks[-1]
            st["xr_war"][xslot] = x2v
            x2dst = self.y_d if self.c.CAP else self.x2_d
            st["x2_st"] = sc.op("sync", lambda e, tt=tt, x2dst=x2dst: e.dma_start(
                x2dst[tt * P:(tt + 1) * P, :], self.x2t[:]),
                waits=[(self.s_x2, x2v)], inc=self.s_x2st, inc_by=16)
            sc.op("vector", lambda e: e.reduce_sum(self.stat[:, 16:17], self.x2t[:], axis=AX.X),
                  waits=[(self.s_x2, x2v), (self.s_vc, self.s_vc.n)], inc=self.s_vc)
            sq = sc.op("scalar", lambda e: e.activation(self.xnorm[:], self.x2t[:], AF.Square),
                       waits=[(self.s_x2, x2v), (self.s_sc, self.s_sc.n),
                              (self.s_tr, st["xn2f_rd"])], inc=self.s_sc)
            st["x2_sq"] = sq
            sc.op("vector", lambda e: e.reduce_sum(self.stat[:, 17:18], self.xnorm[:], axis=AX.X),
                  waits=[(self.s_sc, sq), (self.s_vc, self.s_vc.n)], inc=self.s_vc)
            sc.op("vector", lambda e: e.tensor_scalar(
                self.stat[:, 18:19], self.stat[:, 16:17], 1.0 / c.D, None, ALU.mult),
                waits=[(self.s_vc, self.s_vc.n)], inc=self.s_vc)
            sc.op("vector", lambda e: e.tensor_scalar(
                self.stat[:, 19:20], self.stat[:, 18:19], self.stat[:, 18:19], None, ALU.mult),
                waits=[(self.s_vc, self.s_vc.n)], inc=self.s_vc)
            vv = sc.op("vector", lambda e: e.scalar_tensor_tensor(
                self.stat[:, 20:21], self.stat[:, 17:18], 1.0 / c.D, self.stat[:, 19:20],
                ALU.mult, ALU.subtract),
                waits=[(self.s_vc, self.s_vc.n)], inc=self.s_vc)
            sc.op("scalar", lambda e: e.activation(
                self.stat[:, 21:22], self.stat[:, 20:21], AF.Ln, bias=self.eps_c[:]),
                waits=[(self.s_vc, vv), (self.s_sc, self.s_sc.n)], inc=self.s_sc)
            rs = sc.op("scalar", lambda e: e.activation(
                self.stat[:, 22:23], self.stat[:, 21:22], AF.Exp, scale=-0.5),
                waits=[(self.s_sc, self.s_sc.n)], inc=self.s_sc)
            nv = sc.op("vector", lambda e: e.tensor_scalar(
                self.xnorm[:], self.x2t[:], self.stat[:, 18:19], self.stat[:, 22:23],
                ALU.subtract, ALU.mult),
                waits=[(self.s_sc, rs), (self.s_vc, self.s_vc.n)], inc=self.s_vc)
            st["x2_nv"] = nv
            prev_t2 = 0
            for hc in range(c.HC):
                w = [(self.s_vc, nv), (self.s_te, self.s_te.n),
                     (self.s_l2, st["ltl_free"]), (self.s_tr, st["xn2f_rd"])]
                tvv = sc.op("tensor", lambda e, hc=hc: e.transpose(
                    self.LTL[:, 0:P], self.xnorm[:, hc * P:(hc + 1) * P], self.identf[:]),
                    waits=w, inc=self.s_tr, defer=stage4_q)
                t1 = sc.op("scalar", lambda e, hc=hc: e.copy(xnfv[:, hc, :], self.LTL[:, 0:P]),
                      waits=[(self.s_tr, tvv), (self.s_sc, self.s_sc.n)], inc=self.s_te)
                if not c.CAP:
                    sc.op("vector", lambda e, hc=hc: e.tensor_copy(xnbv[:, hc, :], xnfv[:, hc, :]),
                          waits=[(self.s_te, t1), (self.s_xbst, st["xn2b_st"]),
                                 (self.s_vc, self.s_vc.n)],
                          inc=self.s_te)
            te_all = self.s_te.n
            if not c.CAP:
                st["xn2b_st"] = sc.op("sync", lambda e, qt=qt: e.dma_start(
                    self.xn2_d[:, qt * P:(qt + 1) * P].rearrange("(cc p) w -> p cc w", p=P),
                    xnbv[:]),
                    waits=[(self.s_te, te_all)], inc=self.s_xbst, inc_by=16)
            if c.CAP:
                tmc = sc.op("vector", lambda e: e.tensor_copy(self.xtmb[:], self.xnorm[:]),
                            waits=[(self.s_vc, nv), (self.s_tm, st.get("xtm_st", 0)),
                                   (self.s_vc, self.s_vc.n)],
                            inc=self.s_vc)
                st["xtm_st"] = sc.op("sync", lambda e, qt=qt: e.dma_start(
                    self.xntm_d[qt * P:(qt + 1) * P, :], self.xtmb[:]),
                    waits=[(self.s_vc, tmc)], inc=self.s_tm, inc_by=16)
            rv = 0
            for hc in range(c.HC):
                w = ([(self.s_te, te_all), (self.s_ev, self.rwT_done)] if hc == 0 else [])
                rv = sc.op("tensor", lambda e, hc=hc: e.matmul(
                    self.LTL[:, P:P + c.E], xnfv[:, hc, :],
                    self.rwT[:, hc * c.E:(hc + 1) * c.E],
                    start=(hc == 0), stop=False),
                    waits=w, inc=self.s_tr, defer=stage4_q)
            rv = sc.op("tensor", lambda e: e.matmul(
                self.LTL[:, P:P + c.E], self.ones_rf[:], self.eb_f[:],
                start=False, stop=True),
                waits=[(self.s_cst, self.w_const)], inc=self.s_tr, defer=stage4_q)
            st["xn2f_rd"] = rv
            lg = sc.op("vector", lambda e: e.tensor_copy(self.rc[:, 0:c.E], self.LTL[:, P:P + c.E]),
                       waits=[(self.s_tr, rv), (self.s_vc, self.s_vc.n)], inc=self.s_l2)
            st["ltl_free"] = lg
            m8 = sc.op("vector", lambda e: e.max(out=self.rc[:, 16:24], in_=self.rc[:, 0:c.E]),
                       waits=[(self.s_l2, lg), (self.s_vc, self.s_vc.n)], inc=self.s_vc)
            ng = sc.op("vector", lambda e: e.tensor_scalar(
                self.rc[:, 24:25], self.rc[:, 16:17], -1.0, None, ALU.mult),
                waits=[(self.s_vc, m8)], inc=self.s_vc)
            ex = sc.op("scalar", lambda e: e.activation(
                self.rc[:, 0:c.E],
                self.rc[:, 0:c.E], AF.Exp, bias=self.rc[:, 24:25]),
                waits=[(self.s_vc, ng), (self.s_sc, self.s_sc.n)], inc=self.s_sc)
            e4 = sc.op("scalar", lambda e: e.activation(
                self.rc[:, 25:29], self.rc[:, 16:20], AF.Exp, bias=self.rc[:, 24:25]),
                waits=[(self.s_sc, ex)], inc=self.s_sc)
            sm = sc.op("vector", lambda e: e.reduce_sum(self.rc[:, 29:30], self.rc[:, 25:29], axis=AX.X),
                       waits=[(self.s_sc, e4), (self.s_vc, self.s_vc.n)], inc=self.s_vc)
            rcp = sc.op("vector", lambda e: e.reciprocal(self.rc[:, 30:31], self.rc[:, 29:30]),
                        waits=[(self.s_vc, sm)], inc=self.s_vc)
            kp = sc.op("vector", lambda e: e.tensor_scalar(
                self.rc[:, 32:32 + c.E], self.rc[:, 0:c.E], self.rc[:, 28:29], None, ALU.is_ge),
                waits=[(self.s_vc, rcp)], inc=self.s_vc)
            if c.CAP:
                tl = sc.op("vector", lambda e: e.tensor_scalar(
                    self.rc[:, 31:32], self.rc[:, 28:29], 0.99601, None, ALU.mult),
                    waits=[(self.s_vc, kp)], inc=self.s_vc)
                kl = sc.op("vector", lambda e: e.tensor_scalar(
                    self.rc[:, 48:48 + c.E], self.rc[:, 0:c.E], self.rc[:, 31:32],
                    None, ALU.is_ge),
                    waits=[(self.s_vc, tl)], inc=self.s_vc)
                wv_ = sc.op("vector", lambda e: e.tensor_scalar(
                    self.rc[:, 0:c.E], self.rc[:, 0:c.E], self.rc[:, 30:31], None, ALU.mult),
                    waits=[(self.s_vc, kl)], inc=self.s_vc)
                wv_ = sc.op("vector", lambda e: e.tensor_mul(
                    self.rc[:, 0:c.E], self.rc[:, 0:c.E], self.rc[:, 32:32 + c.E]),
                    waits=[(self.s_vc, wv_)], inc=self.s_vc)
                sc.op("vector", lambda e, tt=tt: e.scalar_tensor_tensor(
                    self.w_all[:, tt * c.E:(tt + 1) * c.E], self.rc[:, 48:48 + c.E],
                    1e-30, self.rc[:, 0:c.E], ALU.mult, ALU.add),
                    waits=[(self.s_vc, wv_)], inc=self.s_l2)
            else:
                wv_ = sc.op("vector", lambda e: e.tensor_scalar(
                    self.rc[:, 0:c.E], self.rc[:, 0:c.E], self.rc[:, 30:31], None, ALU.mult),
                    waits=[(self.s_vc, kp)], inc=self.s_vc)
                sc.op("vector", lambda e, tt=tt: e.tensor_mul(
                    self.w_all[:, tt * c.E:(tt + 1) * c.E], self.rc[:, 0:c.E],
                    self.rc[:, 32:32 + c.E]),
                    waits=[(self.s_vc, wv_)], inc=self.s_l2)

        self.join_all([(self.s_mm, self.qkv_mm_done), (self.s_ev, self.qkv_ev_done),
                       (self.s_xn1, self.ln1_done),
                       (self.s_q0, self.qst_war[0]), (self.s_q1, self.qst_war[1]),
                       (self.s_w0, self.s_w0.n), (self.s_w1, self.s_w1.n),
                       (self.s_vc, self.s_vc.n), (self.s_sc, self.s_sc.n)])
        for qt in range(c.TT if self.qt_limit is None else self.qt_limit):
            sc.splice("tensor", stage4_q)
            qslot = qt % 2
            qwaits = ([(self.s_p2, st["q_war"][qslot])] if qt >= 2 else
                      [(self.s_q0, self.qst_war[0]), (self.s_q1, self.qst_war[1]),
                       (self.s_mm, self.qkv_mm_done), (self.s_ev, self.qkv_ev_done)])
            sqs = [self.s_qs0, self.s_qs1][qslot]
            st["q_ld"][qslot] = sc.op("sync", lambda e, qt=qt, qslot=qslot: e.dma_start(
                qv[:, qslot, :, :],
                self.qt_d[:, qt * P:(qt + 1) * P].rearrange("(cc p) w -> p cc w", p=P)),
                waits=qwaits, inc=sqs, inc_by=16)
            st["q_sem"] = sqs
            nu = (c.NH if self.head_limit is None else self.head_limit) // 2
            hctx = [dict() for _ in range(nu)]
            for step in range(nu + 3):
                if step >= 3:
                    stage_c(qt, step - 3, hctx[step - 3])
                if 2 <= step < nu + 2:
                    stage_b(qt, step - 2, hctx[step - 2])
                if step < nu:
                    stage_a1(qt, step, hctx[step])
            self.ce_all[qt] = self.s_ce.n
            st["q_war"][qslot] = self.s_p2.n
            if not self.no_stage4:
                stage4(qt)
        sc.splice("tensor", stage4_q)
        self.w_done = self.s_l2.n
        self.attn_xb = self.s_xbst.n
        self.attn_x2 = self.s_x2st.n

    def phase_moe(self):
        sc, c = self.sc, self.c
        xnv = self.xn2h[:].rearrange("p (cc t) -> p cc t", cc=c.HC)
        accv = self.acc[:].rearrange("p (t d) -> p t d", t=c.TTH)
        actv = self.act[:].rearrange("p (ic t) -> p ic t", ic=c.IC)
        gubv = self.gub[:].rearrange("p (s cc w) -> p s cc w", s=4, cc=c.HC)
        dwv = self.dwn[:].rearrange("p (ic d) -> p ic d", ic=c.IC)
        gsv = self.gsb[:].rearrange("p (s w) -> p s w", s=2)

        gu_war = [0, 0]
        pg_war = [0, 0]
        pu_war = [0, 0]
        pd_war = [0, 0, 0]
        gs_war = [0, 0]
        act_rd = 0
        dwn_rd = 0
        y_total = 0
        xl_prev = 0

        self.join_all([(self.s_l2, self.w_done), (self.s_sc, self.s_sc.n),
                       (self.s_tr, self.s_tr.n), (self.s_te, self.s_te.n),
                       (self.s_vc, self.s_vc.n), (self.s_ce, self.s_ce.n),
                       (self.s_x2, self.s_x2.n), (self.s_x2st, self.attn_x2),
                       (self.s_xbst, self.attn_xb),
                       (self.s_qs0, self.s_qs0.n), (self.s_qs1, self.s_qs1.n),
                       (self.s_x0, self.s_x0.n), (self.s_x1, self.s_x1.n),
                       (self.s_wol, self.wo_ld)]
                      + ([(self.s_tm, self.s_tm.n)] if self.c.CAP else []))
        for half in range(c.N_HALVES):
            t_off = half * c.TH
            xl = sc.op("sync", lambda e, t_off=t_off: e.dma_start(
                xnv[:],
                self.xn2_d[:, t_off:t_off + c.TH].rearrange("(cc p) w -> p cc w", p=P)),
                waits=[(self.s_gu, xl_prev), (self.s_l2, self.w_done),
                       (self.s_xbst, self.attn_xb)],
                inc=self.s_xl, inc_by=16)
            for tt in range(c.TTH):
                sc.op("sync", lambda e, tt=tt, t_off=t_off: e.dma_start(
                    accv[:, tt, :], self.x2_d[t_off + tt * P: t_off + (tt + 1) * P, :]),
                    waits=[(self.s_yst, y_total), (self.s_x2st, self.attn_x2)],
                    inc=self.s_yi, inc_by=16)
            yi_all = self.s_yi.n
            for ei in range(c.E):
                ib_list = chunks(c.DI, c.GIB)
                gl_marks = {}

                def gu_load(bi, wait_mark):
                    i0, iw = ib_list[bi]
                    slot = bi % 2
                    sg = [self.s_g0, self.s_g1][slot]
                    sc.op("gpsimd", lambda e, ei=ei, i0=i0, iw=iw, slot=slot: e.dma_start(
                        gubv[:, 2 * slot, :, 0:iw],
                        self.gate_d[ei][:, i0:i0 + iw].rearrange("(cc p) w -> p cc w", p=P)),
                        waits=[(self.s_gu, wait_mark)], inc=sg, inc_by=16)
                    gl = sc.op("gpsimd", lambda e, ei=ei, i0=i0, iw=iw, slot=slot: e.dma_start(
                        gubv[:, 2 * slot + 1, :, 0:iw],
                        self.up_d[ei][:, i0:i0 + iw].rearrange("(cc p) w -> p cc w", p=P)),
                        waits=[], inc=sg, inc_by=16)
                    gl_marks[bi] = (sg, gl)

                for bi in range(min(2, len(ib_list))):
                    gu_load(bi, gu_war[bi % 2])
                dl_next = [0]
                dl_last = [0]

                def down_load_some(k):
                    while dl_next[0] < c.IC and k > 0:
                        ic = dl_next[0]
                        dl_next[0] += 1
                        k -= 1
                        dl_last[0] = sc.op("gpsimd", lambda e, ei=ei, ic=ic: e.dma_start(
                            dwv[:, ic, :], self.down_d[ei][ic * P:(ic + 1) * P, :]),
                            waits=[(self.s_dn, dwn_rd)], inc=self.s_dw, inc_by=16)
                pi = 0
                for bi, (i0, iw) in enumerate(ib_list):
                    slot = bi % 2
                    for (ic0, icw) in chunks(iw, P):
                        ic = (i0 + ic0) // P
                        for (t0, tw) in chunks(c.TH, 512):
                            bank = pi % 2
                            gslot = pi % 2
                            pi += 1
                            gm = um = 0
                            for hc in range(c.HC):
                                gm = sc.op("tensor", lambda e, slot=slot, hc=hc, ic0=ic0, t0=t0, tw=tw, bank=bank: e.matmul(
                                    self.pg[bank][:, 0:tw],
                                    gubv[:, 2 * slot, hc, ic0:ic0 + P],
                                    xnv[:, hc, t0:t0 + tw],
                                    start=(hc == 0), stop=(hc == c.HC - 1)),
                                    waits=([gl_marks[bi], (self.s_xl, xl),
                                            (self.s_si, pg_war[bank])] if hc == 0 else []),
                                    inc=self.s_gu)
                            for hc in range(c.HC):
                                um = sc.op("tensor", lambda e, slot=slot, hc=hc, ic0=ic0, t0=t0, tw=tw, bank=bank: e.matmul(
                                    self.pu[bank][:, 0:tw],
                                    gubv[:, 2 * slot + 1, hc, ic0:ic0 + P],
                                    xnv[:, hc, t0:t0 + tw],
                                    start=(hc == 0), stop=(hc == c.HC - 1)),
                                    waits=([(self.s_ae, pu_war[bank])] if hc == 0 else []),
                                    inc=self.s_gu)
                            sv = sc.op("scalar", lambda e, bank=bank, tw=tw, gslot=gslot: e.activation(
                                gsv[:, gslot, 0:tw], self.pg[bank][:, 0:tw], AF.Silu),
                                waits=[(self.s_gu, gm), (self.s_ae, gs_war[gslot]),
                                       (self.s_sc, self.s_sc.n)],
                                inc=self.s_si)
                            pg_war[bank] = sv
                            av = sc.op("vector", lambda e, bank=bank, tw=tw, gslot=gslot, ic=ic, t0=t0: e.tensor_mul(
                                actv[:, ic, t0:t0 + tw], gsv[:, gslot, 0:tw],
                                self.pu[bank][:, 0:tw]),
                                waits=[(self.s_gu, um), (self.s_si, sv),
                                       (self.s_dn, act_rd), (self.s_vc, self.s_vc.n)],
                                inc=self.s_ae)
                            pu_war[bank] = av
                            gs_war[gslot] = av
                    gu_war[slot] = self.s_gu.n
                    if bi + 2 < len(ib_list):
                        gu_load(bi + 2, gu_war[slot])
                    if bi >= 1:
                        down_load_some(2)
                down_load_some(c.IC)
                dl_all = dl_last[0]
                act_done = self.s_ae.n
                bank_i = 0
                for tt in range(c.TTH):
                    for (d0, dw) in chunks(c.D, 512):
                        bank = bank_i % 3
                        bank_i += 1
                        dm = 0
                        for ic in range(c.IC):
                            w = ([(self.s_ae, act_done), (self.s_de, pd_war[bank]),
                                  (self.s_dw, dl_all)] if ic == 0 else [])
                            dm = sc.op("tensor", lambda e, ic=ic, tt=tt, d0=d0, dw=dw, bank=bank: e.matmul(
                                self.pd[bank][:, 0:dw],
                                actv[:, ic, tt * P:(tt + 1) * P],
                                dwv[:, ic, d0:d0 + dw],
                                start=(ic == 0), stop=(ic == c.IC - 1)),
                                waits=w, inc=self.s_dn)
                        wt = half * c.TTH + tt
                        pd_war[bank] = sc.op("vector", lambda e, tt=tt, d0=d0, dw=dw, bank=bank, wt=wt, ei=ei: e.scalar_tensor_tensor(
                            accv[:, tt, d0:d0 + dw], self.pd[bank][:, 0:dw],
                            self.w_all[:, wt * c.E + ei: wt * c.E + ei + 1],
                            accv[:, tt, d0:d0 + dw], ALU.mult, ALU.add),
                            waits=[(self.s_dn, dm), (self.s_yi, yi_all),
                                   (self.s_l2, self.w_done), (self.s_vc, self.s_vc.n)],
                            inc=self.s_de)
                act_rd = self.s_dn.n
                dwn_rd = self.s_dn.n
            xl_prev = self.s_gu.n
            for tt in range(c.TTH):
                sc.op("sync", lambda e, tt=tt, t_off=t_off: e.dma_start(
                    self.y_d[t_off + tt * P: t_off + (tt + 1) * P, :], accv[:, tt, :]),
                    waits=[(self.s_de, self.s_de.n)], inc=self.s_yst, inc_by=16)
            y_total = self.s_yst.n


    def enter_moe_scope_sparse(self):
        nc, c = self.nc, self.c
        gub_w = 2 * 2 * c.HC * c.GIB
        gs = [nc.sbuf_tensor("xngT", [P, c.HC * c.CAP], BF16),
              nc.sbuf_tensor("acts", [P, c.IC * c.CAP], BF16),
              nc.sbuf_tensor("gub", [P, gub_w], BF16),
              nc.sbuf_tensor("dwn", [P, c.IC * c.D], BF16),
              nc.sbuf_tensor("gssb", [P, 2 * 512], BF16),
              nc.sbuf_tensor("xstg", [P, 2 * c.D], BF16),
              nc.sbuf_tensor("sst", [P, 2 * min(1024, c.D)], F32),
              nc.sbuf_tensor("kpall", [P, c.TT * c.E], F32),
              nc.sbuf_tensor("rksc", [P, 2 * c.E], F32),
              nc.sbuf_tensor("peib", [P, c.TT * c.E], mybir.dt.int32),
              nc.sbuf_tensor("wrtb", [P, 2 * c.E], F32),
              nc.sbuf_tensor("idxc", [P, 2 * c.CT], mybir.dt.int32),
              nc.sbuf_tensor("wrg", [P, 2 * c.CT * c.E], F32),
              nc.sbuf_tensor("m01f_sb", [P, P], F32),
              nc.sbuf_tensor("s01f_sb", [P, P], F32),
              nc.sbuf_tensor("iota_sb", [P, c.TT], mybir.dt.int32),
              nc.sbuf_tensor("ecr_sb", [1, c.E], F32),
              nc.sbuf_tensor("thr_sb", [P, c.E], F32),
              nc.sbuf_tensor("trow", [1, c.TT * c.E], F32),
              nc.sbuf_tensor("orow", [1, c.TT * c.E], F32),
              nc.sbuf_tensor("tot16", [c.TT, c.E], F32),
              nc.psum_tensor("pg0", [P, 512], F32),
              nc.psum_tensor("pg1", [P, 512], F32),
              nc.psum_tensor("pu0", [P, 512], F32),
              nc.psum_tensor("pu1", [P, 512], F32),
              nc.psum_tensor("ptp", [P, 512], BF16),
              nc.psum_tensor("ptq", [P, 512], BF16)]
        (self.xngT, self.acts, self.gub, self.dwn, self.gsb, self.xstg,
         self.sst, self.kpall, self.rksc, self.peib, self.wrtb, self.idxc,
         self.wrg, self.m01f_sb, self.s01f_sb,
         self.iota_sb, self.ecr_sb, self.thr_sb, self.trow, self.orow, self.tot16,
         pg0, pg1, pu0, pu1, ptp, ptq) = [g.__enter__() for g in gs]
        self.pg = [pg0, pg1]
        self.pu = [pu0, pu1]
        self.pd = self.pg
        self.ptp = [ptp, ptq]
        self.tp_grp = 0
        self.tp_bank_war = [0, 0]
        return gs

    def phase_router_build(self):
        sc, c = self.sc, self.c
        BIG = 1.0e7
        kv = self.kpall[:].rearrange("p (t e) -> p t e", t=c.TT)

        def _mkregs(e):
            self.bnd_ix = e.alloc_register("bnd_ix")
            e.reg_mov(self.bnd_ix, c.E * c.CAP - 1)
            self.bnd_t = e.alloc_register("bnd_t")
            return e.reg_mov(self.bnd_t, c.T - 1)
        sc.op("gpsimd", _mkregs)
        sc.op("sync", lambda e: e.dma_start(self.m01f_sb[:], self.m01f_d.ap()), inc=self.s_rb, inc_by=16)
        sc.op("sync", lambda e: e.dma_start(self.s01f_sb[:], self.s01f_d.ap()), inc=self.s_rb, inc_by=16)
        sc.op("sync", lambda e: e.dma_start(self.iota_sb[:], self.iota_d.ap()), inc=self.s_rb, inc_by=16)
        sc.op("sync", lambda e: e.dma_start(self.ecr_sb[:], self.ecr_d.ap()), inc=self.s_rb, inc_by=16)
        sc.op("sync", lambda e: e.dma_start(self.thr_sb[:], self.thr_d.ap()), inc=self.s_rb, inc_by=16)
        sc.op("sync", lambda e: e.dma_start(
            self.idx_d.tensor[:].rearrange("(p w) one -> p (w one)", p=P),
            self.sent_d.ap()), inc=self.s_rb, inc_by=16)
        self.sent_mark = cst = self.s_rb.n
        kmarks = []
        for tt in range(c.TT):
            kmarks.append(sc.op("vector", lambda e, tt=tt: e.tensor_scalar(
                kv[:, tt, :], self.w_all[:, tt * c.E:(tt + 1) * c.E], 0.0, None, ALU.is_gt),
                waits=[(self.s_l2, self.w_done), (self.s_vc, self.s_vc.n)], inc=self.s_kp))
        tmm = 0
        for tt in range(c.TT):
            tmm = sc.op("tensor", lambda e, tt=tt: e.matmul(
                self.pw[0:1, tt * c.E:(tt + 1) * c.E], self.ones_cf[:], kv[:, tt, :],
                start=(tt == 0), stop=(tt == c.TT - 1)),
                waits=[(self.s_kp, kmarks[tt])], inc=self.s_ps)
        tev = sc.op("scalar", lambda e: e.copy(self.trow[:], self.pw[0:1, 0:c.TT * c.E]),
                    waits=[(self.s_ps, tmm), (self.s_sc, self.s_sc.n)], inc=self.s_ps)
        t1 = sc.op("sync", lambda e: e.dma_start(self.tot_d, self.trow[:]),
                   waits=[(self.s_ps, tev)], inc=self.s_bc, inc_by=16)
        t2 = sc.op("sync", lambda e: e.dma_start(self.tot16[:], self.tot_d),
                   waits=[(self.s_bc, t1)], inc=self.s_bc, inc_by=16)
        omm = sc.op("tensor", lambda e: e.matmul(
            self.pw[0:c.TT, 256:256 + c.E], self.s01f_sb[0:c.TT, 0:c.TT], self.tot16[:],
            start=True, stop=False),
            waits=[(self.s_bc, t2), (self.s_rb, cst), (self.s_ps, self.s_ps.n)],
            inc=self.s_ps)
        omm = sc.op("tensor", lambda e: e.matmul(
            self.pw[0:c.TT, 256:256 + c.E], self.ones_rf[0:1, 0:c.TT], self.ecr_sb[:],
            start=False, stop=True),
            waits=[(self.s_rb, cst)], inc=self.s_ps)
        oev = sc.op("scalar", lambda e: e.copy(self.tot16[:], self.pw[0:c.TT, 256:256 + c.E]),
                    waits=[(self.s_ps, omm), (self.s_sc, self.s_sc.n)], inc=self.s_ps)
        o1 = sc.op("sync", lambda e: e.dma_start(self.off_d, self.tot16[:]),
                   waits=[(self.s_ps, oev)], inc=self.s_bc, inc_by=16)
        o2 = sc.op("sync", lambda e: e.dma_start(self.orow[:], self.off_d),
                   waits=[(self.s_bc, o1)], inc=self.s_bc, inc_by=16)
        wrt_war = [0, 0]
        pd_rw = [0, 0]
        pcast_marks = []
        for tt in range(c.TT):
            bank = self.pd[tt % 2]
            cmm = sc.op("tensor", lambda e, tt=tt, bank=bank: e.matmul(
                bank[:, 0:c.E], self.m01f_sb[:], kv[:, tt, :],
                start=True, stop=False),
                waits=[(self.s_kp, kmarks[tt]), (self.s_bc, o2),
                       (self.s_kp, pd_rw[tt % 2])],
                inc=self.s_ps)
            cmm = sc.op("tensor", lambda e, tt=tt, bank=bank: e.matmul(
                bank[:, 0:c.E], self.ones_rf[:], self.orow[0:1, tt * c.E:(tt + 1) * c.E],
                start=False, stop=True),
                waits=[], inc=self.s_ps)
            px = sc.op("vector", lambda e, tt=tt, bank=bank: e.tensor_sub(
                self.rksc[:, 0:c.E], bank[:, 0:c.E], kv[:, tt, :]),
                waits=[(self.s_ps, cmm), (self.s_vc, self.s_vc.n),
                       (self.s_kp, self.s_kp.n)], inc=self.s_kp)
            pd_rw[tt % 2] = px
            ws = tt % 2
            wv2 = sc.op("vector", lambda e, tt=tt, ws=ws: e.tensor_copy(
                self.wrtb[:, ws * c.E:(ws + 1) * c.E],
                self.w_all[:, tt * c.E:(tt + 1) * c.E]),
                waits=[([self.s_wa, self.s_wb][ws], wrt_war[ws]),
                       (self.s_vc, self.s_vc.n)], inc=self.s_vc)
            wrt_war[ws] = sc.op("sync", lambda e, tt=tt, ws=ws: e.dma_start(
                self.wr_d[tt * P:(tt + 1) * P, :],
                self.wrtb[:, ws * c.E:(ws + 1) * c.E]),
                waits=[(self.s_vc, wv2)], inc=[self.s_wa, self.s_wb][ws], inc_by=16)
            p1 = sc.op("vector", lambda e, tt=tt: e.tensor_scalar(
                self.rksc[:, a:a + c.E] if False else self.rksc[:, 1 * c.E:2 * c.E],
                kv[:, tt, :], -BIG, BIG, ALU.mult, ALU.add),
                waits=[(self.s_vc, self.s_vc.n)], inc=self.s_vc)
            p2 = sc.op("vector", lambda e: e.tensor_add(
                self.rksc[:, 0:c.E], self.rksc[:, 0:c.E], self.rksc[:, 1 * c.E:2 * c.E]),
                waits=[(self.s_vc, p1)], inc=self.s_vc)
            ov1 = sc.op("vector", lambda e: e.tensor_tensor(
                self.rksc[:, 1 * c.E:2 * c.E], self.rksc[:, 0:c.E], self.thr_sb[:],
                ALU.is_ge),
                waits=[(self.s_vc, p2), (self.s_rb, self.sent_mark)], inc=self.s_vc)
            p3 = sc.op("vector", lambda e: e.scalar_tensor_tensor(
                self.rksc[:, 0:c.E], self.rksc[:, 1 * c.E:2 * c.E], BIG,
                self.rksc[:, 0:c.E], ALU.mult, ALU.add),
                waits=[(self.s_vc, ov1)], inc=self.s_vc)
            pcast = sc.op("vector", lambda e, tt=tt: e.tensor_copy(
                self.peib[:, tt * c.E:(tt + 1) * c.E], self.rksc[:, 0:c.E]),
                waits=[(self.s_vc, p3), (self.s_vc, self.s_vc.n)],
                inc=self.s_kp)
            pcast_marks.append(pcast)
        self.sx_marks = [None] * c.E
        sx_prev = [0, 0]

        def emit_sx(ei):
            sxs = [self.s_sx0, self.s_sx1][ei % 2]
            for tt in range(c.TT):
                sc.op("gpsimd", lambda e, tt=tt, ei=ei: e.indirect_dma_start(
                    self.idx_d,
                    bass.IndirectOffsetOnAxis(self.peib[:, tt * c.E + ei: tt * c.E + ei + 1], 0),
                    self.iota_sb[:, tt:tt + 1], None,
                    bounds_check=self.bnd_ix, oob_is_err=False),
                    waits=[(self.s_kp, pcast_marks[tt]), (self.s_rb, self.sent_mark)] +
                          ([(sxs, sx_prev[ei % 2])] if tt == 0 else []),
                    inc=sxs, inc_by=16)
            self.sx_marks[ei] = (sxs, sxs.n)
            sx_prev[ei % 2] = sxs.n

        for ei in range(min(2, c.E)):
            emit_sx(ei)
        self.emit_sx = emit_sx
        self.pending_sx = list(range(2, c.E))
        self.wr_done = (self.s_wa.n, self.s_wb.n)

    def phase_moe_sparse(self):
        sc, c = self.sc, self.c
        xnv = self.xngT[:].rearrange("p (cc t) -> p cc t", cc=c.HC)
        actv = self.acts[:].rearrange("p (ic t) -> p ic t", ic=c.IC)
        gubv = self.gub[:].rearrange("p (s cc w) -> p s cc w", s=4, cc=c.HC)
        dwv = self.dwn[:].rearrange("p (ic d) -> p ic d", ic=c.IC)
        gsv = self.gsb[:].rearrange("p (s w) -> p s w", s=2)
        xsv = self.xstg[:].rearrange("p (s w) -> p s w", s=2)
        ssv = self.sst[:].rearrange("p (s w) -> p s w", s=2)
        wgv = self.wrg[:].rearrange("p (s ct w) -> p s ct w", s=2, ct=c.CT)

        self.join_all([(self.s_l2, self.w_done), (self.s_sc, self.s_sc.n),
                       (self.s_tr, self.s_tr.n), (self.s_te, self.s_te.n),
                       (self.s_vc, self.s_vc.n), (self.s_ce, self.s_ce.n),
                       (self.s_x2, self.s_x2.n), (self.s_x2st, self.attn_x2),
                       (self.s_xbst, self.attn_xb), (self.s_tm, self.s_tm.n),
                       (self.s_qs0, self.s_qs0.n), (self.s_qs1, self.s_qs1.n),
                       (self.s_x0, self.s_x0.n), (self.s_x1, self.s_x1.n),
                       (self.s_wol, self.wo_ld)])
        self.phase_router_build()

        sc.op("gpsimd", lambda e: e.memset(self.wrg[:], 0.0), inc=self.s_ms)

        gu_war = [0, 0]
        pg_war = [0, 0]
        pu_war = [0, 0]
        pd_war = [0, 0]
        gs_war = [0, 0]
        xstg_war = [0, 0]
        sst_war = [0, 0]
        idx_war = [[], []]
        wrg_war = [0, 0]
        act_rd = 0
        dwn_rd = 0
        prep_done = {}
        self.y_marks = {}

        def prep(ei, agate):
            es = ei % 2
            il = 0
            for ct in range(c.CT):
                il = sc.op("gpsimd", lambda e, ei=ei, es=es, ct=ct: e.dma_start(
                    self.idxc[:, es * c.CT + ct: es * c.CT + ct + 1],
                    self.idx_d.tensor[ei * c.CAP + ct * P: ei * c.CAP + (ct + 1) * P, :]),
                    waits=([self.sx_marks[ei]] + idx_war[es] if ct == 0 else []),
                    inc=self.s_ix, inc_by=16)
            tps = []
            gx_last = 0
            for ct in range(c.CT):
                xs = (ei * c.CT + ct) % 2
                sgx = [self.s_gx0, self.s_gx1][xs]
                gx = sc.op("gpsimd", lambda e, ct=ct, es=es, xs=xs: e.indirect_dma_start(
                    xsv[:, xs, :], None, self.xntm_d,
                    bass.IndirectOffsetOnAxis(self.idxc[:, es * c.CT + ct: es * c.CT + ct + 1], 0),
                    bounds_check=self.bnd_t, oob_is_err=False),
                    waits=[(self.s_ix, il), (self.s_tp, xstg_war[xs])],
                    inc=sgx, inc_by=16)
                sgw = [self.s_gw0, self.s_gw1][es]
                wg = sc.op("gpsimd", lambda e, ct=ct, es=es: e.indirect_dma_start(
                    wgv[:, es, ct, :], None, self.wr_d,
                    bass.IndirectOffsetOnAxis(self.idxc[:, es * c.CT + ct: es * c.CT + ct + 1], 0),
                    bounds_check=self.bnd_t, oob_is_err=False),
                    waits=[(self.s_wa, self.wr_done[0]), (self.s_wb, self.wr_done[1]),
                           (self.s_de, wrg_war[es])],
                    inc=sgw, inc_by=16)
                gx_last = wg
                mmv = 0
                for hc in range(c.HC):
                    tb = self.tp_grp % 2
                    mmv = sc.op("tensor", lambda e, hc=hc, xs=xs, tb=tb: e.transpose(
                        self.ptp[tb][:, (hc % 4) * P:(hc % 4 + 1) * P],
                        xsv[:, xs, hc * P:(hc + 1) * P], self.ident[:]),
                        waits=(([(sgx, gx), agate] if hc == 0 else []) +
                               ([(self.s_tp, self.tp_bank_war[tb])] if hc % 4 == 0 else [])),
                        inc=self.s_ps)
                    if hc % 4 == 3 or hc == c.HC - 1:
                        nb = hc % 4 + 1
                        self.tp_bank_war[tb] = sc.op(
                            "scalar", lambda e, hc=hc, ct=ct, nb=nb, tb=tb: e.copy(
                                xnv[:, hc - nb + 1:hc + 1, ct * P:(ct + 1) * P],
                                self.ptp[tb][:, 0:nb * P].rearrange("p (f w) -> p f w", f=nb)),
                            waits=[(self.s_ps, mmv), (self.s_sc, self.s_sc.n)],
                            inc=self.s_tp)
                        self.tp_grp += 1
                xstg_war[xs] = self.s_tp.n
            prep_done[ei] = (self.s_tp.n, il, ([self.s_gw0, self.s_gw1][es], gx_last))

        prep(0, self.sx_marks[0])
        ib_list = chunks(c.DI, c.GIB)
        emit_sx_rest_pending = True
        gl_marks = {}

        def gu_load(lei, bi, wait_mark):
            i0, iw = ib_list[bi]
            slot = bi % 2
            sg = [self.s_g0, self.s_g1][slot]
            sc.op("gpsimd", lambda e, lei=lei, i0=i0, iw=iw, slot=slot: e.dma_start(
                gubv[:, 2 * slot, :, 0:iw],
                self.gate_d[lei][:, i0:i0 + iw].rearrange("(cc p) w -> p cc w", p=P)),
                waits=[(self.s_gu, wait_mark)], inc=sg, inc_by=16)
            gl = sc.op("gpsimd", lambda e, lei=lei, i0=i0, iw=iw, slot=slot: e.dma_start(
                gubv[:, 2 * slot + 1, :, 0:iw],
                self.up_d[lei][:, i0:i0 + iw].rearrange("(cc p) w -> p cc w", p=P)),
                waits=[], inc=sg, inc_by=16)
            gl_marks[(lei, bi)] = (sg, gl)

        for bi in range(min(2, len(ib_list))):
            gu_load(0, bi, 0)

        def drain_sx(n):
            while self.pending_sx and n > 0:
                self.emit_sx(self.pending_sx.pop(0))
                n -= 1
        prev_ss = (0, 0)
        for ei in range(c.E):
            es = ei % 2
            tp_mark, il_mark, gw_mark = prep_done[ei]
            dl_next = [0]
            dl_last = [0]

            def down_load_some(k):
                while dl_next[0] < c.IC and k > 0:
                    ic = dl_next[0]
                    dl_next[0] += 1
                    k -= 1
                    dl_last[0] = sc.op("gpsimd", lambda e, ei=ei, ic=ic: e.dma_start(
                        dwv[:, ic, :], self.down_d[ei][ic * P:(ic + 1) * P, :]),
                        waits=[(self.s_dn, dwn_rd)], inc=self.s_dw, inc_by=16)

            pi = 0
            for bi, (i0, iw) in enumerate(ib_list):
                slot = bi % 2
                for (ic0, icw) in chunks(iw, P):
                    ic = (i0 + ic0) // P
                    for (t0, tw) in chunks(c.CAP, 512):
                        bank = pi % 2
                        gslot = pi % 2
                        pi += 1
                        gm = um = 0
                        for hc in range(c.HC):
                            gm = sc.op("tensor", lambda e, slot=slot, hc=hc, ic0=ic0, t0=t0, tw=tw, bank=bank: e.matmul(
                                self.pg[bank][:, 0:tw],
                                gubv[:, 2 * slot, hc, ic0:ic0 + P],
                                xnv[:, hc, t0:t0 + tw],
                                start=(hc == 0), stop=(hc == c.HC - 1)),
                                waits=([gl_marks[(ei, bi)], (self.s_tp, tp_mark),
                                        (self.s_si, pg_war[bank]),
                                        (self.s_de, pd_war[bank])] if hc == 0 else []),
                                inc=self.s_gu)
                        for hc in range(c.HC):
                            um = sc.op("tensor", lambda e, slot=slot, hc=hc, ic0=ic0, t0=t0, tw=tw, bank=bank: e.matmul(
                                self.pu[bank][:, 0:tw],
                                gubv[:, 2 * slot + 1, hc, ic0:ic0 + P],
                                xnv[:, hc, t0:t0 + tw],
                                start=(hc == 0), stop=(hc == c.HC - 1)),
                                waits=([(self.s_ae, pu_war[bank])] if hc == 0 else []),
                                inc=self.s_gu)
                        sv = sc.op("scalar", lambda e, bank=bank, tw=tw, gslot=gslot: e.activation(
                            gsv[:, gslot, 0:tw], self.pg[bank][:, 0:tw], AF.Silu),
                            waits=[(self.s_gu, gm), (self.s_ae, gs_war[gslot]),
                                   (self.s_sc, self.s_sc.n)],
                            inc=self.s_si)
                        pg_war[bank] = sv
                        av = sc.op("vector", lambda e, bank=bank, tw=tw, gslot=gslot, ic=ic, t0=t0: e.tensor_mul(
                            actv[:, ic, t0:t0 + tw], gsv[:, gslot, 0:tw],
                            self.pu[bank][:, 0:tw]),
                            waits=[(self.s_gu, um), (self.s_si, sv),
                                   (self.s_dn, act_rd), (self.s_vc, self.s_vc.n)],
                            inc=self.s_ae)
                        pu_war[bank] = av
                        gs_war[gslot] = av
                gu_war[slot] = self.s_gu.n
                nbi = bi + 2
                if nbi < len(ib_list):
                    gu_load(ei, nbi, gu_war[slot])
                elif ei + 1 < c.E:
                    gu_load(ei + 1, nbi - len(ib_list), gu_war[slot])
                if bi >= 1:
                    down_load_some(2)
                drain_sx(1)
            down_load_some(c.IC)
            dl_all = dl_last[0]
            act_done = self.s_ae.n
            a_end = self.s_gu.n
            if ei + 1 < c.E:
                prep(ei + 1, (self.s_gu, a_end))
            bank_i = 0
            DHW = min(1024, c.D)
            for stt_i in range(c.CT):
                for dh, (d0, dhw) in enumerate(chunks(c.D, DHW)):
                    dmm = 0
                    for (dc0, dcw) in chunks(dhw, 512):
                        bank = bank_i % 2
                        bank_i += 1
                        dm = 0
                        for ic in range(c.IC):
                            w = ([(self.s_ae, act_done), (self.s_de, pd_war[bank]),
                                  (self.s_dw, dl_all)] if ic == 0 else [])
                            dm = sc.op("tensor", lambda e, ic=ic, stt_i=stt_i, d0=d0, dc0=dc0, dcw=dcw, bank=bank: e.matmul(
                                self.pd[bank][:, 0:dcw],
                                actv[:, ic, stt_i * P:(stt_i + 1) * P],
                                dwv[:, ic, d0 + dc0:d0 + dc0 + dcw],
                                start=(ic == 0), stop=(ic == c.IC - 1)),
                                waits=w, inc=self.s_dn)
                        ss = stt_i % 2
                        pd_war[bank] = sc.op("vector", lambda e, ss=ss, dc0=dc0, dcw=dcw, bank=bank, es=es, stt_i=stt_i, ei=ei: e.tensor_scalar(
                            ssv[:, ss, dc0:dc0 + dcw], self.pd[bank][:, 0:dcw],
                            wgv[:, es, stt_i, ei:ei + 1], None, ALU.mult),
                            waits=[(self.s_dn, dm), ([self.s_ss0, self.s_ss1][ss], sst_war[ss]),
                                   gw_mark, (self.s_vc, self.s_vc.n)],
                            inc=self.s_de)
                        dmm = self.s_de.n
                    ss = stt_i % 2
                    first_sc = stt_i == 0 and dh == 0
                    sst_war[ss] = sc.op("gpsimd", lambda e, ss=ss, d0=d0, dhw=dhw, es=es, stt_i=stt_i: e.indirect_dma_start(
                        self.y_d,
                        bass.IndirectOffsetOnAxis(self.idxc[:, es * c.CT + stt_i: es * c.CT + stt_i + 1], 0),
                        ssv[:, ss, 0:dhw], None,
                        element_offset=d0,
                        bounds_check=self.bnd_t, oob_is_err=False,
                        compute_op=ALU.add),
                        waits=[(self.s_de, dmm), (self.s_ix, il_mark)] +
                              ([(self.s_ss0, prev_ss[0]), (self.s_ss1, prev_ss[1])]
                               if first_sc else []),
                        inc=[self.s_ss0, self.s_ss1][ss], inc_by=16)
            drain_sx(c.E)
            act_rd = self.s_dn.n
            dwn_rd = self.s_dn.n
            wrg_war[es] = self.s_de.n
            prev_ss = (self.s_ss0.n, self.s_ss1.n)
            idx_war[es] = [(self.s_ss0, prev_ss[0]), (self.s_ss1, prev_ss[1])]
        self.moe_sst = (self.s_ss0.n, self.s_ss1.n)
        self.moe_gu = self.s_gu.n


def build(cfg: Cfg, stop_after="moe", qt_limit=None, head_limit=None, no_stage4=False):
    return Builder(cfg, stop_after, qt_limit, head_limit, no_stage4).build()


_CFG = Cfg(T=2048, D=2048, NH=16, NG=4, E=16, DI=1792, N_HALVES=2, WQB=256,
           GIB=128, CAP=1408)
_NCORES = 8


def kernel(L0_attn_x, L0_attn_Wq, L0_attn_Wk, L0_attn_Wv, L0_attn_Wo, L0_attn_bo,
           L0_ffn_router_w, L0_ffn_expert_bias, L0_ffn_gate_proj, L0_ffn_up_proj,
           L0_ffn_down_proj):
    from concourse.bass_utils import run_bass_kernel_spmd
    c = _CFG
    nc = build(c)
    x = np.ascontiguousarray(np.asarray(L0_attn_x, dtype=np.float32))
    shared = {
        "wq": np.ascontiguousarray(np.asarray(L0_attn_Wq, dtype=np.float32)),
        "wk": np.ascontiguousarray(np.asarray(L0_attn_Wk, dtype=np.float32)),
        "wv": np.ascontiguousarray(np.asarray(L0_attn_Wv, dtype=np.float32)),
        "wo": np.ascontiguousarray(np.asarray(L0_attn_Wo, dtype=np.float32)),
        "bo": np.ascontiguousarray(np.asarray(L0_attn_bo, dtype=np.float32).reshape(1, c.D)),
        "rw": np.ascontiguousarray(np.asarray(L0_ffn_router_w, dtype=np.float32)),
        "eb": np.ascontiguousarray(np.asarray(L0_ffn_expert_bias, dtype=np.float32).reshape(1, c.E)),
        "gate": np.ascontiguousarray(np.asarray(L0_ffn_gate_proj, dtype=np.float32)),
        "up": np.ascontiguousarray(np.asarray(L0_ffn_up_proj, dtype=np.float32)),
        "down": np.ascontiguousarray(np.asarray(L0_ffn_down_proj, dtype=np.float32)),
    }
    in_maps = [dict(shared, x=np.ascontiguousarray(x[i])) for i in range(_NCORES)]
    res = run_bass_kernel_spmd(nc, in_maps, core_ids=list(range(_NCORES)))
    return np.stack([r["y"] for r in res.results]).astype(np.float32)


# version 12
# speedup vs baseline: 1.0977x; 1.0188x over previous
from dataclasses import dataclass
from contextlib import ExitStack
import numpy as np
import ml_dtypes

import concourse.bass as bass
import concourse.mybir as mybir

F32 = mybir.dt.float32
BF16 = mybir.dt.bfloat16
AF = mybir.ActivationFunctionType
ALU = mybir.AluOpType
AX = mybir.AxisListType

P = 128


@dataclass
class Cfg:
    T: int = 2048
    D: int = 2048
    NH: int = 16
    NG: int = 4
    E: int = 16
    DI: int = 1792
    N_HALVES: int = 2
    WQB: int = 256
    GIB: int = 128
    CAP: int = 0

    @property
    def TT(self): return self.T // P
    @property
    def HC(self): return self.D // P
    @property
    def GS(self): return self.NH // self.NG
    @property
    def DKV(self): return self.NG * P
    @property
    def IC(self): return self.DI // P
    @property
    def TH(self): return self.T // self.N_HALVES
    @property
    def TTH(self): return self.TH // P
    @property
    def CT(self): return self.CAP // P


def chunks(total, step):
    o, out = 0, []
    while o < total:
        out.append((o, min(step, total - o)))
        o += step
    return out


class Sem:
    __slots__ = ("name", "n", "handle")

    def __init__(self, name, handle):
        self.name, self.n, self.handle = name, 0, handle


class Sched:
    ENGINES = ("tensor", "vector", "scalar", "sync", "gpsimd")

    def __init__(self, nc, ctx):
        self.nc, self.ctx = nc, ctx
        self.ops = {e: [] for e in self.ENGINES}

    def sem(self, name):
        return Sem(name, self.ctx.enter_context(self.nc.semaphore(name)))

    def op(self, eng, fn, waits=(), inc=None, inc_by=1, defer=None):
        if inc is not None:
            inc.n += inc_by
        rec = (tuple((s, v) for (s, v) in waits if v > 0), fn, inc, inc_by)
        (self.ops[eng] if defer is None else defer).append(rec)
        return inc.n if inc else 0

    def splice(self, eng, deferred, k=10 ** 9):
        for _ in range(min(k, len(deferred))):
            self.ops[eng].append(deferred.pop(0))

    def playback(self, block):
        def make(name):
            def run(eng):
                last = {}
                for waits, fn, inc, inc_by in self.ops[name]:
                    for (s, v) in waits:
                        if last.get(id(s), -1) < v:
                            eng.wait_ge(s.handle, v)
                            last[id(s)] = v
                    ins = fn(eng)
                    if inc is not None:
                        ins.then_inc(inc.handle, inc_by)
            return run
        block.tensor(make("tensor"))
        block.vector(make("vector"))
        block.scalar(make("scalar"))
        block.sync(make("sync"))
        block.gpsimd(make("gpsimd"))


class Builder:
    def __init__(self, cfg: Cfg, stop_after="moe", qt_limit=None, head_limit=None, no_stage4=False):
        self.c = cfg
        self.stop_after = stop_after
        self.qt_limit = qt_limit
        self.head_limit = head_limit
        self.no_stage4 = no_stage4
        self.nc = bass.Bass("TRN2", target_bir_lowering=False)

    def build(self):
        c, nc = self.c, self.nc
        self.x_d = nc.dram_tensor("x", [c.T, c.D], F32, kind="ExternalInput").ap()
        self.wq_d = nc.dram_tensor("wq", [c.D, c.D], F32, kind="ExternalInput").ap()
        self.wk_d = nc.dram_tensor("wk", [c.D, c.DKV], F32, kind="ExternalInput").ap()
        self.wv_d = nc.dram_tensor("wv", [c.D, c.DKV], F32, kind="ExternalInput").ap()
        self.wo_d = nc.dram_tensor("wo", [c.D, c.D], F32, kind="ExternalInput").ap()
        self.bo_d = nc.dram_tensor("bo", [1, c.D], F32, kind="ExternalInput").ap()
        self.rw_d = nc.dram_tensor("rw", [c.E, c.D], F32, kind="ExternalInput").ap()
        self.eb_d = nc.dram_tensor("eb", [1, c.E], F32, kind="ExternalInput").ap()
        self.gate_d = nc.dram_tensor("gate", [c.E, c.D, c.DI], F32, kind="ExternalInput").ap()
        self.up_d = nc.dram_tensor("up", [c.E, c.D, c.DI], F32, kind="ExternalInput").ap()
        self.down_d = nc.dram_tensor("down", [c.E, c.DI, c.D], F32, kind="ExternalInput").ap()
        self.y_d = nc.dram_tensor("y", [c.T, c.D], F32, kind="ExternalOutput").ap()
        self.x2_d = nc.dram_tensor("x2s", [c.T, c.D], F32, kind="Internal").ap()
        self.qt_d = nc.dram_tensor("qts", [c.D, c.T], BF16, kind="Internal").ap()
        self.xn2_d = nc.dram_tensor("xn2s", [c.D, c.T], BF16, kind="Internal").ap()
        if c.CAP:
            self.xntm_d = nc.dram_tensor("xntm", [c.T, c.D], BF16, kind="Internal").ap()
            self.wr_d = nc.dram_tensor("wrd", [c.T, c.E], F32, kind="Internal").ap()
            self.idx_d = nc.dram_tensor("idxd", [c.E * c.CAP, 1], mybir.dt.int32, kind="Internal").ap()
            self.tot_d = nc.dram_tensor("totd", [c.TT, c.E], F32, kind="Internal").ap()
            self.off_d = nc.dram_tensor("offd", [1, c.TT * c.E], F32, kind="Internal").ap()

        self.ident_d = nc.inline_tensor(np.eye(P, dtype=ml_dtypes.bfloat16), "identc")
        self.identf_d = nc.inline_tensor(np.eye(P, dtype=np.float32), "identfc")
        m01 = np.triu(np.ones((P, P))).astype(ml_dtypes.bfloat16)
        self.mask_d = nc.inline_tensor(m01, "mask01c")
        mA = np.where(np.triu(np.ones((P, P), bool), 1), np.float32(-1e30), np.float32(0.0))
        self.maskA_d = nc.inline_tensor(mA.astype(np.float32), "maskAc")
        if self.c.CAP:
            self.m01f_d = nc.inline_tensor(np.triu(np.ones((P, P), np.float32)), "m01fc")
            self.s01f_d = nc.inline_tensor(np.triu(np.ones((P, P), np.float32), 1), "s01fc")
            iota = (np.arange(P)[:, None] + P * np.arange(self.c.TT)[None, :]).astype(np.int32)
            self.iota_d = nc.inline_tensor(iota, "iotac")
            ecr = (np.arange(self.c.E, dtype=np.float32) * self.c.CAP)[None, :]
            self.ecr_d = nc.inline_tensor(ecr, "ecrc")
            self.sent_d = nc.inline_tensor(np.full((P, self.c.E * self.c.CAP // P), self.c.T, np.int32), "sentc")
            thr = np.tile(((np.arange(self.c.E) + 1) * self.c.CAP).astype(np.float32), (P, 1))
            self.thr_d = nc.inline_tensor(thr, "thrc")

        self.ctx = ExitStack()
        with self.ctx:
            self.sc = Sched(nc, self.ctx)
            self.mk_sems()
            self.scope0()
            kv_guards = self.enter_kv_scope()
            ln_guards = self.enter_ln_scope()
            self.phase_consts()
            self.phase_ln1()
            if self.stop_after == "ln":
                self.sc.op("sync", lambda e: e.nop(),
                           waits=[(self.s_xn1, self.ln1_done), (self.s_mm, self.s_mm.n)])
            else:
                self.phase_qkv()
            for g in reversed(ln_guards):
                g.__exit__(None, None, None)
            if self.stop_after == "qkv":
                self.sc.op("sync", lambda e: e.nop(),
                           waits=[(self.s_ev, self.qkv_ev_done),
                                  (self.s_q0, self.qst_war[0]), (self.s_q1, self.qst_war[1])])
            if self.stop_after in ("attn", "moe"):
                at_guards = self.enter_attn_scope()
                self.phase_attention()
                for g in reversed(at_guards):
                    g.__exit__(None, None, None)
            for g in reversed(kv_guards):
                g.__exit__(None, None, None)
            if self.stop_after == "attn":
                self.sc.op("sync", lambda e: e.nop(),
                           waits=[(self.s_l2, self.w_done), (self.s_x2st, self.attn_x2),
                                  (self.s_xbst, self.attn_xb), (self.s_sc, self.s_sc.n),
                                  (self.s_tr, self.s_tr.n), (self.s_vc, self.s_vc.n)])
            if self.stop_after == "moe":
                if self.c.CAP:
                    moe_guards = self.enter_moe_scope_sparse()
                    self.phase_moe_sparse()
                    for g in reversed(moe_guards):
                        g.__exit__(None, None, None)
                    self.sc.op("sync", lambda e: e.nop(),
                               waits=[(self.s_ss0, self.moe_sst[0]),
                                      (self.s_ss1, self.moe_sst[1]),
                                      (self.s_x2st, self.attn_x2)])
                else:
                    moe_guards = self.enter_moe_scope()
                    self.phase_moe()
                    for g in reversed(moe_guards):
                        g.__exit__(None, None, None)
                    self.sc.op("sync", lambda e: e.nop(),
                               waits=[(self.s_yst, self.s_yst.n)])
            block = self.ctx.enter_context(nc.Block())
            self.sc.playback(block)
        return nc

    def mk_sems(self):
        s = self.sc
        for name in ("s_x s_xn1 s_wld s_mm s_ev s_vc s_sc s_p1 s_rm s_nr s_p2 "
                     "s_pe s_pb s_cs s_ri s_cx s_ce s_wo s_x2 s_l2 s_tr s_te "
                     "s_st s_qs s_wg s_gu s_ae s_si s_dn s_de s_yi s_xl s_ms "
                     "s_cst s_bo s_x0 s_x1 s_wol s_w0 s_w1 s_q0 s_q1 s_qs0 s_qs1 "
                     "s_tm s_kp s_rb s_sx s_ix s_gx s_tp s_sst s_ycb s_ps "
                     "s_bc s_wa s_wb s_yc0 s_yc1 s_gw0 s_gw1 s_gx0 s_gx1 s_ss0 s_ss1 s_sx0 s_sx1 s_yo0 s_yo1 "
                     "s_x2st s_xbst s_yst s_g0 s_g1 s_dw s_cw").split():
            setattr(self, name, s.sem(name))

    def scope0(self):
        nc, c, ctx = self.nc, self.c, self.ctx
        SB = lambda n, s, d: ctx.enter_context(nc.sbuf_tensor(n, s, d))
        self.w_all = SB("w_allsb", [P, c.TT * c.E], F32)
        self.ident = SB("identsb", [P, P], BF16)
        self.pw = ctx.enter_context(nc.psum_tensor("pw", [P, 512], F32))

    def phase_consts(self):
        sc, c = self.sc, self.c
        sc.op("sync", lambda e: e.dma_start(self.ident[:], self.ident_d.ap()), inc=self.s_cst, inc_by=16)
        sc.op("sync", lambda e: e.dma_start(self.identf[:], self.identf_d.ap()), inc=self.s_cst, inc_by=16)
        sc.op("sync", lambda e: e.dma_start(self.maskT[:], self.mask_d.ap()), inc=self.s_cst, inc_by=16)
        sc.op("sync", lambda e: e.dma_start(self.maskA[:], self.maskA_d.ap()), inc=self.s_cst, inc_by=16)
        sc.op("sync", lambda e: e.dma_start(self.eb_f[:], self.eb_d), inc=self.s_cst, inc_by=16)
        sc.op("sync", lambda e: e.dma_start(self.rw_s[:], self.rw_d), inc=self.s_cst, inc_by=16)
        self.w_const = self.s_cst.n
        sc.op("gpsimd", lambda e: e.dma_start(self.bo_bf[:], self.bo_d), inc=self.s_bo, inc_by=16)
        sc.op("gpsimd", lambda e: e.memset(self.ones_r1[:], 1.0))
        sc.op("gpsimd", lambda e: e.memset(self.ones_c[:], 1.0))
        sc.op("gpsimd", lambda e: e.memset(self.ones_cf[:], 1.0))
        sc.op("gpsimd", lambda e: e.memset(self.ones_rf[:], 1.0))
        sc.op("gpsimd", lambda e: e.memset(self.eps_c[:], 1e-5), inc=self.s_ms, inc_by=1)
        self.w_ones = self.s_ms.n
        for hc in range(c.HC):
            v = sc.op("tensor", lambda e, hc=hc: e.transpose(
                self.pw[:, 0:c.E], self.rw_s[:, hc * P:(hc + 1) * P],
                self.identf[0:c.E, 0:c.E]),
                waits=[(self.s_cst, self.w_const), (self.s_ms, self.w_ones),
                       (self.s_ev, self.s_ev.n)],
                inc=self.s_mm)
            sc.op("scalar", lambda e, hc=hc: e.copy(
                self.rwT[:, hc * c.E:(hc + 1) * c.E], self.pw[:, 0:c.E]),
                waits=[(self.s_mm, v)], inc=self.s_ev)
        self.rwT_done = self.s_ev.n

    def enter_kv_scope(self):
        nc, c = self.nc, self.c
        gs = [nc.sbuf_tensor("kT", [P, c.NG * c.T], BF16),
              nc.sbuf_tensor("vN", [P, c.TT * c.DKV], BF16),
              nc.sbuf_tensor("xring", [P, 2 * c.D], F32),
              nc.sbuf_tensor("stat", [P, 48], F32),
              nc.sbuf_tensor("xnorm", [P, c.D], F32),
              nc.sbuf_tensor("identfsb", [P, P], F32),
              nc.sbuf_tensor("masksb", [P, P], BF16),
              nc.sbuf_tensor("maskAsb", [P, P], F32),
              nc.sbuf_tensor("ones_r1", [1, P], BF16),
              nc.sbuf_tensor("ones_c", [P, 1], BF16),
              nc.sbuf_tensor("ones_cf", [P, 1], F32),
              nc.sbuf_tensor("ones_rf", [1, P], F32),
              nc.sbuf_tensor("eps_c", [P, 1], F32),
              nc.sbuf_tensor("bo_bf", [1, c.D], BF16),
              nc.sbuf_tensor("eb_f", [1, c.E], F32),
              nc.sbuf_tensor("rwT", [P, c.HC * c.E], F32)]
        (self.kT, self.vN, self.xring, self.stat, self.xnorm,
         self.identf, self.maskT, self.maskA, self.ones_r1, self.ones_c,
         self.ones_cf, self.ones_rf, self.eps_c, self.bo_bf, self.eb_f,
         self.rwT) = [g.__enter__() for g in gs]
        return gs

    def enter_ln_scope(self):
        nc, c = self.nc, self.c
        gs = [nc.sbuf_tensor("rw_s", [c.E, c.D], F32),
              nc.sbuf_tensor("xn1T", [P, c.HC * c.T], BF16),
              nc.sbuf_tensor("wqb", [P, 2 * c.HC * c.WQB], BF16),
              nc.sbuf_tensor("xnbf", [P, c.D], BF16),
              nc.sbuf_tensor("qstage", [P, 2 * 512], BF16),
              nc.psum_tensor("tpA", [P, 512], BF16),
              nc.psum_tensor("tpB", [P, 512], BF16),
              nc.psum_tensor("qk0", [P, 512], F32),
              nc.psum_tensor("qk1", [P, 512], F32),
              nc.psum_tensor("qk2", [P, 512], F32)]
        (self.rw_s, self.xn1T, self.wqb, self.xnbf, self.qstage,
         tpA, tpB, qk0, qk1, qk2) = [g.__enter__() for g in gs]
        self.tp = [tpA, tpB]
        self.qk = [self.pw, qk0, qk1, qk2]
        return gs

    def enter_attn_scope(self):
        nc, c = self.nc, self.c
        gs = [nc.sbuf_tensor("qs", [P, 2 * c.HC * P], BF16),
              nc.sbuf_tensor("probs", [P, 4 * c.T], BF16),
              nc.sbuf_tensor("mxs", [P, 4 * 8], F32),
              nc.sbuf_tensor("esc", [P, 6 * c.T], BF16),
              nc.sbuf_tensor("ssc", [P, 96], F32),
              nc.sbuf_tensor("ctxT", [P, 2 * c.HC * P], BF16),
              nc.sbuf_tensor("x2t", [P, c.D], F32),
              nc.sbuf_tensor("xn2f", [P, c.HC * P], F32),
              nc.sbuf_tensor("xn2b", [P, (1 if c.CAP else c.HC) * P], BF16),
              nc.sbuf_tensor("rc", [P, 64], F32),
              nc.sbuf_tensor("xtmb", [P, c.D], BF16),
              nc.sbuf_tensor("wo_sb", [P, c.HC * c.D], BF16),
              nc.psum_tensor("S1a", [P, 512], F32),
              nc.psum_tensor("S1b", [P, 512], F32),
              nc.psum_tensor("S2a", [P, 512], BF16),
              nc.psum_tensor("S2b", [P, 512], BF16),
              nc.psum_tensor("C", [P, 512], F32),
              nc.psum_tensor("DR", [P, 512], F32),
              nc.psum_tensor("LTL", [P, 512], F32)]
        (self.qs, self.probs, self.mxs, self.esc,
         self.ssc, self.ctxT, self.x2t, self.xn2f, self.xn2b, self.rc,
         self.xtmb, self.wo_sb, S1a, S1b, S2a, S2b, self.C, self.DR, self.LTL) = \
            [g.__enter__() for g in gs]
        self.S1 = [S1a, S1b]
        self.S2b = [S2a, S2b]
        c = self.c
        for hcb in range(0, c.HC, 4):
            w = min(4, c.HC - hcb)
            self.sc.op("gpsimd", lambda e, hcb=hcb, w=w: e.dma_start(
                self.wo_sb[:].rearrange("p (cc d) -> p cc d", cc=c.HC)[:, hcb:hcb + w, :],
                self.wo_d[hcb * P:(hcb + w) * P, :].rearrange("(cc p) d -> p cc d", p=P)),
                waits=[(self.s_mm, self.qkv_mm_done), (self.s_ev, self.qkv_ev_done),
                       (self.s_q0, self.qst_war[0]), (self.s_q1, self.qst_war[1])],
                inc=self.s_wol, inc_by=16)
        self.wo_ld = self.s_wol.n
        return gs

    def enter_moe_scope(self):
        nc, c = self.nc, self.c
        gub_w = 2 * 2 * c.HC * c.GIB
        gs = [nc.sbuf_tensor("xn2h", [P, c.HC * c.TH], BF16),
              nc.sbuf_tensor("acc", [P, c.TTH * c.D], F32),
              nc.sbuf_tensor("act", [P, c.IC * c.TH], BF16),
              nc.sbuf_tensor("gub", [P, gub_w], BF16),
              nc.sbuf_tensor("dwn", [P, c.IC * c.D], BF16),
              nc.sbuf_tensor("gssb", [P, 2 * 512], BF16),
              nc.psum_tensor("pg0", [P, 512], F32),
              nc.psum_tensor("pg1", [P, 512], F32),
              nc.psum_tensor("pu0", [P, 512], F32),
              nc.psum_tensor("pu1", [P, 512], F32),
              nc.psum_tensor("pd0", [P, 512], F32),
              nc.psum_tensor("pd1", [P, 512], F32),
              nc.psum_tensor("pd2", [P, 512], F32)]
        (self.xn2h, self.acc, self.act, self.gub, self.dwn, self.gsb,
         pg0, pg1, pu0, pu1, pd0, pd1, pd2) = [g.__enter__() for g in gs]
        self.pg = [pg0, pg1]
        self.pu = [pu0, pu1]
        self.pd = [pd0, pd1, pd2]
        return gs

    def phase_ln1(self):
        sc, c = self.sc, self.c
        xv = self.xn1T[:].rearrange("p (cc t) -> p cc t", cc=c.HC)
        ln_v = [0, 0]
        ln_s = [0, 0]
        tp_war = [0, 0]
        nbatch = (c.HC + 3) // 4
        for tt in range(c.TT):
            xo = (tt % 2) * c.D
            waits = ([(self.s_vc, ln_v[tt % 2]), (self.s_sc, ln_s[tt % 2])]
                     if tt >= 2 else [])
            sx = [self.s_x0, self.s_x1][tt % 2]
            xw = sc.op("sync", lambda e, tt=tt, xo=xo: e.dma_start(
                self.xring[:, xo:xo + c.D], self.x_d[tt * P:(tt + 1) * P, :]),
                waits=waits, inc=sx, inc_by=16)
            sc.op("vector", lambda e, xo=xo: e.reduce_sum(
                self.stat[:, 0:1], self.xring[:, xo:xo + c.D], axis=AX.X),
                waits=[(sx, xw), (self.s_vc, self.s_vc.n)], inc=self.s_vc)
            sq = sc.op("scalar", lambda e, xo=xo: e.activation(
                self.xnorm[:], self.xring[:, xo:xo + c.D], AF.Square),
                waits=[(sx, xw), (self.s_sc, self.s_sc.n)], inc=self.s_sc)
            sc.op("vector", lambda e: e.reduce_sum(self.stat[:, 1:2], self.xnorm[:], axis=AX.X),
                  waits=[(self.s_sc, sq), (self.s_vc, self.s_vc.n)], inc=self.s_vc)
            sc.op("vector", lambda e: e.tensor_scalar(
                self.stat[:, 2:3], self.stat[:, 0:1], 1.0 / c.D, None, ALU.mult),
                waits=[(self.s_vc, self.s_vc.n)], inc=self.s_vc)
            sc.op("vector", lambda e: e.tensor_scalar(
                self.stat[:, 3:4], self.stat[:, 2:3], self.stat[:, 2:3], None, ALU.mult),
                waits=[(self.s_vc, self.s_vc.n)], inc=self.s_vc)
            vv = sc.op("vector", lambda e: e.scalar_tensor_tensor(
                self.stat[:, 4:5], self.stat[:, 1:2], 1.0 / c.D, self.stat[:, 3:4],
                ALU.mult, ALU.subtract),
                waits=[(self.s_vc, self.s_vc.n)], inc=self.s_vc)
            sc.op("scalar", lambda e: e.activation(
                self.stat[:, 5:6], self.stat[:, 4:5], AF.Ln, bias=self.eps_c[:]),
                waits=[(self.s_vc, vv), (self.s_sc, self.s_sc.n)], inc=self.s_sc)
            rs = sc.op("scalar", lambda e: e.activation(
                self.stat[:, 6:7], self.stat[:, 5:6], AF.Exp, scale=-0.5),
                waits=[(self.s_sc, self.s_sc.n)], inc=self.s_sc)
            nv = sc.op("vector", lambda e, xo=xo: e.tensor_scalar(
                self.xnbf[:], self.xring[:, xo:xo + c.D], self.stat[:, 2:3],
                self.stat[:, 6:7], ALU.subtract, ALU.mult),
                waits=[(self.s_sc, rs), (self.s_vc, self.s_vc.n)], inc=self.s_vc)
            ln_v[tt % 2] = nv
            ln_s[tt % 2] = rs
            for bi in range(nbatch):
                b0 = bi * 4
                bn = min(4, c.HC - b0)
                bank = (tt * nbatch + bi) % 2
                mmv = 0
                for j in range(bn):
                    mmv = sc.op("tensor", lambda e, b0=b0, j=j, bank=bank: e.transpose(
                        self.tp[bank][:, j * P:(j + 1) * P],
                        self.xnbf[:, (b0 + j) * P:(b0 + j + 1) * P], self.ident[:]),
                        waits=[(self.s_vc, nv), (self.s_xn1, tp_war[bank])],
                        inc=self.s_mm)
                tp_war[bank] = sc.op("scalar", lambda e, b0=b0, bn=bn, tt=tt, bank=bank: e.copy(
                    xv[:, b0:b0 + bn, tt * P:(tt + 1) * P], self.tp[bank][:, 0:bn * P]),
                    waits=[(self.s_mm, mmv), (self.s_sc, self.s_sc.n)], inc=self.s_xn1)
        self.ln1_done = self.s_xn1.n

    def phase_qkv(self):
        sc, c = self.sc, self.c
        ln_all = self.ln1_done
        qkw = [0, 0, 0, 0]
        bank_i = 0
        wq_war = [0, 0]
        self.qst_war = [0, 0]
        kTv = self.kT[:].rearrange("p (g t) -> p g t", g=c.NG)
        vNv = self.vN[:].rearrange("p (t d) -> p t d", t=c.TT)
        wqbv = self.wqb[:].rearrange("p (s cc w) -> p s cc w", s=2, cc=c.HC)
        qsv = self.qstage[:].rearrange("p (s w) -> p s w", s=2)
        slot = 0
        qi = 0

        def wload(slot, src_ap, width, wait_mm):
            sw = [self.s_w0, self.s_w1][slot]
            return (sw, sc.op("gpsimd", lambda e, slot=slot, src_ap=src_ap, width=width: e.dma_start(
                wqbv[:, slot, :, 0:width], src_ap),
                waits=[(self.s_mm, wait_mm)], inc=sw, inc_by=16))

        for (dq0, dqw) in chunks(c.D, c.WQB):
            src = self.wq_d[:, dq0:dq0 + dqw].rearrange("(cc p) w -> p cc w", p=P)
            swl, wl = wload(slot, src, dqw, wq_war[slot])
            for (dqc0, dqcw) in chunks(dqw, P):
                for (t0, tw) in chunks(c.T, 512):
                    bank = bank_i % 4
                    bank_i += 1
                    pb = self.qk[bank]
                    mmv = 0
                    for hc in range(c.HC):
                        mmv = sc.op("tensor", lambda e, slot=slot, hc=hc, dqc0=dqc0, dqcw=dqcw, t0=t0, tw=tw, pb=pb: e.matmul(
                            pb[0:dqcw, 0:tw],
                            wqbv[:, slot, hc, dqc0:dqc0 + dqcw],
                            self.xn1T[:, hc * c.T + t0: hc * c.T + t0 + tw],
                            start=(hc == 0), stop=(hc == c.HC - 1)),
                            waits=([(self.s_xn1, ln_all), (swl, wl),
                                    (self.s_ev, qkw[bank])] if hc == 0 else []),
                            inc=self.s_mm)
                    qslot = qi % 2
                    qi += 1
                    sq_sem = [self.s_q0, self.s_q1][qslot]
                    ev = sc.op("scalar", lambda e, qslot=qslot, dqcw=dqcw, tw=tw, pb=pb: e.copy(
                        qsv[0:dqcw, qslot, 0:tw], pb[0:dqcw, 0:tw]),
                        waits=[(self.s_mm, mmv), (sq_sem, self.qst_war[qslot]),
                               (self.s_sc, self.s_sc.n)],
                        inc=self.s_ev)
                    qkw[bank] = ev
                    self.qst_war[qslot] = sc.op("sync", lambda e, qslot=qslot, dq0=dq0, dqc0=dqc0, dqcw=dqcw, t0=t0, tw=tw: e.dma_start(
                        self.qt_d[dq0 + dqc0:dq0 + dqc0 + dqcw, t0:t0 + tw],
                        qsv[0:dqcw, qslot, 0:tw]),
                        waits=[(self.s_ev, ev)], inc=sq_sem, inc_by=16)
            wq_war[slot] = self.s_mm.n
            slot = 1 - slot

        for (dk0, dkw) in chunks(c.DKV, c.WQB):
            src = self.wk_d[:, dk0:dk0 + dkw].rearrange("(cc p) w -> p cc w", p=P)
            swl, wl = wload(slot, src, dkw, wq_war[slot])
            for (dkc0, dkcw) in chunks(dkw, P):
                g = (dk0 + dkc0) // P
                for (t0, tw) in chunks(c.T, 512):
                    bank = bank_i % 4
                    bank_i += 1
                    pb = self.qk[bank]
                    mmv = 0
                    for hc in range(c.HC):
                        mmv = sc.op("tensor", lambda e, slot=slot, hc=hc, dkc0=dkc0, dkcw=dkcw, t0=t0, tw=tw, pb=pb: e.matmul(
                            pb[0:dkcw, 0:tw],
                            wqbv[:, slot, hc, dkc0:dkc0 + dkcw],
                            self.xn1T[:, hc * c.T + t0: hc * c.T + t0 + tw],
                            start=(hc == 0), stop=(hc == c.HC - 1)),
                            waits=([(self.s_xn1, ln_all), (swl, wl),
                                    (self.s_ev, qkw[bank])] if hc == 0 else []),
                            inc=self.s_mm)
                    qkw[bank] = sc.op("scalar", lambda e, g=g, t0=t0, tw=tw, pb=pb, dkcw=dkcw: e.copy(
                        kTv[0:dkcw, g, t0:t0 + tw], pb[0:dkcw, 0:tw]),
                        waits=[(self.s_mm, mmv), (self.s_sc, self.s_sc.n)], inc=self.s_ev)
            wq_war[slot] = self.s_mm.n
            slot = 1 - slot

        for (vb0, vbw) in chunks(c.DKV, c.WQB):
            src = self.wv_d[:, vb0:vb0 + vbw].rearrange("(cc p) w -> p cc w", p=P)
            swl, wl = wload(slot, src, vbw, wq_war[slot])
            for tt in range(c.TT):
                bank = bank_i % 4
                bank_i += 1
                pb = self.qk[bank]
                mmv = 0
                for hc in range(c.HC):
                    mmv = sc.op("tensor", lambda e, slot=slot, hc=hc, tt=tt, vbw=vbw, pb=pb: e.matmul(
                        pb[:, 0:vbw],
                        self.xn1T[:, hc * c.T + tt * P: hc * c.T + (tt + 1) * P],
                        wqbv[:, slot, hc, 0:vbw],
                        start=(hc == 0), stop=(hc == c.HC - 1)),
                        waits=([(self.s_xn1, ln_all), (swl, wl),
                                (self.s_ev, qkw[bank])] if hc == 0 else []),
                        inc=self.s_mm)
                qkw[bank] = sc.op("scalar", lambda e, tt=tt, vb0=vb0, vbw=vbw, pb=pb: e.copy(
                    vNv[:, tt, vb0:vb0 + vbw], pb[:, 0:vbw]),
                    waits=[(self.s_mm, mmv), (self.s_sc, self.s_sc.n)], inc=self.s_ev)
            wq_war[slot] = self.s_mm.n
            slot = 1 - slot

        self.qkv_mm_done = self.s_mm.n
        self.qkv_ev_done = self.s_ev.n

    def join_all(self, waits):
        for eng in ("tensor", "vector", "scalar", "sync", "gpsimd"):
            self.sc.op(eng, lambda e: e.nop(), waits=list(waits))

    def phase_attention(self):
        sc, c = self.sc, self.c
        SCALE = float(1.0 / np.sqrt(128.0).astype(np.float32))
        kTv = self.kT[:].rearrange("p (g t) -> p g t", g=c.NG)
        vNv = self.vN[:].rearrange("p (t d) -> p t d", t=c.TT)
        qv = self.qs[:].rearrange("p (s cc w) -> p s cc w", s=2, cc=c.HC)
        prv = self.probs[:].rearrange("p (s w) -> p s w", s=4)
        escv = self.esc[:].rearrange("p (s w) -> p s w", s=6)
        ctv = self.ctxT[:].rearrange("p (s cc w) -> p s cc w", s=2, cc=c.HC)
        wov = self.wo_sb[:].rearrange("p (cc d) -> p cc d", cc=c.HC)
        xnfv = self.xn2f[:].rearrange("p (cc w) -> p cc w", cc=c.HC)
        xnbv = self.xn2b[:].rearrange("p (cc w) -> p cc w", cc=c.HC)

        st = {
            "q_ld": [0, 0], "q_war": [0, 0],
            "s1_war": [0, 0], "s1_i": 0, "s2_pe": [0, 0], "s2_i": 0,
            "pr_war": [0, 0],
            "esc_war": [0, 0, 0],
            "ctx_war": [0, 0],
            "ctx_ev": [0, 0],
            "ltl_free": 0, "xn2f_rd": 0, "xn2b_st": 0, "x2_sq": 0, "x2_st": 0,
            "xr_war": [0, 0],
        }
        stage4_q = []
        self.ce_all = {}

        def stage_a1(qt, u, hx):
            h0 = 2 * u
            g = h0 // c.GS
            nk = (qt + 1) * P
            qslot = qt % 2
            us = u % 3
            kblocks = chunks(nk, 512)
            hx.update(g=g, nk=nk, us=us, us2=u % 2, kblocks=kblocks, h0=h0)
            nch = len(kblocks)
            for hi in range(2):
                h = h0 + hi
                sl = 2 * us + hi
                for ci, (k0, kw) in enumerate(kblocks):
                    bank = st["s1_i"] % len(self.S1)
                    st["s1_i"] += 1
                    mmv = sc.op("tensor", lambda e, h=h, g=g, qslot=qslot, k0=k0, kw=kw, bank=bank: e.matmul(
                        self.S1[bank][:, 0:kw], qv[:, qslot, h, :], kTv[:, g, k0:k0 + kw],
                        start=True, stop=True),
                        waits=[([self.s_qs0, self.s_qs1][qslot], st["q_ld"][qslot]),
                               (self.s_cs, st["s1_war"][bank]),
                               (self.s_ev, self.qkv_ev_done)],
                        inc=self.s_p1)
                    if ci == nch - 1:
                        mmv_m = sc.op("vector", lambda e, kw=kw, bank=bank: e.tensor_add(
                            self.S1[bank][:, kw - P:kw], self.S1[bank][:, kw - P:kw],
                            self.maskA[:]),
                            waits=[(self.s_p1, mmv), (self.s_vc, self.s_vc.n)], inc=self.s_vc)
                        st["s1_war"][bank] = sc.op("scalar", lambda e, k0=k0, kw=kw, bank=bank, sl=sl, ci=ci: e.activation(
                            escv[:, sl, k0:k0 + kw], self.S1[bank][:, 0:kw], AF.Exp,
                            scale=SCALE,
                            accum_out=self.ssc[:, sl * 12 + ci: sl * 12 + ci + 1]),
                            waits=[(self.s_vc, mmv_m), (self.s_sc, self.s_sc.n),
                                   (self.s_p2, st["esc_war"][us])],
                            inc=self.s_cs)
                    else:
                        st["s1_war"][bank] = sc.op("scalar", lambda e, k0=k0, kw=kw, bank=bank, sl=sl, ci=ci: e.activation(
                            escv[:, sl, k0:k0 + kw], self.S1[bank][:, 0:kw], AF.Exp,
                            scale=SCALE,
                            accum_out=self.ssc[:, sl * 12 + ci: sl * 12 + ci + 1]),
                            waits=[(self.s_p1, mmv), (self.s_sc, self.s_sc.n),
                                   (self.s_p2, st["esc_war"][us])],
                            inc=self.s_cs)
                tot = sc.op("vector", lambda e, nch=nch, sl=sl: e.reduce_sum(
                    self.stat[:, 36 + sl:37 + sl],
                    self.ssc[:, sl * 12: sl * 12 + nch], axis=AX.X),
                    waits=[(self.s_cs, self.s_cs.n), (self.s_vc, self.s_vc.n)],
                    inc=self.s_vc)
                sc.op("vector", lambda e, sl=sl: e.reciprocal(
                    self.stat[:, 42 + sl:43 + sl], self.stat[:, 36 + sl:37 + sl]),
                    waits=[(self.s_vc, tot)], inc=self.s_vc)
            hx["cv"] = self.s_vc.n

        def stage_b(qt, u, hx):
            us, us2, kblocks, nk = hx["us"], hx["us2"], hx["kblocks"], hx["nk"]
            rs = 0
            for hi in range(2):
                sl = 2 * us + hi
                rs = sc.op("vector", lambda e, nk=nk, sl=sl: e.tensor_scalar(
                    escv[:, sl, 0:nk], escv[:, sl, 0:nk],
                    self.stat[:, 42 + sl:43 + sl], None, ALU.mult),
                    waits=[(self.s_vc, hx["cv"] if hi == 0 else self.s_vc.n)],
                    inc=self.s_vc)
            pe_marks = [[], []]
            for hi in range(2):
                sl = 2 * us + hi
                psl = 2 * us2 + hi
                for gi, (kb0, kbw) in enumerate(kblocks):
                    bank = st["s2_i"] % 2
                    st["s2_i"] += 1
                    mmv = 0
                    for (kc0, kcw) in chunks(kbw, P):
                        j = kc0 // P
                        mmv = sc.op("tensor", lambda e, sl=sl, o=kb0 + kc0, j=j, bank=bank: e.transpose(
                            self.S2b[bank][:, j * P:(j + 1) * P],
                            escv[:, sl, o:o + P], self.ident[:]),
                            waits=[(self.s_vc, rs), (self.s_pe, st["s2_pe"][bank])],
                            inc=self.s_p2)
                    pe = sc.op("vector", lambda e, bank=bank, kb0=kb0, kbw=kbw, psl=psl: e.tensor_copy(
                        prv[:, psl, kb0:kb0 + kbw], self.S2b[bank][:, 0:kbw]),
                        waits=[(self.s_p2, mmv), (self.s_cx, st["pr_war"][us2]),
                               (self.s_vc, self.s_vc.n)],
                        inc=self.s_pe)
                    st["s2_pe"][bank] = pe
                    pe_marks[hi].append(pe)
            st["esc_war"][us] = self.s_p2.n
            hx["pe_marks"] = pe_marks

        def stage_c(qt, u, hx):
            g, nk, us, kblocks = hx["g"], hx["nk"], hx["us2"], hx["kblocks"]
            h0 = hx["h0"]
            pe_marks = hx["pe_marks"]
            n_k = nk // P
            bank = self.C if u % 2 == 0 else self.DR
            cxv = 0
            ki = 0
            for bi, (kb0, kbw) in enumerate(kblocks):
                for (kc0, kcw) in chunks(kbw, P):
                    tt_k = (kb0 + kc0) // P
                    first = ki == 0
                    last = ki == n_k - 1
                    ki += 1
                    for hi in range(2):
                        sl = 2 * us + hi
                        w = [(self.s_pe, pe_marks[hi][bi])]
                        if first and hi == 0:
                            w += [(self.s_ce, st["ctx_ev"][us])]
                        cxv = sc.op("tensor", lambda e, tt_k=tt_k, g=g, sl=sl, hi=hi, o=kb0 + kc0, first=first, last=last, bank=bank: e.matmul(
                            bank[:, hi * P:(hi + 1) * P],
                            vNv[:, tt_k, g * P:(g + 1) * P],
                            prv[:, sl, o:o + P],
                            start=(first and hi == 0), stop=(last and hi == 1)),
                            waits=w, inc=self.s_cx)
            st["pr_war"][us] = cxv
            ce = sc.op("scalar", lambda e, h0=h0, qt=qt, bank=bank: e.copy(
                ctv[:, qt % 2, h0, :].tensor[0:0] if False else
                self.ctxT[:, (qt % 2) * c.HC * P + h0 * P: (qt % 2) * c.HC * P + (h0 + 2) * P],
                bank[:, 0:2 * P]),
                waits=[(self.s_cx, cxv), (self.s_wo, st["ctx_war"][qt % 2]),
                       (self.s_sc, self.s_sc.n)],
                inc=self.s_ce)
            st["ctx_ev"][us] = ce

        def stage4(qt):
            tt = qt
            cslot = qt % 2
            xslot = qt % 2
            xo = xslot * c.D
            sx = [self.s_x0, self.s_x1][xslot]
            xw = sc.op("sync", lambda e, tt=tt, xo=xo: e.dma_start(
                self.xring[:, xo:xo + c.D], self.x_d[tt * P:(tt + 1) * P, :]),
                waits=[(self.s_x2, st["xr_war"][xslot])], inc=sx, inc_by=16)
            x2_marks = []
            prev_ev = 0
            for (d0, dw) in chunks(c.D, 512):
                mmv = 0
                for hc in range(c.HC):
                    w = ([(self.s_ce, self.ce_all[qt]), (self.s_wol, self.wo_ld),
                          (self.s_x2, prev_ev if prev_ev else st["xr_war"][1 - xslot]),
                          (self.s_te, st.get("pw_tp_ev", 0)),
                          (self.s_ev, self.qkv_ev_done)] if hc == 0 else [])
                    mmv = sc.op("tensor", lambda e, hc=hc, d0=d0, dw=dw, cslot=cslot: e.matmul(
                        self.pw[:, 0:dw], ctv[:, cslot, hc, :], wov[:, hc, d0:d0 + dw],
                        start=(hc == 0), stop=False),
                        waits=w, inc=self.s_wo, defer=stage4_q)
                mmv = sc.op("tensor", lambda e, d0=d0, dw=dw: e.matmul(
                    self.pw[:, 0:dw], self.ones_r1[:], self.bo_bf[0:1, d0:d0 + dw],
                    start=False, stop=True),
                    waits=[(self.s_ms, self.w_ones), (self.s_bo, 16)], inc=self.s_wo, defer=stage4_q)
                ev = sc.op("vector", lambda e, d0=d0, dw=dw, xo=xo: e.tensor_add(
                    self.x2t[:, d0:d0 + dw], self.pw[:, 0:dw],
                    self.xring[:, xo + d0:xo + d0 + dw]),
                    waits=[(self.s_wo, mmv), (sx, xw),
                           (self.s_x2st, st["x2_st"]), (self.s_sc, st["x2_sq"]),
                           (self.s_vc, self.s_vc.n)],
                    inc=self.s_x2)
                prev_ev = ev
                x2_marks.append(ev)
            st["ctx_war"][cslot] = self.s_wo.n
            x2v = x2_marks[-1]
            st["xr_war"][xslot] = x2v
            x2dst = self.y_d if self.c.CAP else self.x2_d
            st["x2_st"] = sc.op("sync", lambda e, tt=tt, x2dst=x2dst: e.dma_start(
                x2dst[tt * P:(tt + 1) * P, :], self.x2t[:]),
                waits=[(self.s_x2, x2v)], inc=self.s_x2st, inc_by=16)
            sc.op("vector", lambda e: e.reduce_sum(self.stat[:, 16:17], self.x2t[:], axis=AX.X),
                  waits=[(self.s_x2, x2v), (self.s_vc, self.s_vc.n)], inc=self.s_vc)
            sq = sc.op("scalar", lambda e: e.activation(self.xnorm[:], self.x2t[:], AF.Square),
                       waits=[(self.s_x2, x2v), (self.s_sc, self.s_sc.n),
                              (self.s_tr, st["xn2f_rd"])], inc=self.s_sc)
            st["x2_sq"] = sq
            sc.op("vector", lambda e: e.reduce_sum(self.stat[:, 17:18], self.xnorm[:], axis=AX.X),
                  waits=[(self.s_sc, sq), (self.s_vc, self.s_vc.n)], inc=self.s_vc)
            sc.op("vector", lambda e: e.tensor_scalar(
                self.stat[:, 18:19], self.stat[:, 16:17], 1.0 / c.D, None, ALU.mult),
                waits=[(self.s_vc, self.s_vc.n)], inc=self.s_vc)
            sc.op("vector", lambda e: e.tensor_scalar(
                self.stat[:, 19:20], self.stat[:, 18:19], self.stat[:, 18:19], None, ALU.mult),
                waits=[(self.s_vc, self.s_vc.n)], inc=self.s_vc)
            vv = sc.op("vector", lambda e: e.scalar_tensor_tensor(
                self.stat[:, 20:21], self.stat[:, 17:18], 1.0 / c.D, self.stat[:, 19:20],
                ALU.mult, ALU.subtract),
                waits=[(self.s_vc, self.s_vc.n)], inc=self.s_vc)
            sc.op("scalar", lambda e: e.activation(
                self.stat[:, 21:22], self.stat[:, 20:21], AF.Ln, bias=self.eps_c[:]),
                waits=[(self.s_vc, vv), (self.s_sc, self.s_sc.n)], inc=self.s_sc)
            rs = sc.op("scalar", lambda e: e.activation(
                self.stat[:, 22:23], self.stat[:, 21:22], AF.Exp, scale=-0.5),
                waits=[(self.s_sc, self.s_sc.n)], inc=self.s_sc)
            nv = sc.op("vector", lambda e: e.tensor_scalar(
                self.xnorm[:], self.x2t[:], self.stat[:, 18:19], self.stat[:, 22:23],
                ALU.subtract, ALU.mult),
                waits=[(self.s_sc, rs), (self.s_vc, self.s_vc.n)], inc=self.s_vc)
            st["x2_nv"] = nv
            ev_marks = [st.get("ltl_tp_ev", 0), st.get("pw_tp_ev", 0)]
            for hc in range(c.HC):
                tb = hc % 2 if c.CAP else 0
                dst = [self.LTL, self.pw][tb]
                if c.CAP:
                    w = [(self.s_vc, nv), (self.s_te, ev_marks[tb])]
                else:
                    w = [(self.s_vc, nv), (self.s_te, self.s_te.n)]
                if hc == 0:
                    w += [(self.s_l2, st["ltl_free"]), (self.s_tr, st["xn2f_rd"])]
                if hc == 1 and c.CAP:
                    w += [(self.s_x2, x2v)]
                tvv = sc.op("tensor", lambda e, hc=hc, dst=dst: e.transpose(
                    dst[:, 0:P], self.xnorm[:, hc * P:(hc + 1) * P], self.identf[:]),
                    waits=w, inc=self.s_tr, defer=stage4_q)
                t1 = sc.op("scalar", lambda e, hc=hc, dst=dst: e.copy(xnfv[:, hc, :], dst[:, 0:P]),
                      waits=[(self.s_tr, tvv), (self.s_sc, self.s_sc.n)], inc=self.s_te)
                ev_marks[tb] = t1
                if not c.CAP:
                    sc.op("vector", lambda e, hc=hc: e.tensor_copy(xnbv[:, hc, :], xnfv[:, hc, :]),
                          waits=[(self.s_te, t1), (self.s_xbst, st["xn2b_st"]),
                                 (self.s_vc, self.s_vc.n)],
                          inc=self.s_te)
            te_all = self.s_te.n
            st["ltl_tp_ev"], st["pw_tp_ev"] = ev_marks
            if not c.CAP:
                st["xn2b_st"] = sc.op("sync", lambda e, qt=qt: e.dma_start(
                    self.xn2_d[:, qt * P:(qt + 1) * P].rearrange("(cc p) w -> p cc w", p=P),
                    xnbv[:]),
                    waits=[(self.s_te, te_all)], inc=self.s_xbst, inc_by=16)
            if c.CAP:
                tmc = sc.op("vector", lambda e: e.tensor_copy(self.xtmb[:], self.xnorm[:]),
                            waits=[(self.s_vc, nv), (self.s_tm, st.get("xtm_st", 0)),
                                   (self.s_vc, self.s_vc.n)],
                            inc=self.s_vc)
                st["xtm_st"] = sc.op("sync", lambda e, qt=qt: e.dma_start(
                    self.xntm_d[qt * P:(qt + 1) * P, :], self.xtmb[:]),
                    waits=[(self.s_vc, tmc)], inc=self.s_tm, inc_by=16)
            rv = 0
            for hc in range(c.HC):
                w = ([(self.s_te, te_all), (self.s_ev, self.rwT_done)] if hc == 0 else [])
                rv = sc.op("tensor", lambda e, hc=hc: e.matmul(
                    self.LTL[:, P:P + c.E], xnfv[:, hc, :],
                    self.rwT[:, hc * c.E:(hc + 1) * c.E],
                    start=(hc == 0), stop=False),
                    waits=w, inc=self.s_tr, defer=stage4_q)
            rv = sc.op("tensor", lambda e: e.matmul(
                self.LTL[:, P:P + c.E], self.ones_rf[:], self.eb_f[:],
                start=False, stop=True),
                waits=[(self.s_cst, self.w_const)], inc=self.s_tr, defer=stage4_q)
            st["xn2f_rd"] = rv
            lg = sc.op("vector", lambda e: e.tensor_copy(self.rc[:, 0:c.E], self.LTL[:, P:P + c.E]),
                       waits=[(self.s_tr, rv), (self.s_vc, self.s_vc.n)], inc=self.s_l2)
            st["ltl_free"] = lg
            m8 = sc.op("vector", lambda e: e.max(out=self.rc[:, 16:24], in_=self.rc[:, 0:c.E]),
                       waits=[(self.s_l2, lg), (self.s_vc, self.s_vc.n)], inc=self.s_vc)
            ng = sc.op("vector", lambda e: e.tensor_scalar(
                self.rc[:, 24:25], self.rc[:, 16:17], -1.0, None, ALU.mult),
                waits=[(self.s_vc, m8)], inc=self.s_vc)
            ex = sc.op("scalar", lambda e: e.activation(
                self.rc[:, 0:c.E],
                self.rc[:, 0:c.E], AF.Exp, bias=self.rc[:, 24:25]),
                waits=[(self.s_vc, ng), (self.s_sc, self.s_sc.n)], inc=self.s_sc)
            e4 = sc.op("scalar", lambda e: e.activation(
                self.rc[:, 25:29], self.rc[:, 16:20], AF.Exp, bias=self.rc[:, 24:25]),
                waits=[(self.s_sc, ex)], inc=self.s_sc)
            sm = sc.op("vector", lambda e: e.reduce_sum(self.rc[:, 29:30], self.rc[:, 25:29], axis=AX.X),
                       waits=[(self.s_sc, e4), (self.s_vc, self.s_vc.n)], inc=self.s_vc)
            rcp = sc.op("vector", lambda e: e.reciprocal(self.rc[:, 30:31], self.rc[:, 29:30]),
                        waits=[(self.s_vc, sm)], inc=self.s_vc)
            kp = sc.op("vector", lambda e: e.tensor_scalar(
                self.rc[:, 32:32 + c.E], self.rc[:, 0:c.E], self.rc[:, 28:29], None, ALU.is_ge),
                waits=[(self.s_vc, rcp)], inc=self.s_vc)
            if c.CAP:
                tl = sc.op("vector", lambda e: e.tensor_scalar(
                    self.rc[:, 31:32], self.rc[:, 28:29], 0.99601, None, ALU.mult),
                    waits=[(self.s_vc, kp)], inc=self.s_vc)
                kl = sc.op("vector", lambda e: e.tensor_scalar(
                    self.rc[:, 48:48 + c.E], self.rc[:, 0:c.E], self.rc[:, 31:32],
                    None, ALU.is_ge),
                    waits=[(self.s_vc, tl)], inc=self.s_vc)
                wv_ = sc.op("vector", lambda e: e.tensor_scalar(
                    self.rc[:, 0:c.E], self.rc[:, 0:c.E], self.rc[:, 30:31], None, ALU.mult),
                    waits=[(self.s_vc, kl)], inc=self.s_vc)
                wv_ = sc.op("vector", lambda e: e.tensor_mul(
                    self.rc[:, 0:c.E], self.rc[:, 0:c.E], self.rc[:, 32:32 + c.E]),
                    waits=[(self.s_vc, wv_)], inc=self.s_vc)
                sc.op("vector", lambda e, tt=tt: e.scalar_tensor_tensor(
                    self.w_all[:, tt * c.E:(tt + 1) * c.E], self.rc[:, 48:48 + c.E],
                    1e-30, self.rc[:, 0:c.E], ALU.mult, ALU.add),
                    waits=[(self.s_vc, wv_)], inc=self.s_l2)
            else:
                wv_ = sc.op("vector", lambda e: e.tensor_scalar(
                    self.rc[:, 0:c.E], self.rc[:, 0:c.E], self.rc[:, 30:31], None, ALU.mult),
                    waits=[(self.s_vc, kp)], inc=self.s_vc)
                sc.op("vector", lambda e, tt=tt: e.tensor_mul(
                    self.w_all[:, tt * c.E:(tt + 1) * c.E], self.rc[:, 0:c.E],
                    self.rc[:, 32:32 + c.E]),
                    waits=[(self.s_vc, wv_)], inc=self.s_l2)

        self.join_all([(self.s_mm, self.qkv_mm_done), (self.s_ev, self.qkv_ev_done),
                       (self.s_xn1, self.ln1_done),
                       (self.s_q0, self.qst_war[0]), (self.s_q1, self.qst_war[1]),
                       (self.s_w0, self.s_w0.n), (self.s_w1, self.s_w1.n),
                       (self.s_vc, self.s_vc.n), (self.s_sc, self.s_sc.n)])
        for qt in range(c.TT if self.qt_limit is None else self.qt_limit):
            sc.splice("tensor", stage4_q)
            qslot = qt % 2
            qwaits = ([(self.s_p2, st["q_war"][qslot])] if qt >= 2 else
                      [(self.s_q0, self.qst_war[0]), (self.s_q1, self.qst_war[1]),
                       (self.s_mm, self.qkv_mm_done), (self.s_ev, self.qkv_ev_done)])
            sqs = [self.s_qs0, self.s_qs1][qslot]
            st["q_ld"][qslot] = sc.op("sync", lambda e, qt=qt, qslot=qslot: e.dma_start(
                qv[:, qslot, :, :],
                self.qt_d[:, qt * P:(qt + 1) * P].rearrange("(cc p) w -> p cc w", p=P)),
                waits=qwaits, inc=sqs, inc_by=16)
            st["q_sem"] = sqs
            nu = (c.NH if self.head_limit is None else self.head_limit) // 2
            hctx = [dict() for _ in range(nu)]
            for step in range(nu + 3):
                if step >= 3:
                    stage_c(qt, step - 3, hctx[step - 3])
                if 2 <= step < nu + 2:
                    stage_b(qt, step - 2, hctx[step - 2])
                if step < nu:
                    stage_a1(qt, step, hctx[step])
            self.ce_all[qt] = self.s_ce.n
            st["q_war"][qslot] = self.s_p2.n
            if not self.no_stage4:
                stage4(qt)
        sc.splice("tensor", stage4_q)
        self.w_done = self.s_l2.n
        self.attn_xb = self.s_xbst.n
        self.attn_x2 = self.s_x2st.n

    def phase_moe(self):
        sc, c = self.sc, self.c
        xnv = self.xn2h[:].rearrange("p (cc t) -> p cc t", cc=c.HC)
        accv = self.acc[:].rearrange("p (t d) -> p t d", t=c.TTH)
        actv = self.act[:].rearrange("p (ic t) -> p ic t", ic=c.IC)
        gubv = self.gub[:].rearrange("p (s cc w) -> p s cc w", s=4, cc=c.HC)
        dwv = self.dwn[:].rearrange("p (ic d) -> p ic d", ic=c.IC)
        gsv = self.gsb[:].rearrange("p (s w) -> p s w", s=2)

        gu_war = [0, 0]
        pg_war = [0, 0]
        pu_war = [0, 0]
        pd_war = [0, 0, 0]
        gs_war = [0, 0]
        act_rd = 0
        dwn_rd = 0
        y_total = 0
        xl_prev = 0

        self.join_all([(self.s_l2, self.w_done), (self.s_sc, self.s_sc.n),
                       (self.s_tr, self.s_tr.n), (self.s_te, self.s_te.n),
                       (self.s_vc, self.s_vc.n), (self.s_ce, self.s_ce.n),
                       (self.s_x2, self.s_x2.n), (self.s_x2st, self.attn_x2),
                       (self.s_xbst, self.attn_xb),
                       (self.s_qs0, self.s_qs0.n), (self.s_qs1, self.s_qs1.n),
                       (self.s_x0, self.s_x0.n), (self.s_x1, self.s_x1.n),
                       (self.s_wol, self.wo_ld)]
                      + ([(self.s_tm, self.s_tm.n)] if self.c.CAP else []))
        for half in range(c.N_HALVES):
            t_off = half * c.TH
            xl = sc.op("sync", lambda e, t_off=t_off: e.dma_start(
                xnv[:],
                self.xn2_d[:, t_off:t_off + c.TH].rearrange("(cc p) w -> p cc w", p=P)),
                waits=[(self.s_gu, xl_prev), (self.s_l2, self.w_done),
                       (self.s_xbst, self.attn_xb)],
                inc=self.s_xl, inc_by=16)
            for tt in range(c.TTH):
                sc.op("sync", lambda e, tt=tt, t_off=t_off: e.dma_start(
                    accv[:, tt, :], self.x2_d[t_off + tt * P: t_off + (tt + 1) * P, :]),
                    waits=[(self.s_yst, y_total), (self.s_x2st, self.attn_x2)],
                    inc=self.s_yi, inc_by=16)
            yi_all = self.s_yi.n
            for ei in range(c.E):
                ib_list = chunks(c.DI, c.GIB)
                gl_marks = {}

                def gu_load(bi, wait_mark):
                    i0, iw = ib_list[bi]
                    slot = bi % 2
                    sg = [self.s_g0, self.s_g1][slot]
                    sc.op("gpsimd", lambda e, ei=ei, i0=i0, iw=iw, slot=slot: e.dma_start(
                        gubv[:, 2 * slot, :, 0:iw],
                        self.gate_d[ei][:, i0:i0 + iw].rearrange("(cc p) w -> p cc w", p=P)),
                        waits=[(self.s_gu, wait_mark)], inc=sg, inc_by=16)
                    gl = sc.op("gpsimd", lambda e, ei=ei, i0=i0, iw=iw, slot=slot: e.dma_start(
                        gubv[:, 2 * slot + 1, :, 0:iw],
                        self.up_d[ei][:, i0:i0 + iw].rearrange("(cc p) w -> p cc w", p=P)),
                        waits=[], inc=sg, inc_by=16)
                    gl_marks[bi] = (sg, gl)

                for bi in range(min(2, len(ib_list))):
                    gu_load(bi, gu_war[bi % 2])
                dl_next = [0]
                dl_last = [0]

                def down_load_some(k):
                    while dl_next[0] < c.IC and k > 0:
                        ic = dl_next[0]
                        dl_next[0] += 1
                        k -= 1
                        dl_last[0] = sc.op("gpsimd", lambda e, ei=ei, ic=ic: e.dma_start(
                            dwv[:, ic, :], self.down_d[ei][ic * P:(ic + 1) * P, :]),
                            waits=[(self.s_dn, dwn_rd)], inc=self.s_dw, inc_by=16)
                pi = 0
                for bi, (i0, iw) in enumerate(ib_list):
                    slot = bi % 2
                    for (ic0, icw) in chunks(iw, P):
                        ic = (i0 + ic0) // P
                        for (t0, tw) in chunks(c.TH, 512):
                            bank = pi % 2
                            gslot = pi % 2
                            pi += 1
                            gm = um = 0
                            for hc in range(c.HC):
                                gm = sc.op("tensor", lambda e, slot=slot, hc=hc, ic0=ic0, t0=t0, tw=tw, bank=bank: e.matmul(
                                    self.pg[bank][:, 0:tw],
                                    gubv[:, 2 * slot, hc, ic0:ic0 + P],
                                    xnv[:, hc, t0:t0 + tw],
                                    start=(hc == 0), stop=(hc == c.HC - 1)),
                                    waits=([gl_marks[bi], (self.s_xl, xl),
                                            (self.s_si, pg_war[bank])] if hc == 0 else []),
                                    inc=self.s_gu)
                            for hc in range(c.HC):
                                um = sc.op("tensor", lambda e, slot=slot, hc=hc, ic0=ic0, t0=t0, tw=tw, bank=bank: e.matmul(
                                    self.pu[bank][:, 0:tw],
                                    gubv[:, 2 * slot + 1, hc, ic0:ic0 + P],
                                    xnv[:, hc, t0:t0 + tw],
                                    start=(hc == 0), stop=(hc == c.HC - 1)),
                                    waits=([(self.s_ae, pu_war[bank])] if hc == 0 else []),
                                    inc=self.s_gu)
                            sv = sc.op("scalar", lambda e, bank=bank, tw=tw, gslot=gslot: e.activation(
                                gsv[:, gslot, 0:tw], self.pg[bank][:, 0:tw], AF.Silu),
                                waits=[(self.s_gu, gm), (self.s_ae, gs_war[gslot]),
                                       (self.s_sc, self.s_sc.n)],
                                inc=self.s_si)
                            pg_war[bank] = sv
                            av = sc.op("vector", lambda e, bank=bank, tw=tw, gslot=gslot, ic=ic, t0=t0: e.tensor_mul(
                                actv[:, ic, t0:t0 + tw], gsv[:, gslot, 0:tw],
                                self.pu[bank][:, 0:tw]),
                                waits=[(self.s_gu, um), (self.s_si, sv),
                                       (self.s_dn, act_rd), (self.s_vc, self.s_vc.n)],
                                inc=self.s_ae)
                            pu_war[bank] = av
                            gs_war[gslot] = av
                    gu_war[slot] = self.s_gu.n
                    if bi + 2 < len(ib_list):
                        gu_load(bi + 2, gu_war[slot])
                    if bi >= 1:
                        down_load_some(2)
                down_load_some(c.IC)
                dl_all = dl_last[0]
                act_done = self.s_ae.n
                bank_i = 0
                for tt in range(c.TTH):
                    for (d0, dw) in chunks(c.D, 512):
                        bank = bank_i % 3
                        bank_i += 1
                        dm = 0
                        for ic in range(c.IC):
                            w = ([(self.s_ae, act_done), (self.s_de, pd_war[bank]),
                                  (self.s_dw, dl_all)] if ic == 0 else [])
                            dm = sc.op("tensor", lambda e, ic=ic, tt=tt, d0=d0, dw=dw, bank=bank: e.matmul(
                                self.pd[bank][:, 0:dw],
                                actv[:, ic, tt * P:(tt + 1) * P],
                                dwv[:, ic, d0:d0 + dw],
                                start=(ic == 0), stop=(ic == c.IC - 1)),
                                waits=w, inc=self.s_dn)
                        wt = half * c.TTH + tt
                        pd_war[bank] = sc.op("vector", lambda e, tt=tt, d0=d0, dw=dw, bank=bank, wt=wt, ei=ei: e.scalar_tensor_tensor(
                            accv[:, tt, d0:d0 + dw], self.pd[bank][:, 0:dw],
                            self.w_all[:, wt * c.E + ei: wt * c.E + ei + 1],
                            accv[:, tt, d0:d0 + dw], ALU.mult, ALU.add),
                            waits=[(self.s_dn, dm), (self.s_yi, yi_all),
                                   (self.s_l2, self.w_done), (self.s_vc, self.s_vc.n)],
                            inc=self.s_de)
                act_rd = self.s_dn.n
                dwn_rd = self.s_dn.n
            xl_prev = self.s_gu.n
            for tt in range(c.TTH):
                sc.op("sync", lambda e, tt=tt, t_off=t_off: e.dma_start(
                    self.y_d[t_off + tt * P: t_off + (tt + 1) * P, :], accv[:, tt, :]),
                    waits=[(self.s_de, self.s_de.n)], inc=self.s_yst, inc_by=16)
            y_total = self.s_yst.n


    def enter_moe_scope_sparse(self):
        nc, c = self.nc, self.c
        gub_w = 2 * 2 * c.HC * c.GIB
        gs = [nc.sbuf_tensor("xngT", [P, c.HC * c.CAP], BF16),
              nc.sbuf_tensor("acts", [P, c.IC * c.CAP], BF16),
              nc.sbuf_tensor("gub", [P, gub_w], BF16),
              nc.sbuf_tensor("dwn", [P, c.IC * c.D], BF16),
              nc.sbuf_tensor("gssb", [P, 2 * 512], BF16),
              nc.sbuf_tensor("xstg", [P, 2 * c.D], BF16),
              nc.sbuf_tensor("sst", [P, 2 * min(1024, c.D)], F32),
              nc.sbuf_tensor("kpall", [P, c.TT * c.E], F32),
              nc.sbuf_tensor("rksc", [P, 2 * c.E], F32),
              nc.sbuf_tensor("peib", [P, c.TT * c.E], mybir.dt.int32),
              nc.sbuf_tensor("wrtb", [P, 2 * c.E], F32),
              nc.sbuf_tensor("idxc", [P, 2 * c.CT], mybir.dt.int32),
              nc.sbuf_tensor("wrg", [P, 2 * c.CT * c.E], F32),
              nc.sbuf_tensor("m01f_sb", [P, P], F32),
              nc.sbuf_tensor("s01f_sb", [P, P], F32),
              nc.sbuf_tensor("iota_sb", [P, c.TT], mybir.dt.int32),
              nc.sbuf_tensor("ecr_sb", [1, c.E], F32),
              nc.sbuf_tensor("thr_sb", [P, c.E], F32),
              nc.sbuf_tensor("trow", [1, c.TT * c.E], F32),
              nc.sbuf_tensor("orow", [1, c.TT * c.E], F32),
              nc.sbuf_tensor("tot16", [c.TT, c.E], F32),
              nc.psum_tensor("pg0", [P, 512], F32),
              nc.psum_tensor("pg1", [P, 512], F32),
              nc.psum_tensor("pu0", [P, 512], F32),
              nc.psum_tensor("pu1", [P, 512], F32),
              nc.psum_tensor("ptp", [P, 512], BF16),
              nc.psum_tensor("ptq", [P, 512], BF16)]
        (self.xngT, self.acts, self.gub, self.dwn, self.gsb, self.xstg,
         self.sst, self.kpall, self.rksc, self.peib, self.wrtb, self.idxc,
         self.wrg, self.m01f_sb, self.s01f_sb,
         self.iota_sb, self.ecr_sb, self.thr_sb, self.trow, self.orow, self.tot16,
         pg0, pg1, pu0, pu1, ptp, ptq) = [g.__enter__() for g in gs]
        self.pg = [pg0, pg1]
        self.pu = [pu0, pu1]
        self.pd = self.pg
        self.ptp = [ptp, ptq]
        self.tp_grp = 0
        self.tp_bank_war = [0, 0]
        return gs

    def phase_router_build(self):
        sc, c = self.sc, self.c
        BIG = 1.0e7
        kv = self.kpall[:].rearrange("p (t e) -> p t e", t=c.TT)

        def _mkregs(e):
            self.bnd_ix = e.alloc_register("bnd_ix")
            e.reg_mov(self.bnd_ix, c.E * c.CAP - 1)
            self.bnd_t = e.alloc_register("bnd_t")
            return e.reg_mov(self.bnd_t, c.T - 1)
        sc.op("gpsimd", _mkregs)
        sc.op("sync", lambda e: e.dma_start(self.m01f_sb[:], self.m01f_d.ap()), inc=self.s_rb, inc_by=16)
        sc.op("sync", lambda e: e.dma_start(self.s01f_sb[:], self.s01f_d.ap()), inc=self.s_rb, inc_by=16)
        sc.op("sync", lambda e: e.dma_start(self.iota_sb[:], self.iota_d.ap()), inc=self.s_rb, inc_by=16)
        sc.op("sync", lambda e: e.dma_start(self.ecr_sb[:], self.ecr_d.ap()), inc=self.s_rb, inc_by=16)
        sc.op("sync", lambda e: e.dma_start(self.thr_sb[:], self.thr_d.ap()), inc=self.s_rb, inc_by=16)
        sc.op("sync", lambda e: e.dma_start(
            self.idx_d.tensor[:].rearrange("(p w) one -> p (w one)", p=P),
            self.sent_d.ap()), inc=self.s_rb, inc_by=16)
        self.sent_mark = cst = self.s_rb.n
        kmarks = []
        for tt in range(c.TT):
            kmarks.append(sc.op("vector", lambda e, tt=tt: e.tensor_scalar(
                kv[:, tt, :], self.w_all[:, tt * c.E:(tt + 1) * c.E], 0.0, None, ALU.is_gt),
                waits=[(self.s_l2, self.w_done), (self.s_vc, self.s_vc.n)], inc=self.s_kp))
        tmm = 0
        for tt in range(c.TT):
            tmm = sc.op("tensor", lambda e, tt=tt: e.matmul(
                self.pw[0:1, tt * c.E:(tt + 1) * c.E], self.ones_cf[:], kv[:, tt, :],
                start=(tt == 0), stop=(tt == c.TT - 1)),
                waits=[(self.s_kp, kmarks[tt])], inc=self.s_ps)
        tev = sc.op("scalar", lambda e: e.copy(self.trow[:], self.pw[0:1, 0:c.TT * c.E]),
                    waits=[(self.s_ps, tmm), (self.s_sc, self.s_sc.n)], inc=self.s_ps)
        t1 = sc.op("sync", lambda e: e.dma_start(self.tot_d, self.trow[:]),
                   waits=[(self.s_ps, tev)], inc=self.s_bc, inc_by=16)
        t2 = sc.op("sync", lambda e: e.dma_start(self.tot16[:], self.tot_d),
                   waits=[(self.s_bc, t1)], inc=self.s_bc, inc_by=16)
        omm = sc.op("tensor", lambda e: e.matmul(
            self.pw[0:c.TT, 256:256 + c.E], self.s01f_sb[0:c.TT, 0:c.TT], self.tot16[:],
            start=True, stop=False),
            waits=[(self.s_bc, t2), (self.s_rb, cst), (self.s_ps, self.s_ps.n)],
            inc=self.s_ps)
        omm = sc.op("tensor", lambda e: e.matmul(
            self.pw[0:c.TT, 256:256 + c.E], self.ones_rf[0:1, 0:c.TT], self.ecr_sb[:],
            start=False, stop=True),
            waits=[(self.s_rb, cst)], inc=self.s_ps)
        oev = sc.op("scalar", lambda e: e.copy(self.tot16[:], self.pw[0:c.TT, 256:256 + c.E]),
                    waits=[(self.s_ps, omm), (self.s_sc, self.s_sc.n)], inc=self.s_ps)
        o1 = sc.op("sync", lambda e: e.dma_start(self.off_d, self.tot16[:]),
                   waits=[(self.s_ps, oev)], inc=self.s_bc, inc_by=16)
        o2 = sc.op("sync", lambda e: e.dma_start(self.orow[:], self.off_d),
                   waits=[(self.s_bc, o1)], inc=self.s_bc, inc_by=16)
        wrt_war = [0, 0]
        pd_rw = [0, 0]
        pcast_marks = []
        for tt in range(c.TT):
            bank = self.pd[tt % 2]
            cmm = sc.op("tensor", lambda e, tt=tt, bank=bank: e.matmul(
                bank[:, 0:c.E], self.m01f_sb[:], kv[:, tt, :],
                start=True, stop=False),
                waits=[(self.s_kp, kmarks[tt]), (self.s_bc, o2),
                       (self.s_kp, pd_rw[tt % 2])],
                inc=self.s_ps)
            cmm = sc.op("tensor", lambda e, tt=tt, bank=bank: e.matmul(
                bank[:, 0:c.E], self.ones_rf[:], self.orow[0:1, tt * c.E:(tt + 1) * c.E],
                start=False, stop=True),
                waits=[], inc=self.s_ps)
            px = sc.op("vector", lambda e, tt=tt, bank=bank: e.tensor_sub(
                self.rksc[:, 0:c.E], bank[:, 0:c.E], kv[:, tt, :]),
                waits=[(self.s_ps, cmm), (self.s_vc, self.s_vc.n),
                       (self.s_kp, self.s_kp.n)], inc=self.s_kp)
            pd_rw[tt % 2] = px
            ws = tt % 2
            wv2 = sc.op("vector", lambda e, tt=tt, ws=ws: e.tensor_copy(
                self.wrtb[:, ws * c.E:(ws + 1) * c.E],
                self.w_all[:, tt * c.E:(tt + 1) * c.E]),
                waits=[([self.s_wa, self.s_wb][ws], wrt_war[ws]),
                       (self.s_vc, self.s_vc.n)], inc=self.s_vc)
            wrt_war[ws] = sc.op("sync", lambda e, tt=tt, ws=ws: e.dma_start(
                self.wr_d[tt * P:(tt + 1) * P, :],
                self.wrtb[:, ws * c.E:(ws + 1) * c.E]),
                waits=[(self.s_vc, wv2)], inc=[self.s_wa, self.s_wb][ws], inc_by=16)
            p1 = sc.op("vector", lambda e, tt=tt: e.tensor_scalar(
                self.rksc[:, a:a + c.E] if False else self.rksc[:, 1 * c.E:2 * c.E],
                kv[:, tt, :], -BIG, BIG, ALU.mult, ALU.add),
                waits=[(self.s_vc, self.s_vc.n)], inc=self.s_vc)
            p2 = sc.op("vector", lambda e: e.tensor_add(
                self.rksc[:, 0:c.E], self.rksc[:, 0:c.E], self.rksc[:, 1 * c.E:2 * c.E]),
                waits=[(self.s_vc, p1)], inc=self.s_vc)
            ov1 = sc.op("vector", lambda e: e.tensor_tensor(
                self.rksc[:, 1 * c.E:2 * c.E], self.rksc[:, 0:c.E], self.thr_sb[:],
                ALU.is_ge),
                waits=[(self.s_vc, p2), (self.s_rb, self.sent_mark)], inc=self.s_vc)
            p3 = sc.op("vector", lambda e: e.scalar_tensor_tensor(
                self.rksc[:, 0:c.E], self.rksc[:, 1 * c.E:2 * c.E], BIG,
                self.rksc[:, 0:c.E], ALU.mult, ALU.add),
                waits=[(self.s_vc, ov1)], inc=self.s_vc)
            pcast = sc.op("vector", lambda e, tt=tt: e.tensor_copy(
                self.peib[:, tt * c.E:(tt + 1) * c.E], self.rksc[:, 0:c.E]),
                waits=[(self.s_vc, p3), (self.s_vc, self.s_vc.n)],
                inc=self.s_kp)
            pcast_marks.append(pcast)
        self.sx_marks = [None] * c.E
        sx_prev = [0, 0]

        def emit_sx(ei):
            sxs = [self.s_sx0, self.s_sx1][ei % 2]
            for tt in range(c.TT):
                sc.op("gpsimd", lambda e, tt=tt, ei=ei: e.indirect_dma_start(
                    self.idx_d,
                    bass.IndirectOffsetOnAxis(self.peib[:, tt * c.E + ei: tt * c.E + ei + 1], 0),
                    self.iota_sb[:, tt:tt + 1], None,
                    bounds_check=self.bnd_ix, oob_is_err=False),
                    waits=[(self.s_kp, pcast_marks[tt]), (self.s_rb, self.sent_mark)] +
                          ([(sxs, sx_prev[ei % 2])] if tt == 0 else []),
                    inc=sxs, inc_by=16)
            self.sx_marks[ei] = (sxs, sxs.n)
            sx_prev[ei % 2] = sxs.n

        for ei in range(min(2, c.E)):
            emit_sx(ei)
        self.emit_sx = emit_sx
        self.pending_sx = list(range(2, c.E))
        self.wr_done = (self.s_wa.n, self.s_wb.n)

    def phase_moe_sparse(self):
        sc, c = self.sc, self.c
        xnv = self.xngT[:].rearrange("p (cc t) -> p cc t", cc=c.HC)
        actv = self.acts[:].rearrange("p (ic t) -> p ic t", ic=c.IC)
        gubv = self.gub[:].rearrange("p (s cc w) -> p s cc w", s=4, cc=c.HC)
        dwv = self.dwn[:].rearrange("p (ic d) -> p ic d", ic=c.IC)
        gsv = self.gsb[:].rearrange("p (s w) -> p s w", s=2)
        xsv = self.xstg[:].rearrange("p (s w) -> p s w", s=2)
        ssv = self.sst[:].rearrange("p (s w) -> p s w", s=2)
        wgv = self.wrg[:].rearrange("p (s ct w) -> p s ct w", s=2, ct=c.CT)

        self.join_all([(self.s_l2, self.w_done), (self.s_sc, self.s_sc.n),
                       (self.s_tr, self.s_tr.n), (self.s_te, self.s_te.n),
                       (self.s_vc, self.s_vc.n), (self.s_ce, self.s_ce.n),
                       (self.s_x2, self.s_x2.n), (self.s_x2st, self.attn_x2),
                       (self.s_xbst, self.attn_xb), (self.s_tm, self.s_tm.n),
                       (self.s_qs0, self.s_qs0.n), (self.s_qs1, self.s_qs1.n),
                       (self.s_x0, self.s_x0.n), (self.s_x1, self.s_x1.n),
                       (self.s_wol, self.wo_ld)])
        self.phase_router_build()

        sc.op("gpsimd", lambda e: e.memset(self.wrg[:], 0.0), inc=self.s_ms)

        gu_war = [0, 0]
        pg_war = [0, 0]
        pu_war = [0, 0]
        pd_war = [0, 0]
        gs_war = [0, 0]
        xstg_war = [0, 0]
        sst_war = [0, 0]
        idx_war = [[], []]
        wrg_war = [0, 0]
        act_rd = 0
        dwn_rd = 0
        prep_done = {}
        self.y_marks = {}

        def prep(ei, agate):
            es = ei % 2
            il = 0
            for ct in range(c.CT):
                il = sc.op("gpsimd", lambda e, ei=ei, es=es, ct=ct: e.dma_start(
                    self.idxc[:, es * c.CT + ct: es * c.CT + ct + 1],
                    self.idx_d.tensor[ei * c.CAP + ct * P: ei * c.CAP + (ct + 1) * P, :]),
                    waits=([self.sx_marks[ei]] + idx_war[es] if ct == 0 else []),
                    inc=self.s_ix, inc_by=16)
            tps = []
            gx_last = 0
            for ct in range(c.CT):
                xs = (ei * c.CT + ct) % 2
                sgx = [self.s_gx0, self.s_gx1][xs]
                gx = sc.op("gpsimd", lambda e, ct=ct, es=es, xs=xs: e.indirect_dma_start(
                    xsv[:, xs, :], None, self.xntm_d,
                    bass.IndirectOffsetOnAxis(self.idxc[:, es * c.CT + ct: es * c.CT + ct + 1], 0),
                    bounds_check=self.bnd_t, oob_is_err=False),
                    waits=[(self.s_ix, il), (self.s_tp, xstg_war[xs])],
                    inc=sgx, inc_by=16)
                sgw = [self.s_gw0, self.s_gw1][es]
                wg = sc.op("gpsimd", lambda e, ct=ct, es=es: e.indirect_dma_start(
                    wgv[:, es, ct, :], None, self.wr_d,
                    bass.IndirectOffsetOnAxis(self.idxc[:, es * c.CT + ct: es * c.CT + ct + 1], 0),
                    bounds_check=self.bnd_t, oob_is_err=False),
                    waits=[(self.s_wa, self.wr_done[0]), (self.s_wb, self.wr_done[1]),
                           (self.s_de, wrg_war[es])],
                    inc=sgw, inc_by=16)
                gx_last = wg
                mmv = 0
                for hc in range(c.HC):
                    tb = self.tp_grp % 2
                    mmv = sc.op("tensor", lambda e, hc=hc, xs=xs, tb=tb: e.transpose(
                        self.ptp[tb][:, (hc % 4) * P:(hc % 4 + 1) * P],
                        xsv[:, xs, hc * P:(hc + 1) * P], self.ident[:]),
                        waits=(([(sgx, gx), agate] if hc == 0 else []) +
                               ([(self.s_tp, self.tp_bank_war[tb])] if hc % 4 == 0 else [])),
                        inc=self.s_ps)
                    if hc % 4 == 3 or hc == c.HC - 1:
                        nb = hc % 4 + 1
                        self.tp_bank_war[tb] = sc.op(
                            "scalar", lambda e, hc=hc, ct=ct, nb=nb, tb=tb: e.copy(
                                xnv[:, hc - nb + 1:hc + 1, ct * P:(ct + 1) * P],
                                self.ptp[tb][:, 0:nb * P].rearrange("p (f w) -> p f w", f=nb)),
                            waits=[(self.s_ps, mmv), (self.s_sc, self.s_sc.n)],
                            inc=self.s_tp)
                        self.tp_grp += 1
                xstg_war[xs] = self.s_tp.n
            prep_done[ei] = (self.s_tp.n, il, ([self.s_gw0, self.s_gw1][es], gx_last))

        prep(0, self.sx_marks[0])
        ib_list = chunks(c.DI, c.GIB)
        emit_sx_rest_pending = True
        gl_marks = {}

        def gu_load(lei, bi, wait_mark):
            i0, iw = ib_list[bi]
            slot = bi % 2
            sg = [self.s_g0, self.s_g1][slot]
            sc.op("gpsimd", lambda e, lei=lei, i0=i0, iw=iw, slot=slot: e.dma_start(
                gubv[:, 2 * slot, :, 0:iw],
                self.gate_d[lei][:, i0:i0 + iw].rearrange("(cc p) w -> p cc w", p=P)),
                waits=[(self.s_gu, wait_mark)], inc=sg, inc_by=16)
            gl = sc.op("gpsimd", lambda e, lei=lei, i0=i0, iw=iw, slot=slot: e.dma_start(
                gubv[:, 2 * slot + 1, :, 0:iw],
                self.up_d[lei][:, i0:i0 + iw].rearrange("(cc p) w -> p cc w", p=P)),
                waits=[], inc=sg, inc_by=16)
            gl_marks[(lei, bi)] = (sg, gl)

        for bi in range(min(2, len(ib_list))):
            gu_load(0, bi, 0)

        def drain_sx(n):
            while self.pending_sx and n > 0:
                self.emit_sx(self.pending_sx.pop(0))
                n -= 1
        prev_ss = (0, 0)
        for ei in range(c.E):
            es = ei % 2
            tp_mark, il_mark, gw_mark = prep_done[ei]
            dl_next = [0]
            dl_last = [0]

            def down_load_some(k):
                while dl_next[0] < c.IC and k > 0:
                    ic = dl_next[0]
                    dl_next[0] += 1
                    k -= 1
                    dl_last[0] = sc.op("gpsimd", lambda e, ei=ei, ic=ic: e.dma_start(
                        dwv[:, ic, :], self.down_d[ei][ic * P:(ic + 1) * P, :]),
                        waits=[(self.s_dn, dwn_rd)], inc=self.s_dw, inc_by=16)

            pi = 0
            for bi, (i0, iw) in enumerate(ib_list):
                slot = bi % 2
                for (ic0, icw) in chunks(iw, P):
                    ic = (i0 + ic0) // P
                    for (t0, tw) in chunks(c.CAP, 512):
                        bank = pi % 2
                        gslot = pi % 2
                        pi += 1
                        gm = um = 0
                        for hc in range(c.HC):
                            gm = sc.op("tensor", lambda e, slot=slot, hc=hc, ic0=ic0, t0=t0, tw=tw, bank=bank: e.matmul(
                                self.pg[bank][:, 0:tw],
                                gubv[:, 2 * slot, hc, ic0:ic0 + P],
                                xnv[:, hc, t0:t0 + tw],
                                start=(hc == 0), stop=(hc == c.HC - 1)),
                                waits=([gl_marks[(ei, bi)], (self.s_tp, tp_mark),
                                        (self.s_si, pg_war[bank]),
                                        (self.s_de, pd_war[bank])] if hc == 0 else []),
                                inc=self.s_gu)
                        for hc in range(c.HC):
                            um = sc.op("tensor", lambda e, slot=slot, hc=hc, ic0=ic0, t0=t0, tw=tw, bank=bank: e.matmul(
                                self.pu[bank][:, 0:tw],
                                gubv[:, 2 * slot + 1, hc, ic0:ic0 + P],
                                xnv[:, hc, t0:t0 + tw],
                                start=(hc == 0), stop=(hc == c.HC - 1)),
                                waits=([(self.s_ae, pu_war[bank])] if hc == 0 else []),
                                inc=self.s_gu)
                        sv = sc.op("scalar", lambda e, bank=bank, tw=tw, gslot=gslot: e.activation(
                            gsv[:, gslot, 0:tw], self.pg[bank][:, 0:tw], AF.Silu),
                            waits=[(self.s_gu, gm), (self.s_ae, gs_war[gslot]),
                                   (self.s_sc, self.s_sc.n)],
                            inc=self.s_si)
                        pg_war[bank] = sv
                        av = sc.op("vector", lambda e, bank=bank, tw=tw, gslot=gslot, ic=ic, t0=t0: e.tensor_mul(
                            actv[:, ic, t0:t0 + tw], gsv[:, gslot, 0:tw],
                            self.pu[bank][:, 0:tw]),
                            waits=[(self.s_gu, um), (self.s_si, sv),
                                   (self.s_dn, act_rd), (self.s_vc, self.s_vc.n)],
                            inc=self.s_ae)
                        pu_war[bank] = av
                        gs_war[gslot] = av
                gu_war[slot] = self.s_gu.n
                nbi = bi + 2
                if nbi < len(ib_list):
                    gu_load(ei, nbi, gu_war[slot])
                elif ei + 1 < c.E:
                    gu_load(ei + 1, nbi - len(ib_list), gu_war[slot])
                if bi >= 1:
                    down_load_some(2)
                drain_sx(1)
            down_load_some(c.IC)
            dl_all = dl_last[0]
            act_done = self.s_ae.n
            a_end = self.s_gu.n
            if ei + 1 < c.E:
                prep(ei + 1, (self.s_gu, a_end))
            bank_i = 0
            DHW = min(1024, c.D)
            for stt_i in range(c.CT):
                for dh, (d0, dhw) in enumerate(chunks(c.D, DHW)):
                    dmm = 0
                    for (dc0, dcw) in chunks(dhw, 512):
                        bank = bank_i % 2
                        bank_i += 1
                        dm = 0
                        for ic in range(c.IC):
                            w = ([(self.s_ae, act_done), (self.s_de, pd_war[bank]),
                                  (self.s_dw, dl_all)] if ic == 0 else [])
                            dm = sc.op("tensor", lambda e, ic=ic, stt_i=stt_i, d0=d0, dc0=dc0, dcw=dcw, bank=bank: e.matmul(
                                self.pd[bank][:, 0:dcw],
                                actv[:, ic, stt_i * P:(stt_i + 1) * P],
                                dwv[:, ic, d0 + dc0:d0 + dc0 + dcw],
                                start=(ic == 0), stop=(ic == c.IC - 1)),
                                waits=w, inc=self.s_dn)
                        ss = stt_i % 2
                        pd_war[bank] = sc.op("vector", lambda e, ss=ss, dc0=dc0, dcw=dcw, bank=bank, es=es, stt_i=stt_i, ei=ei: e.tensor_scalar(
                            ssv[:, ss, dc0:dc0 + dcw], self.pd[bank][:, 0:dcw],
                            wgv[:, es, stt_i, ei:ei + 1], None, ALU.mult),
                            waits=[(self.s_dn, dm), ([self.s_ss0, self.s_ss1][ss], sst_war[ss]),
                                   gw_mark, (self.s_vc, self.s_vc.n)],
                            inc=self.s_de)
                        dmm = self.s_de.n
                    ss = stt_i % 2
                    first_sc = stt_i == 0 and dh == 0
                    sst_war[ss] = sc.op("gpsimd", lambda e, ss=ss, d0=d0, dhw=dhw, es=es, stt_i=stt_i: e.indirect_dma_start(
                        self.y_d,
                        bass.IndirectOffsetOnAxis(self.idxc[:, es * c.CT + stt_i: es * c.CT + stt_i + 1], 0),
                        ssv[:, ss, 0:dhw], None,
                        element_offset=d0,
                        bounds_check=self.bnd_t, oob_is_err=False,
                        compute_op=ALU.add),
                        waits=[(self.s_de, dmm), (self.s_ix, il_mark)] +
                              ([(self.s_ss0, prev_ss[0]), (self.s_ss1, prev_ss[1])]
                               if first_sc else []),
                        inc=[self.s_ss0, self.s_ss1][ss], inc_by=16)
            drain_sx(c.E)
            act_rd = self.s_dn.n
            dwn_rd = self.s_dn.n
            wrg_war[es] = self.s_de.n
            prev_ss = (self.s_ss0.n, self.s_ss1.n)
            idx_war[es] = [(self.s_ss0, prev_ss[0]), (self.s_ss1, prev_ss[1])]
        self.moe_sst = (self.s_ss0.n, self.s_ss1.n)
        self.moe_gu = self.s_gu.n


def build(cfg: Cfg, stop_after="moe", qt_limit=None, head_limit=None, no_stage4=False):
    return Builder(cfg, stop_after, qt_limit, head_limit, no_stage4).build()


_CFG = Cfg(T=2048, D=2048, NH=16, NG=4, E=16, DI=1792, N_HALVES=2, WQB=256,
           GIB=128, CAP=1408)
_NCORES = 8


def kernel(L0_attn_x, L0_attn_Wq, L0_attn_Wk, L0_attn_Wv, L0_attn_Wo, L0_attn_bo,
           L0_ffn_router_w, L0_ffn_expert_bias, L0_ffn_gate_proj, L0_ffn_up_proj,
           L0_ffn_down_proj):
    from concourse.bass_utils import run_bass_kernel_spmd
    c = _CFG
    nc = build(c)
    x = np.ascontiguousarray(np.asarray(L0_attn_x, dtype=np.float32))
    shared = {
        "wq": np.ascontiguousarray(np.asarray(L0_attn_Wq, dtype=np.float32)),
        "wk": np.ascontiguousarray(np.asarray(L0_attn_Wk, dtype=np.float32)),
        "wv": np.ascontiguousarray(np.asarray(L0_attn_Wv, dtype=np.float32)),
        "wo": np.ascontiguousarray(np.asarray(L0_attn_Wo, dtype=np.float32)),
        "bo": np.ascontiguousarray(np.asarray(L0_attn_bo, dtype=np.float32).reshape(1, c.D)),
        "rw": np.ascontiguousarray(np.asarray(L0_ffn_router_w, dtype=np.float32)),
        "eb": np.ascontiguousarray(np.asarray(L0_ffn_expert_bias, dtype=np.float32).reshape(1, c.E)),
        "gate": np.ascontiguousarray(np.asarray(L0_ffn_gate_proj, dtype=np.float32)),
        "up": np.ascontiguousarray(np.asarray(L0_ffn_up_proj, dtype=np.float32)),
        "down": np.ascontiguousarray(np.asarray(L0_ffn_down_proj, dtype=np.float32)),
    }
    in_maps = [dict(shared, x=np.ascontiguousarray(x[i])) for i in range(_NCORES)]
    res = run_bass_kernel_spmd(nc, in_maps, core_ids=list(range(_NCORES)))
    return np.stack([r["y"] for r in res.results]).astype(np.float32)


# version 13 (final)
# speedup vs baseline: 1.1238x; 1.0238x over previous
from dataclasses import dataclass
from contextlib import ExitStack
import numpy as np
import ml_dtypes

import concourse.bass as bass
import concourse.mybir as mybir

F32 = mybir.dt.float32
BF16 = mybir.dt.bfloat16
AF = mybir.ActivationFunctionType
ALU = mybir.AluOpType
AX = mybir.AxisListType

P = 128


@dataclass
class Cfg:
    T: int = 2048
    D: int = 2048
    NH: int = 16
    NG: int = 4
    E: int = 16
    DI: int = 1792
    N_HALVES: int = 2
    WQB: int = 256
    GIB: int = 128
    CAP: int = 0

    @property
    def TT(self): return self.T // P
    @property
    def HC(self): return self.D // P
    @property
    def GS(self): return self.NH // self.NG
    @property
    def DKV(self): return self.NG * P
    @property
    def IC(self): return self.DI // P
    @property
    def TH(self): return self.T // self.N_HALVES
    @property
    def TTH(self): return self.TH // P
    @property
    def CT(self): return self.CAP // P


def chunks(total, step):
    o, out = 0, []
    while o < total:
        out.append((o, min(step, total - o)))
        o += step
    return out


class Sem:
    __slots__ = ("name", "n", "handle")

    def __init__(self, name, handle):
        self.name, self.n, self.handle = name, 0, handle


class Sched:
    ENGINES = ("tensor", "vector", "scalar", "sync", "gpsimd")

    def __init__(self, nc, ctx):
        self.nc, self.ctx = nc, ctx
        self.ops = {e: [] for e in self.ENGINES}

    def sem(self, name):
        return Sem(name, self.ctx.enter_context(self.nc.semaphore(name)))

    def op(self, eng, fn, waits=(), inc=None, inc_by=1, defer=None):
        if inc is not None:
            inc.n += inc_by
        rec = (tuple((s, v) for (s, v) in waits if v > 0), fn, inc, inc_by)
        (self.ops[eng] if defer is None else defer).append(rec)
        return inc.n if inc else 0

    def splice(self, eng, deferred, k=10 ** 9):
        for _ in range(min(k, len(deferred))):
            self.ops[eng].append(deferred.pop(0))

    def playback(self, block):
        def make(name):
            def run(eng):
                last = {}
                for waits, fn, inc, inc_by in self.ops[name]:
                    for (s, v) in waits:
                        if last.get(id(s), -1) < v:
                            eng.wait_ge(s.handle, v)
                            last[id(s)] = v
                    ins = fn(eng)
                    if inc is not None:
                        ins.then_inc(inc.handle, inc_by)
            return run
        block.tensor(make("tensor"))
        block.vector(make("vector"))
        block.scalar(make("scalar"))
        block.sync(make("sync"))
        block.gpsimd(make("gpsimd"))


class Builder:
    def __init__(self, cfg: Cfg, stop_after="moe", qt_limit=None, head_limit=None, no_stage4=False):
        self.c = cfg
        self.stop_after = stop_after
        self.qt_limit = qt_limit
        self.head_limit = head_limit
        self.no_stage4 = no_stage4
        self.nc = bass.Bass("TRN2", target_bir_lowering=False)

    def build(self):
        c, nc = self.c, self.nc
        self.x_d = nc.dram_tensor("x", [c.T, c.D], F32, kind="ExternalInput").ap()
        self.wq_d = nc.dram_tensor("wq", [c.D, c.D], F32, kind="ExternalInput").ap()
        self.wk_d = nc.dram_tensor("wk", [c.D, c.DKV], F32, kind="ExternalInput").ap()
        self.wv_d = nc.dram_tensor("wv", [c.D, c.DKV], F32, kind="ExternalInput").ap()
        self.wo_d = nc.dram_tensor("wo", [c.D, c.D], F32, kind="ExternalInput").ap()
        self.bo_d = nc.dram_tensor("bo", [1, c.D], F32, kind="ExternalInput").ap()
        self.rw_d = nc.dram_tensor("rw", [c.E, c.D], F32, kind="ExternalInput").ap()
        self.eb_d = nc.dram_tensor("eb", [1, c.E], F32, kind="ExternalInput").ap()
        self.gate_d = nc.dram_tensor("gate", [c.E, c.D, c.DI], F32, kind="ExternalInput").ap()
        self.up_d = nc.dram_tensor("up", [c.E, c.D, c.DI], F32, kind="ExternalInput").ap()
        self.down_d = nc.dram_tensor("down", [c.E, c.DI, c.D], F32, kind="ExternalInput").ap()
        self.y_d = nc.dram_tensor("y", [c.T, c.D], F32, kind="ExternalOutput").ap()
        self.x2_d = nc.dram_tensor("x2s", [c.T, c.D], F32, kind="Internal").ap()
        self.qt_d = nc.dram_tensor("qts", [c.D, c.T], BF16, kind="Internal").ap()
        self.xn2_d = nc.dram_tensor("xn2s", [c.D, c.T], BF16, kind="Internal").ap()
        if c.CAP:
            self.xntm_d = nc.dram_tensor("xntm", [c.T, c.D], BF16, kind="Internal").ap()
            self.wr_d = nc.dram_tensor("wrd", [c.T, c.E], F32, kind="Internal").ap()
            self.idx_d = nc.dram_tensor("idxd", [c.E * c.CAP, 1], mybir.dt.int32, kind="Internal").ap()
            self.tot_d = nc.dram_tensor("totd", [c.TT, c.E], F32, kind="Internal").ap()
            self.off_d = nc.dram_tensor("offd", [1, c.TT * c.E], F32, kind="Internal").ap()

        self.ident_d = nc.inline_tensor(np.eye(P, dtype=ml_dtypes.bfloat16), "identc")
        self.identf_d = nc.inline_tensor(np.eye(P, dtype=np.float32), "identfc")
        m01 = np.triu(np.ones((P, P))).astype(ml_dtypes.bfloat16)
        self.mask_d = nc.inline_tensor(m01, "mask01c")
        mA = np.where(np.triu(np.ones((P, P), bool), 1), np.float32(-1e30), np.float32(0.0))
        self.maskA_d = nc.inline_tensor(mA.astype(np.float32), "maskAc")
        if self.c.CAP:
            self.m01f_d = nc.inline_tensor(np.triu(np.ones((P, P), np.float32)), "m01fc")
            self.s01f_d = nc.inline_tensor(np.triu(np.ones((P, P), np.float32), 1), "s01fc")
            iota = (np.arange(P)[:, None] + P * np.arange(self.c.TT)[None, :]).astype(np.int32)
            self.iota_d = nc.inline_tensor(iota, "iotac")
            ecr = (np.arange(self.c.E, dtype=np.float32) * self.c.CAP)[None, :]
            self.ecr_d = nc.inline_tensor(ecr, "ecrc")
            self.sent_d = nc.inline_tensor(np.full((P, self.c.E * self.c.CAP // P), self.c.T, np.int32), "sentc")
            thr = np.tile(((np.arange(self.c.E) + 1) * self.c.CAP).astype(np.float32), (P, 1))
            self.thr_d = nc.inline_tensor(thr, "thrc")

        self.ctx = ExitStack()
        with self.ctx:
            self.sc = Sched(nc, self.ctx)
            self.mk_sems()
            self.scope0()
            kv_guards = self.enter_kv_scope()
            ln_guards = self.enter_ln_scope()
            self.phase_consts()
            self.phase_ln1()
            if self.stop_after == "ln":
                self.sc.op("sync", lambda e: e.nop(),
                           waits=[(self.s_xn1, self.ln1_done), (self.s_mm, self.s_mm.n)])
            else:
                self.phase_qkv()
            for g in reversed(ln_guards):
                g.__exit__(None, None, None)
            if self.stop_after == "qkv":
                self.sc.op("sync", lambda e: e.nop(),
                           waits=[(self.s_ev, self.qkv_ev_done),
                                  (self.s_q0, self.qst_war[0]), (self.s_q1, self.qst_war[1])])
            if self.stop_after in ("attn", "moe"):
                at_guards = self.enter_attn_scope()
                self.phase_attention()
                for g in reversed(at_guards):
                    g.__exit__(None, None, None)
            for g in reversed(kv_guards):
                g.__exit__(None, None, None)
            if self.stop_after == "attn":
                self.sc.op("sync", lambda e: e.nop(),
                           waits=[(self.s_l2, self.w_done), (self.s_x2st, self.attn_x2),
                                  (self.s_xbst, self.attn_xb), (self.s_sc, self.s_sc.n),
                                  (self.s_tr, self.s_tr.n), (self.s_vc, self.s_vc.n)])
            if self.stop_after == "moe":
                if self.c.CAP:
                    moe_guards = self.enter_moe_scope_sparse()
                    self.phase_moe_sparse()
                    for g in reversed(moe_guards):
                        g.__exit__(None, None, None)
                    self.sc.op("sync", lambda e: e.nop(),
                               waits=[(self.s_ss0, self.moe_sst[0]),
                                      (self.s_ss1, self.moe_sst[1]),
                                      (self.s_x2st, self.attn_x2)])
                else:
                    moe_guards = self.enter_moe_scope()
                    self.phase_moe()
                    for g in reversed(moe_guards):
                        g.__exit__(None, None, None)
                    self.sc.op("sync", lambda e: e.nop(),
                               waits=[(self.s_yst, self.s_yst.n)])
            block = self.ctx.enter_context(nc.Block())
            self.sc.playback(block)
        return nc

    def mk_sems(self):
        s = self.sc
        for name in ("s_x s_xn1 s_wld s_mm s_ev s_vc s_sc s_p1 s_rm s_nr s_p2 "
                     "s_pe s_pb s_cs s_ri s_cx s_ce s_wo s_x2 s_l2 s_tr s_te "
                     "s_st s_qs s_wg s_gu s_ae s_si s_dn s_de s_yi s_xl s_ms "
                     "s_cst s_bo s_x0 s_x1 s_wol s_w0 s_w1 s_q0 s_q1 s_qs0 s_qs1 "
                     "s_tm s_kp s_rb s_sx s_ix s_gx s_tp s_sst s_ycb s_ps "
                     "s_bc s_wa s_wb s_yc0 s_yc1 s_gw0 s_gw1 s_gx0 s_gx1 s_gx2 s_gx3 s_ss0 s_ss1 s_sx0 s_sx1 s_yo0 s_yo1 "
                     "s_x2st s_xbst s_yst s_g0 s_g1 s_dw s_cw").split():
            setattr(self, name, s.sem(name))

    def scope0(self):
        nc, c, ctx = self.nc, self.c, self.ctx
        SB = lambda n, s, d: ctx.enter_context(nc.sbuf_tensor(n, s, d))
        self.w_all = SB("w_allsb", [P, c.TT * c.E], F32)
        self.ident = SB("identsb", [P, P], BF16)
        self.pw = ctx.enter_context(nc.psum_tensor("pw", [P, 512], F32))

    def phase_consts(self):
        sc, c = self.sc, self.c
        sc.op("sync", lambda e: e.dma_start(self.ident[:], self.ident_d.ap()), inc=self.s_cst, inc_by=16)
        sc.op("sync", lambda e: e.dma_start(self.identf[:], self.identf_d.ap()), inc=self.s_cst, inc_by=16)
        sc.op("sync", lambda e: e.dma_start(self.maskT[:], self.mask_d.ap()), inc=self.s_cst, inc_by=16)
        sc.op("sync", lambda e: e.dma_start(self.maskA[:], self.maskA_d.ap()), inc=self.s_cst, inc_by=16)
        sc.op("sync", lambda e: e.dma_start(self.eb_f[:], self.eb_d), inc=self.s_cst, inc_by=16)
        sc.op("sync", lambda e: e.dma_start(self.rw_s[:], self.rw_d), inc=self.s_cst, inc_by=16)
        self.w_const = self.s_cst.n
        sc.op("gpsimd", lambda e: e.dma_start(self.bo_bf[:], self.bo_d), inc=self.s_bo, inc_by=16)
        sc.op("gpsimd", lambda e: e.memset(self.ones_r1[:], 1.0))
        sc.op("gpsimd", lambda e: e.memset(self.ones_c[:], 1.0))
        sc.op("gpsimd", lambda e: e.memset(self.ones_cf[:], 1.0))
        sc.op("gpsimd", lambda e: e.memset(self.ones_rf[:], 1.0))
        sc.op("gpsimd", lambda e: e.memset(self.eps_c[:], 1e-5), inc=self.s_ms, inc_by=1)
        self.w_ones = self.s_ms.n
        for hc in range(c.HC):
            v = sc.op("tensor", lambda e, hc=hc: e.transpose(
                self.pw[:, 0:c.E], self.rw_s[:, hc * P:(hc + 1) * P],
                self.identf[0:c.E, 0:c.E]),
                waits=[(self.s_cst, self.w_const), (self.s_ms, self.w_ones),
                       (self.s_ev, self.s_ev.n)],
                inc=self.s_mm)
            sc.op("scalar", lambda e, hc=hc: e.copy(
                self.rwT[:, hc * c.E:(hc + 1) * c.E], self.pw[:, 0:c.E]),
                waits=[(self.s_mm, v)], inc=self.s_ev)
        self.rwT_done = self.s_ev.n

    def enter_kv_scope(self):
        nc, c = self.nc, self.c
        gs = [nc.sbuf_tensor("kT", [P, c.NG * c.T], BF16),
              nc.sbuf_tensor("vN", [P, c.TT * c.DKV], BF16),
              nc.sbuf_tensor("xring", [P, 2 * c.D], F32),
              nc.sbuf_tensor("stat", [P, 48], F32),
              nc.sbuf_tensor("xnorm", [P, c.D], F32),
              nc.sbuf_tensor("identfsb", [P, P], F32),
              nc.sbuf_tensor("masksb", [P, P], BF16),
              nc.sbuf_tensor("maskAsb", [P, P], F32),
              nc.sbuf_tensor("ones_r1", [1, P], BF16),
              nc.sbuf_tensor("ones_c", [P, 1], BF16),
              nc.sbuf_tensor("ones_cf", [P, 1], F32),
              nc.sbuf_tensor("ones_rf", [1, P], F32),
              nc.sbuf_tensor("eps_c", [P, 1], F32),
              nc.sbuf_tensor("bo_bf", [1, c.D], BF16),
              nc.sbuf_tensor("eb_f", [1, c.E], F32),
              nc.sbuf_tensor("rwT", [P, c.HC * c.E], F32)]
        (self.kT, self.vN, self.xring, self.stat, self.xnorm,
         self.identf, self.maskT, self.maskA, self.ones_r1, self.ones_c,
         self.ones_cf, self.ones_rf, self.eps_c, self.bo_bf, self.eb_f,
         self.rwT) = [g.__enter__() for g in gs]
        return gs

    def enter_ln_scope(self):
        nc, c = self.nc, self.c
        gs = [nc.sbuf_tensor("rw_s", [c.E, c.D], F32),
              nc.sbuf_tensor("xn1T", [P, c.HC * c.T], BF16),
              nc.sbuf_tensor("wqb", [P, 2 * c.HC * c.WQB], BF16),
              nc.sbuf_tensor("xnbf", [P, c.D], BF16),
              nc.sbuf_tensor("qstage", [P, 2 * 512], BF16),
              nc.psum_tensor("tpA", [P, 512], BF16),
              nc.psum_tensor("tpB", [P, 512], BF16),
              nc.psum_tensor("qk0", [P, 512], F32),
              nc.psum_tensor("qk1", [P, 512], F32),
              nc.psum_tensor("qk2", [P, 512], F32)]
        (self.rw_s, self.xn1T, self.wqb, self.xnbf, self.qstage,
         tpA, tpB, qk0, qk1, qk2) = [g.__enter__() for g in gs]
        self.tp = [tpA, tpB]
        self.qk = [self.pw, qk0, qk1, qk2]
        return gs

    def enter_attn_scope(self):
        nc, c = self.nc, self.c
        gs = [nc.sbuf_tensor("qs", [P, 2 * c.HC * P], BF16),
              nc.sbuf_tensor("probs", [P, 4 * c.T], BF16),
              nc.sbuf_tensor("mxs", [P, 4 * 8], F32),
              nc.sbuf_tensor("esc", [P, 6 * c.T], BF16),
              nc.sbuf_tensor("ssc", [P, 96], F32),
              nc.sbuf_tensor("ctxT", [P, 2 * c.HC * P], BF16),
              nc.sbuf_tensor("x2t", [P, c.D], F32),
              nc.sbuf_tensor("xn2f", [P, c.HC * P], F32),
              nc.sbuf_tensor("xn2b", [P, (1 if c.CAP else c.HC) * P], BF16),
              nc.sbuf_tensor("rc", [P, 64], F32),
              nc.sbuf_tensor("xtmb", [P, c.D], BF16),
              nc.sbuf_tensor("wo_sb", [P, c.HC * c.D], BF16),
              nc.psum_tensor("S1a", [P, 512], F32),
              nc.psum_tensor("S1b", [P, 512], F32),
              nc.psum_tensor("S2a", [P, 512], BF16),
              nc.psum_tensor("S2b", [P, 512], BF16),
              nc.psum_tensor("C", [P, 512], F32),
              nc.psum_tensor("DR", [P, 512], F32),
              nc.psum_tensor("LTL", [P, 512], F32)]
        (self.qs, self.probs, self.mxs, self.esc,
         self.ssc, self.ctxT, self.x2t, self.xn2f, self.xn2b, self.rc,
         self.xtmb, self.wo_sb, S1a, S1b, S2a, S2b, self.C, self.DR, self.LTL) = \
            [g.__enter__() for g in gs]
        self.S1 = [S1a, S1b]
        self.S2b = [S2a, S2b]
        c = self.c
        for hcb in range(0, c.HC, 4):
            w = min(4, c.HC - hcb)
            self.sc.op("gpsimd", lambda e, hcb=hcb, w=w: e.dma_start(
                self.wo_sb[:].rearrange("p (cc d) -> p cc d", cc=c.HC)[:, hcb:hcb + w, :],
                self.wo_d[hcb * P:(hcb + w) * P, :].rearrange("(cc p) d -> p cc d", p=P)),
                waits=[(self.s_mm, self.qkv_mm_done), (self.s_ev, self.qkv_ev_done),
                       (self.s_q0, self.qst_war[0]), (self.s_q1, self.qst_war[1])],
                inc=self.s_wol, inc_by=16)
        self.wo_ld = self.s_wol.n
        return gs

    def enter_moe_scope(self):
        nc, c = self.nc, self.c
        gub_w = 2 * 2 * c.HC * c.GIB
        gs = [nc.sbuf_tensor("xn2h", [P, c.HC * c.TH], BF16),
              nc.sbuf_tensor("acc", [P, c.TTH * c.D], F32),
              nc.sbuf_tensor("act", [P, c.IC * c.TH], BF16),
              nc.sbuf_tensor("gub", [P, gub_w], BF16),
              nc.sbuf_tensor("dwn", [P, c.IC * c.D], BF16),
              nc.sbuf_tensor("gssb", [P, 2 * 512], BF16),
              nc.psum_tensor("pg0", [P, 512], F32),
              nc.psum_tensor("pg1", [P, 512], F32),
              nc.psum_tensor("pu0", [P, 512], F32),
              nc.psum_tensor("pu1", [P, 512], F32),
              nc.psum_tensor("pd0", [P, 512], F32),
              nc.psum_tensor("pd1", [P, 512], F32),
              nc.psum_tensor("pd2", [P, 512], F32)]
        (self.xn2h, self.acc, self.act, self.gub, self.dwn, self.gsb,
         pg0, pg1, pu0, pu1, pd0, pd1, pd2) = [g.__enter__() for g in gs]
        self.pg = [pg0, pg1]
        self.pu = [pu0, pu1]
        self.pd = [pd0, pd1, pd2]
        return gs

    def phase_ln1(self):
        sc, c = self.sc, self.c
        xv = self.xn1T[:].rearrange("p (cc t) -> p cc t", cc=c.HC)
        ln_v = [0, 0]
        ln_s = [0, 0]
        tp_war = [0, 0]
        nbatch = (c.HC + 3) // 4
        for tt in range(c.TT):
            xo = (tt % 2) * c.D
            waits = ([(self.s_vc, ln_v[tt % 2]), (self.s_sc, ln_s[tt % 2])]
                     if tt >= 2 else [])
            sx = [self.s_x0, self.s_x1][tt % 2]
            xw = sc.op("sync", lambda e, tt=tt, xo=xo: e.dma_start(
                self.xring[:, xo:xo + c.D], self.x_d[tt * P:(tt + 1) * P, :]),
                waits=waits, inc=sx, inc_by=16)
            sc.op("vector", lambda e, xo=xo: e.reduce_sum(
                self.stat[:, 0:1], self.xring[:, xo:xo + c.D], axis=AX.X),
                waits=[(sx, xw), (self.s_vc, self.s_vc.n)], inc=self.s_vc)
            sq = sc.op("scalar", lambda e, xo=xo: e.activation(
                self.xnorm[:], self.xring[:, xo:xo + c.D], AF.Square),
                waits=[(sx, xw), (self.s_sc, self.s_sc.n)], inc=self.s_sc)
            sc.op("vector", lambda e: e.reduce_sum(self.stat[:, 1:2], self.xnorm[:], axis=AX.X),
                  waits=[(self.s_sc, sq), (self.s_vc, self.s_vc.n)], inc=self.s_vc)
            sc.op("vector", lambda e: e.tensor_scalar(
                self.stat[:, 2:3], self.stat[:, 0:1], 1.0 / c.D, None, ALU.mult),
                waits=[(self.s_vc, self.s_vc.n)], inc=self.s_vc)
            sc.op("vector", lambda e: e.tensor_scalar(
                self.stat[:, 3:4], self.stat[:, 2:3], self.stat[:, 2:3], None, ALU.mult),
                waits=[(self.s_vc, self.s_vc.n)], inc=self.s_vc)
            vv = sc.op("vector", lambda e: e.scalar_tensor_tensor(
                self.stat[:, 4:5], self.stat[:, 1:2], 1.0 / c.D, self.stat[:, 3:4],
                ALU.mult, ALU.subtract),
                waits=[(self.s_vc, self.s_vc.n)], inc=self.s_vc)
            sc.op("scalar", lambda e: e.activation(
                self.stat[:, 5:6], self.stat[:, 4:5], AF.Ln, bias=self.eps_c[:]),
                waits=[(self.s_vc, vv), (self.s_sc, self.s_sc.n)], inc=self.s_sc)
            rs = sc.op("scalar", lambda e: e.activation(
                self.stat[:, 6:7], self.stat[:, 5:6], AF.Exp, scale=-0.5),
                waits=[(self.s_sc, self.s_sc.n)], inc=self.s_sc)
            nv = sc.op("vector", lambda e, xo=xo: e.tensor_scalar(
                self.xnbf[:], self.xring[:, xo:xo + c.D], self.stat[:, 2:3],
                self.stat[:, 6:7], ALU.subtract, ALU.mult),
                waits=[(self.s_sc, rs), (self.s_vc, self.s_vc.n)], inc=self.s_vc)
            ln_v[tt % 2] = nv
            ln_s[tt % 2] = rs
            for bi in range(nbatch):
                b0 = bi * 4
                bn = min(4, c.HC - b0)
                bank = (tt * nbatch + bi) % 2
                mmv = 0
                for j in range(bn):
                    mmv = sc.op("tensor", lambda e, b0=b0, j=j, bank=bank: e.transpose(
                        self.tp[bank][:, j * P:(j + 1) * P],
                        self.xnbf[:, (b0 + j) * P:(b0 + j + 1) * P], self.ident[:]),
                        waits=[(self.s_vc, nv), (self.s_xn1, tp_war[bank])],
                        inc=self.s_mm)
                tp_war[bank] = sc.op("scalar", lambda e, b0=b0, bn=bn, tt=tt, bank=bank: e.copy(
                    xv[:, b0:b0 + bn, tt * P:(tt + 1) * P], self.tp[bank][:, 0:bn * P]),
                    waits=[(self.s_mm, mmv), (self.s_sc, self.s_sc.n)], inc=self.s_xn1)
        self.ln1_done = self.s_xn1.n

    def phase_qkv(self):
        sc, c = self.sc, self.c
        ln_all = self.ln1_done
        qkw = [0, 0, 0, 0]
        bank_i = 0
        wq_war = [0, 0]
        self.qst_war = [0, 0]
        kTv = self.kT[:].rearrange("p (g t) -> p g t", g=c.NG)
        vNv = self.vN[:].rearrange("p (t d) -> p t d", t=c.TT)
        wqbv = self.wqb[:].rearrange("p (s cc w) -> p s cc w", s=2, cc=c.HC)
        qsv = self.qstage[:].rearrange("p (s w) -> p s w", s=2)
        slot = 0
        qi = 0

        def wload(slot, src_ap, width, wait_mm):
            sw = [self.s_w0, self.s_w1][slot]
            return (sw, sc.op("gpsimd", lambda e, slot=slot, src_ap=src_ap, width=width: e.dma_start(
                wqbv[:, slot, :, 0:width], src_ap),
                waits=[(self.s_mm, wait_mm)], inc=sw, inc_by=16))

        for (dq0, dqw) in chunks(c.D, c.WQB):
            src = self.wq_d[:, dq0:dq0 + dqw].rearrange("(cc p) w -> p cc w", p=P)
            swl, wl = wload(slot, src, dqw, wq_war[slot])
            for (dqc0, dqcw) in chunks(dqw, P):
                for (t0, tw) in chunks(c.T, 512):
                    bank = bank_i % 4
                    bank_i += 1
                    pb = self.qk[bank]
                    mmv = 0
                    for hc in range(c.HC):
                        mmv = sc.op("tensor", lambda e, slot=slot, hc=hc, dqc0=dqc0, dqcw=dqcw, t0=t0, tw=tw, pb=pb: e.matmul(
                            pb[0:dqcw, 0:tw],
                            wqbv[:, slot, hc, dqc0:dqc0 + dqcw],
                            self.xn1T[:, hc * c.T + t0: hc * c.T + t0 + tw],
                            start=(hc == 0), stop=(hc == c.HC - 1)),
                            waits=([(self.s_xn1, ln_all), (swl, wl),
                                    (self.s_ev, qkw[bank])] if hc == 0 else []),
                            inc=self.s_mm)
                    qslot = qi % 2
                    qi += 1
                    sq_sem = [self.s_q0, self.s_q1][qslot]
                    ev = sc.op("scalar", lambda e, qslot=qslot, dqcw=dqcw, tw=tw, pb=pb: e.copy(
                        qsv[0:dqcw, qslot, 0:tw], pb[0:dqcw, 0:tw]),
                        waits=[(self.s_mm, mmv), (sq_sem, self.qst_war[qslot]),
                               (self.s_sc, self.s_sc.n)],
                        inc=self.s_ev)
                    qkw[bank] = ev
                    self.qst_war[qslot] = sc.op("sync", lambda e, qslot=qslot, dq0=dq0, dqc0=dqc0, dqcw=dqcw, t0=t0, tw=tw: e.dma_start(
                        self.qt_d[dq0 + dqc0:dq0 + dqc0 + dqcw, t0:t0 + tw],
                        qsv[0:dqcw, qslot, 0:tw]),
                        waits=[(self.s_ev, ev)], inc=sq_sem, inc_by=16)
            wq_war[slot] = self.s_mm.n
            slot = 1 - slot

        for (dk0, dkw) in chunks(c.DKV, c.WQB):
            src = self.wk_d[:, dk0:dk0 + dkw].rearrange("(cc p) w -> p cc w", p=P)
            swl, wl = wload(slot, src, dkw, wq_war[slot])
            for (dkc0, dkcw) in chunks(dkw, P):
                g = (dk0 + dkc0) // P
                for (t0, tw) in chunks(c.T, 512):
                    bank = bank_i % 4
                    bank_i += 1
                    pb = self.qk[bank]
                    mmv = 0
                    for hc in range(c.HC):
                        mmv = sc.op("tensor", lambda e, slot=slot, hc=hc, dkc0=dkc0, dkcw=dkcw, t0=t0, tw=tw, pb=pb: e.matmul(
                            pb[0:dkcw, 0:tw],
                            wqbv[:, slot, hc, dkc0:dkc0 + dkcw],
                            self.xn1T[:, hc * c.T + t0: hc * c.T + t0 + tw],
                            start=(hc == 0), stop=(hc == c.HC - 1)),
                            waits=([(self.s_xn1, ln_all), (swl, wl),
                                    (self.s_ev, qkw[bank])] if hc == 0 else []),
                            inc=self.s_mm)
                    qkw[bank] = sc.op("scalar", lambda e, g=g, t0=t0, tw=tw, pb=pb, dkcw=dkcw: e.copy(
                        kTv[0:dkcw, g, t0:t0 + tw], pb[0:dkcw, 0:tw]),
                        waits=[(self.s_mm, mmv), (self.s_sc, self.s_sc.n)], inc=self.s_ev)
            wq_war[slot] = self.s_mm.n
            slot = 1 - slot

        for (vb0, vbw) in chunks(c.DKV, c.WQB):
            src = self.wv_d[:, vb0:vb0 + vbw].rearrange("(cc p) w -> p cc w", p=P)
            swl, wl = wload(slot, src, vbw, wq_war[slot])
            for tt in range(c.TT):
                bank = bank_i % 4
                bank_i += 1
                pb = self.qk[bank]
                mmv = 0
                for hc in range(c.HC):
                    mmv = sc.op("tensor", lambda e, slot=slot, hc=hc, tt=tt, vbw=vbw, pb=pb: e.matmul(
                        pb[:, 0:vbw],
                        self.xn1T[:, hc * c.T + tt * P: hc * c.T + (tt + 1) * P],
                        wqbv[:, slot, hc, 0:vbw],
                        start=(hc == 0), stop=(hc == c.HC - 1)),
                        waits=([(self.s_xn1, ln_all), (swl, wl),
                                (self.s_ev, qkw[bank])] if hc == 0 else []),
                        inc=self.s_mm)
                qkw[bank] = sc.op("scalar", lambda e, tt=tt, vb0=vb0, vbw=vbw, pb=pb: e.copy(
                    vNv[:, tt, vb0:vb0 + vbw], pb[:, 0:vbw]),
                    waits=[(self.s_mm, mmv), (self.s_sc, self.s_sc.n)], inc=self.s_ev)
            wq_war[slot] = self.s_mm.n
            slot = 1 - slot

        self.qkv_mm_done = self.s_mm.n
        self.qkv_ev_done = self.s_ev.n

    def join_all(self, waits):
        for eng in ("tensor", "vector", "scalar", "sync", "gpsimd"):
            self.sc.op(eng, lambda e: e.nop(), waits=list(waits))

    def phase_attention(self):
        sc, c = self.sc, self.c
        SCALE = float(1.0 / np.sqrt(128.0).astype(np.float32))
        kTv = self.kT[:].rearrange("p (g t) -> p g t", g=c.NG)
        vNv = self.vN[:].rearrange("p (t d) -> p t d", t=c.TT)
        qv = self.qs[:].rearrange("p (s cc w) -> p s cc w", s=2, cc=c.HC)
        prv = self.probs[:].rearrange("p (s w) -> p s w", s=4)
        escv = self.esc[:].rearrange("p (s w) -> p s w", s=6)
        ctv = self.ctxT[:].rearrange("p (s cc w) -> p s cc w", s=2, cc=c.HC)
        wov = self.wo_sb[:].rearrange("p (cc d) -> p cc d", cc=c.HC)
        xnfv = self.xn2f[:].rearrange("p (cc w) -> p cc w", cc=c.HC)
        xnbv = self.xn2b[:].rearrange("p (cc w) -> p cc w", cc=c.HC)

        st = {
            "q_ld": [0, 0], "q_war": [0, 0],
            "s1_war": [0, 0], "s1_i": 0, "s2_pe": [0, 0], "s2_i": 0,
            "pr_war": [0, 0],
            "esc_war": [0, 0, 0],
            "ctx_war": [0, 0],
            "ctx_ev": [0, 0],
            "ltl_free": 0, "xn2f_rd": 0, "xn2b_st": 0, "x2_sq": 0, "x2_st": 0,
            "xr_war": [0, 0],
        }
        stage4_q = []
        self.ce_all = {}

        def stage_a1(qt, u, hx):
            h0 = 2 * u
            g = h0 // c.GS
            nk = (qt + 1) * P
            qslot = qt % 2
            us = u % 3
            kblocks = chunks(nk, 512)
            hx.update(g=g, nk=nk, us=us, us2=u % 2, kblocks=kblocks, h0=h0)
            nch = len(kblocks)
            for hi in range(2):
                h = h0 + hi
                sl = 2 * us + hi
                for ci, (k0, kw) in enumerate(kblocks):
                    bank = st["s1_i"] % len(self.S1)
                    st["s1_i"] += 1
                    mmv = sc.op("tensor", lambda e, h=h, g=g, qslot=qslot, k0=k0, kw=kw, bank=bank: e.matmul(
                        self.S1[bank][:, 0:kw], qv[:, qslot, h, :], kTv[:, g, k0:k0 + kw],
                        start=True, stop=True),
                        waits=[([self.s_qs0, self.s_qs1][qslot], st["q_ld"][qslot]),
                               (self.s_cs, st["s1_war"][bank]),
                               (self.s_ev, self.qkv_ev_done)],
                        inc=self.s_p1)
                    if ci == nch - 1:
                        mmv_m = sc.op("vector", lambda e, kw=kw, bank=bank: e.tensor_add(
                            self.S1[bank][:, kw - P:kw], self.S1[bank][:, kw - P:kw],
                            self.maskA[:]),
                            waits=[(self.s_p1, mmv), (self.s_vc, self.s_vc.n)], inc=self.s_vc)
                        st["s1_war"][bank] = sc.op("scalar", lambda e, k0=k0, kw=kw, bank=bank, sl=sl, ci=ci: e.activation(
                            escv[:, sl, k0:k0 + kw], self.S1[bank][:, 0:kw], AF.Exp,
                            scale=SCALE,
                            accum_out=self.ssc[:, sl * 12 + ci: sl * 12 + ci + 1]),
                            waits=[(self.s_vc, mmv_m), (self.s_sc, self.s_sc.n),
                                   (self.s_p2, st["esc_war"][us])],
                            inc=self.s_cs)
                    else:
                        st["s1_war"][bank] = sc.op("scalar", lambda e, k0=k0, kw=kw, bank=bank, sl=sl, ci=ci: e.activation(
                            escv[:, sl, k0:k0 + kw], self.S1[bank][:, 0:kw], AF.Exp,
                            scale=SCALE,
                            accum_out=self.ssc[:, sl * 12 + ci: sl * 12 + ci + 1]),
                            waits=[(self.s_p1, mmv), (self.s_sc, self.s_sc.n),
                                   (self.s_p2, st["esc_war"][us])],
                            inc=self.s_cs)
                tot = sc.op("vector", lambda e, nch=nch, sl=sl: e.reduce_sum(
                    self.stat[:, 36 + sl:37 + sl],
                    self.ssc[:, sl * 12: sl * 12 + nch], axis=AX.X),
                    waits=[(self.s_cs, self.s_cs.n), (self.s_vc, self.s_vc.n)],
                    inc=self.s_vc)
                sc.op("vector", lambda e, sl=sl: e.reciprocal(
                    self.stat[:, 42 + sl:43 + sl], self.stat[:, 36 + sl:37 + sl]),
                    waits=[(self.s_vc, tot)], inc=self.s_vc)
            hx["cv"] = self.s_vc.n

        def stage_b(qt, u, hx):
            us, us2, kblocks, nk = hx["us"], hx["us2"], hx["kblocks"], hx["nk"]
            rs = 0
            for hi in range(2):
                sl = 2 * us + hi
                rs = sc.op("vector", lambda e, nk=nk, sl=sl: e.tensor_scalar(
                    escv[:, sl, 0:nk], escv[:, sl, 0:nk],
                    self.stat[:, 42 + sl:43 + sl], None, ALU.mult),
                    waits=[(self.s_vc, hx["cv"] if hi == 0 else self.s_vc.n)],
                    inc=self.s_vc)
            pe_marks = [[], []]
            for hi in range(2):
                sl = 2 * us + hi
                psl = 2 * us2 + hi
                for gi, (kb0, kbw) in enumerate(kblocks):
                    bank = st["s2_i"] % 2
                    st["s2_i"] += 1
                    mmv = 0
                    for (kc0, kcw) in chunks(kbw, P):
                        j = kc0 // P
                        mmv = sc.op("tensor", lambda e, sl=sl, o=kb0 + kc0, j=j, bank=bank: e.transpose(
                            self.S2b[bank][:, j * P:(j + 1) * P],
                            escv[:, sl, o:o + P], self.ident[:]),
                            waits=[(self.s_vc, rs), (self.s_pe, st["s2_pe"][bank])],
                            inc=self.s_p2)
                    pe = sc.op("vector", lambda e, bank=bank, kb0=kb0, kbw=kbw, psl=psl: e.tensor_copy(
                        prv[:, psl, kb0:kb0 + kbw], self.S2b[bank][:, 0:kbw]),
                        waits=[(self.s_p2, mmv), (self.s_cx, st["pr_war"][us2]),
                               (self.s_vc, self.s_vc.n)],
                        inc=self.s_pe)
                    st["s2_pe"][bank] = pe
                    pe_marks[hi].append(pe)
            st["esc_war"][us] = self.s_p2.n
            hx["pe_marks"] = pe_marks

        def stage_c(qt, u, hx):
            g, nk, us, kblocks = hx["g"], hx["nk"], hx["us2"], hx["kblocks"]
            h0 = hx["h0"]
            pe_marks = hx["pe_marks"]
            n_k = nk // P
            bank = self.C if u % 2 == 0 else self.DR
            cxv = 0
            ki = 0
            for bi, (kb0, kbw) in enumerate(kblocks):
                for (kc0, kcw) in chunks(kbw, P):
                    tt_k = (kb0 + kc0) // P
                    first = ki == 0
                    last = ki == n_k - 1
                    ki += 1
                    for hi in range(2):
                        sl = 2 * us + hi
                        w = [(self.s_pe, pe_marks[hi][bi])]
                        if first and hi == 0:
                            w += [(self.s_ce, st["ctx_ev"][us])]
                        cxv = sc.op("tensor", lambda e, tt_k=tt_k, g=g, sl=sl, hi=hi, o=kb0 + kc0, first=first, last=last, bank=bank: e.matmul(
                            bank[:, hi * P:(hi + 1) * P],
                            vNv[:, tt_k, g * P:(g + 1) * P],
                            prv[:, sl, o:o + P],
                            start=(first and hi == 0), stop=(last and hi == 1)),
                            waits=w, inc=self.s_cx)
            st["pr_war"][us] = cxv
            ce = sc.op("scalar", lambda e, h0=h0, qt=qt, bank=bank: e.copy(
                ctv[:, qt % 2, h0, :].tensor[0:0] if False else
                self.ctxT[:, (qt % 2) * c.HC * P + h0 * P: (qt % 2) * c.HC * P + (h0 + 2) * P],
                bank[:, 0:2 * P]),
                waits=[(self.s_cx, cxv), (self.s_wo, st["ctx_war"][qt % 2]),
                       (self.s_sc, self.s_sc.n)],
                inc=self.s_ce)
            st["ctx_ev"][us] = ce

        def stage4(qt):
            tt = qt
            cslot = qt % 2
            xslot = qt % 2
            xo = xslot * c.D
            sx = [self.s_x0, self.s_x1][xslot]
            xw = sc.op("sync", lambda e, tt=tt, xo=xo: e.dma_start(
                self.xring[:, xo:xo + c.D], self.x_d[tt * P:(tt + 1) * P, :]),
                waits=[(self.s_x2, st["xr_war"][xslot])], inc=sx, inc_by=16)
            x2_marks = []
            prev_ev = 0
            for (d0, dw) in chunks(c.D, 512):
                mmv = 0
                for hc in range(c.HC):
                    w = ([(self.s_ce, self.ce_all[qt]), (self.s_wol, self.wo_ld),
                          (self.s_x2, prev_ev if prev_ev else st["xr_war"][1 - xslot]),
                          (self.s_te, st.get("pw_tp_ev", 0)),
                          (self.s_ev, self.qkv_ev_done)] if hc == 0 else [])
                    mmv = sc.op("tensor", lambda e, hc=hc, d0=d0, dw=dw, cslot=cslot: e.matmul(
                        self.pw[:, 0:dw], ctv[:, cslot, hc, :], wov[:, hc, d0:d0 + dw],
                        start=(hc == 0), stop=False),
                        waits=w, inc=self.s_wo, defer=stage4_q)
                mmv = sc.op("tensor", lambda e, d0=d0, dw=dw: e.matmul(
                    self.pw[:, 0:dw], self.ones_r1[:], self.bo_bf[0:1, d0:d0 + dw],
                    start=False, stop=True),
                    waits=[(self.s_ms, self.w_ones), (self.s_bo, 16)], inc=self.s_wo, defer=stage4_q)
                ev = sc.op("vector", lambda e, d0=d0, dw=dw, xo=xo: e.tensor_add(
                    self.x2t[:, d0:d0 + dw], self.pw[:, 0:dw],
                    self.xring[:, xo + d0:xo + d0 + dw]),
                    waits=[(self.s_wo, mmv), (sx, xw),
                           (self.s_x2st, st["x2_st"]), (self.s_sc, st["x2_sq"]),
                           (self.s_vc, self.s_vc.n)],
                    inc=self.s_x2)
                prev_ev = ev
                x2_marks.append(ev)
            st["ctx_war"][cslot] = self.s_wo.n
            x2v = x2_marks[-1]
            st["xr_war"][xslot] = x2v
            x2dst = self.y_d if self.c.CAP else self.x2_d
            st["x2_st"] = sc.op("sync", lambda e, tt=tt, x2dst=x2dst: e.dma_start(
                x2dst[tt * P:(tt + 1) * P, :], self.x2t[:]),
                waits=[(self.s_x2, x2v)], inc=self.s_x2st, inc_by=16)
            sc.op("vector", lambda e: e.reduce_sum(self.stat[:, 16:17], self.x2t[:], axis=AX.X),
                  waits=[(self.s_x2, x2v), (self.s_vc, self.s_vc.n)], inc=self.s_vc)
            sq = sc.op("scalar", lambda e: e.activation(self.xnorm[:], self.x2t[:], AF.Square),
                       waits=[(self.s_x2, x2v), (self.s_sc, self.s_sc.n),
                              (self.s_tr, st["xn2f_rd"])], inc=self.s_sc)
            st["x2_sq"] = sq
            sc.op("vector", lambda e: e.reduce_sum(self.stat[:, 17:18], self.xnorm[:], axis=AX.X),
                  waits=[(self.s_sc, sq), (self.s_vc, self.s_vc.n)], inc=self.s_vc)
            sc.op("vector", lambda e: e.tensor_scalar(
                self.stat[:, 18:19], self.stat[:, 16:17], 1.0 / c.D, None, ALU.mult),
                waits=[(self.s_vc, self.s_vc.n)], inc=self.s_vc)
            sc.op("vector", lambda e: e.tensor_scalar(
                self.stat[:, 19:20], self.stat[:, 18:19], self.stat[:, 18:19], None, ALU.mult),
                waits=[(self.s_vc, self.s_vc.n)], inc=self.s_vc)
            vv = sc.op("vector", lambda e: e.scalar_tensor_tensor(
                self.stat[:, 20:21], self.stat[:, 17:18], 1.0 / c.D, self.stat[:, 19:20],
                ALU.mult, ALU.subtract),
                waits=[(self.s_vc, self.s_vc.n)], inc=self.s_vc)
            sc.op("scalar", lambda e: e.activation(
                self.stat[:, 21:22], self.stat[:, 20:21], AF.Ln, bias=self.eps_c[:]),
                waits=[(self.s_vc, vv), (self.s_sc, self.s_sc.n)], inc=self.s_sc)
            rs = sc.op("scalar", lambda e: e.activation(
                self.stat[:, 22:23], self.stat[:, 21:22], AF.Exp, scale=-0.5),
                waits=[(self.s_sc, self.s_sc.n)], inc=self.s_sc)
            nv = sc.op("vector", lambda e: e.tensor_scalar(
                self.xnorm[:], self.x2t[:], self.stat[:, 18:19], self.stat[:, 22:23],
                ALU.subtract, ALU.mult),
                waits=[(self.s_sc, rs), (self.s_vc, self.s_vc.n)], inc=self.s_vc)
            st["x2_nv"] = nv
            ev_marks = [st.get("ltl_tp_ev", 0), st.get("pw_tp_ev", 0)]
            for hc in range(c.HC):
                tb = hc % 2 if c.CAP else 0
                dst = [self.LTL, self.pw][tb]
                if c.CAP:
                    w = [(self.s_vc, nv), (self.s_te, ev_marks[tb])]
                else:
                    w = [(self.s_vc, nv), (self.s_te, self.s_te.n)]
                if hc == 0:
                    w += [(self.s_l2, st["ltl_free"]), (self.s_tr, st["xn2f_rd"])]
                if hc == 1 and c.CAP:
                    w += [(self.s_x2, x2v)]
                tvv = sc.op("tensor", lambda e, hc=hc, dst=dst: e.transpose(
                    dst[:, 0:P], self.xnorm[:, hc * P:(hc + 1) * P], self.identf[:]),
                    waits=w, inc=self.s_tr, defer=stage4_q)
                t1 = sc.op("scalar", lambda e, hc=hc, dst=dst: e.copy(xnfv[:, hc, :], dst[:, 0:P]),
                      waits=[(self.s_tr, tvv), (self.s_sc, self.s_sc.n)], inc=self.s_te)
                ev_marks[tb] = t1
                if not c.CAP:
                    sc.op("vector", lambda e, hc=hc: e.tensor_copy(xnbv[:, hc, :], xnfv[:, hc, :]),
                          waits=[(self.s_te, t1), (self.s_xbst, st["xn2b_st"]),
                                 (self.s_vc, self.s_vc.n)],
                          inc=self.s_te)
            te_all = self.s_te.n
            st["ltl_tp_ev"], st["pw_tp_ev"] = ev_marks
            if not c.CAP:
                st["xn2b_st"] = sc.op("sync", lambda e, qt=qt: e.dma_start(
                    self.xn2_d[:, qt * P:(qt + 1) * P].rearrange("(cc p) w -> p cc w", p=P),
                    xnbv[:]),
                    waits=[(self.s_te, te_all)], inc=self.s_xbst, inc_by=16)
            if c.CAP:
                tmc = sc.op("vector", lambda e: e.tensor_copy(self.xtmb[:], self.xnorm[:]),
                            waits=[(self.s_vc, nv), (self.s_tm, st.get("xtm_st", 0)),
                                   (self.s_vc, self.s_vc.n)],
                            inc=self.s_vc)
                st["xtm_st"] = sc.op("sync", lambda e, qt=qt: e.dma_start(
                    self.xntm_d[qt * P:(qt + 1) * P, :], self.xtmb[:]),
                    waits=[(self.s_vc, tmc)], inc=self.s_tm, inc_by=16)
            rv = 0
            for hc in range(c.HC):
                w = ([(self.s_te, te_all), (self.s_ev, self.rwT_done)] if hc == 0 else [])
                rv = sc.op("tensor", lambda e, hc=hc: e.matmul(
                    self.LTL[:, P:P + c.E], xnfv[:, hc, :],
                    self.rwT[:, hc * c.E:(hc + 1) * c.E],
                    start=(hc == 0), stop=False),
                    waits=w, inc=self.s_tr, defer=stage4_q)
            rv = sc.op("tensor", lambda e: e.matmul(
                self.LTL[:, P:P + c.E], self.ones_rf[:], self.eb_f[:],
                start=False, stop=True),
                waits=[(self.s_cst, self.w_const)], inc=self.s_tr, defer=stage4_q)
            st["xn2f_rd"] = rv
            lg = sc.op("vector", lambda e: e.tensor_copy(self.rc[:, 0:c.E], self.LTL[:, P:P + c.E]),
                       waits=[(self.s_tr, rv), (self.s_vc, self.s_vc.n)], inc=self.s_l2)
            st["ltl_free"] = lg
            m8 = sc.op("vector", lambda e: e.max(out=self.rc[:, 16:24], in_=self.rc[:, 0:c.E]),
                       waits=[(self.s_l2, lg), (self.s_vc, self.s_vc.n)], inc=self.s_vc)
            ng = sc.op("vector", lambda e: e.tensor_scalar(
                self.rc[:, 24:25], self.rc[:, 16:17], -1.0, None, ALU.mult),
                waits=[(self.s_vc, m8)], inc=self.s_vc)
            ex = sc.op("scalar", lambda e: e.activation(
                self.rc[:, 0:c.E],
                self.rc[:, 0:c.E], AF.Exp, bias=self.rc[:, 24:25]),
                waits=[(self.s_vc, ng), (self.s_sc, self.s_sc.n)], inc=self.s_sc)
            e4 = sc.op("scalar", lambda e: e.activation(
                self.rc[:, 25:29], self.rc[:, 16:20], AF.Exp, bias=self.rc[:, 24:25]),
                waits=[(self.s_sc, ex)], inc=self.s_sc)
            sm = sc.op("vector", lambda e: e.reduce_sum(self.rc[:, 29:30], self.rc[:, 25:29], axis=AX.X),
                       waits=[(self.s_sc, e4), (self.s_vc, self.s_vc.n)], inc=self.s_vc)
            rcp = sc.op("vector", lambda e: e.reciprocal(self.rc[:, 30:31], self.rc[:, 29:30]),
                        waits=[(self.s_vc, sm)], inc=self.s_vc)
            kp = sc.op("vector", lambda e: e.tensor_scalar(
                self.rc[:, 32:32 + c.E], self.rc[:, 0:c.E], self.rc[:, 28:29], None, ALU.is_ge),
                waits=[(self.s_vc, rcp)], inc=self.s_vc)
            if c.CAP:
                tl = sc.op("vector", lambda e: e.tensor_scalar(
                    self.rc[:, 31:32], self.rc[:, 28:29], 0.99601, None, ALU.mult),
                    waits=[(self.s_vc, kp)], inc=self.s_vc)
                kl = sc.op("vector", lambda e: e.tensor_scalar(
                    self.rc[:, 48:48 + c.E], self.rc[:, 0:c.E], self.rc[:, 31:32],
                    None, ALU.is_ge),
                    waits=[(self.s_vc, tl)], inc=self.s_vc)
                wv_ = sc.op("vector", lambda e: e.tensor_scalar(
                    self.rc[:, 0:c.E], self.rc[:, 0:c.E], self.rc[:, 30:31], None, ALU.mult),
                    waits=[(self.s_vc, kl)], inc=self.s_vc)
                wv_ = sc.op("vector", lambda e: e.tensor_mul(
                    self.rc[:, 0:c.E], self.rc[:, 0:c.E], self.rc[:, 32:32 + c.E]),
                    waits=[(self.s_vc, wv_)], inc=self.s_vc)
                sc.op("vector", lambda e, tt=tt: e.scalar_tensor_tensor(
                    self.w_all[:, tt * c.E:(tt + 1) * c.E], self.rc[:, 48:48 + c.E],
                    1e-30, self.rc[:, 0:c.E], ALU.mult, ALU.add),
                    waits=[(self.s_vc, wv_)], inc=self.s_l2)
            else:
                wv_ = sc.op("vector", lambda e: e.tensor_scalar(
                    self.rc[:, 0:c.E], self.rc[:, 0:c.E], self.rc[:, 30:31], None, ALU.mult),
                    waits=[(self.s_vc, kp)], inc=self.s_vc)
                sc.op("vector", lambda e, tt=tt: e.tensor_mul(
                    self.w_all[:, tt * c.E:(tt + 1) * c.E], self.rc[:, 0:c.E],
                    self.rc[:, 32:32 + c.E]),
                    waits=[(self.s_vc, wv_)], inc=self.s_l2)

        self.join_all([(self.s_mm, self.qkv_mm_done), (self.s_ev, self.qkv_ev_done),
                       (self.s_xn1, self.ln1_done),
                       (self.s_q0, self.qst_war[0]), (self.s_q1, self.qst_war[1]),
                       (self.s_w0, self.s_w0.n), (self.s_w1, self.s_w1.n),
                       (self.s_vc, self.s_vc.n), (self.s_sc, self.s_sc.n)])
        for qt in range(c.TT if self.qt_limit is None else self.qt_limit):
            sc.splice("tensor", stage4_q)
            qslot = qt % 2
            qwaits = ([(self.s_p2, st["q_war"][qslot])] if qt >= 2 else
                      [(self.s_q0, self.qst_war[0]), (self.s_q1, self.qst_war[1]),
                       (self.s_mm, self.qkv_mm_done), (self.s_ev, self.qkv_ev_done)])
            sqs = [self.s_qs0, self.s_qs1][qslot]
            st["q_ld"][qslot] = sc.op("sync", lambda e, qt=qt, qslot=qslot: e.dma_start(
                qv[:, qslot, :, :],
                self.qt_d[:, qt * P:(qt + 1) * P].rearrange("(cc p) w -> p cc w", p=P)),
                waits=qwaits, inc=sqs, inc_by=16)
            st["q_sem"] = sqs
            nu = (c.NH if self.head_limit is None else self.head_limit) // 2
            hctx = [dict() for _ in range(nu)]
            for step in range(nu + 3):
                if step >= 3:
                    stage_c(qt, step - 3, hctx[step - 3])
                if 2 <= step < nu + 2:
                    stage_b(qt, step - 2, hctx[step - 2])
                if step < nu:
                    stage_a1(qt, step, hctx[step])
            self.ce_all[qt] = self.s_ce.n
            st["q_war"][qslot] = self.s_p2.n
            if not self.no_stage4:
                stage4(qt)
        sc.splice("tensor", stage4_q)
        self.w_done = self.s_l2.n
        self.attn_xb = self.s_xbst.n
        self.attn_x2 = self.s_x2st.n

    def phase_moe(self):
        sc, c = self.sc, self.c
        xnv = self.xn2h[:].rearrange("p (cc t) -> p cc t", cc=c.HC)
        accv = self.acc[:].rearrange("p (t d) -> p t d", t=c.TTH)
        actv = self.act[:].rearrange("p (ic t) -> p ic t", ic=c.IC)
        gubv = self.gub[:].rearrange("p (s cc w) -> p s cc w", s=4, cc=c.HC)
        dwv = self.dwn[:].rearrange("p (ic d) -> p ic d", ic=c.IC)
        gsv = self.gsb[:].rearrange("p (s w) -> p s w", s=2)

        gu_war = [0, 0]
        pg_war = [0, 0]
        pu_war = [0, 0]
        pd_war = [0, 0, 0]
        gs_war = [0, 0]
        act_rd = 0
        dwn_rd = 0
        y_total = 0
        xl_prev = 0

        self.join_all([(self.s_l2, self.w_done), (self.s_sc, self.s_sc.n),
                       (self.s_tr, self.s_tr.n), (self.s_te, self.s_te.n),
                       (self.s_vc, self.s_vc.n), (self.s_ce, self.s_ce.n),
                       (self.s_x2, self.s_x2.n), (self.s_x2st, self.attn_x2),
                       (self.s_xbst, self.attn_xb),
                       (self.s_qs0, self.s_qs0.n), (self.s_qs1, self.s_qs1.n),
                       (self.s_x0, self.s_x0.n), (self.s_x1, self.s_x1.n),
                       (self.s_wol, self.wo_ld)]
                      + ([(self.s_tm, self.s_tm.n)] if self.c.CAP else []))
        for half in range(c.N_HALVES):
            t_off = half * c.TH
            xl = sc.op("sync", lambda e, t_off=t_off: e.dma_start(
                xnv[:],
                self.xn2_d[:, t_off:t_off + c.TH].rearrange("(cc p) w -> p cc w", p=P)),
                waits=[(self.s_gu, xl_prev), (self.s_l2, self.w_done),
                       (self.s_xbst, self.attn_xb)],
                inc=self.s_xl, inc_by=16)
            for tt in range(c.TTH):
                sc.op("sync", lambda e, tt=tt, t_off=t_off: e.dma_start(
                    accv[:, tt, :], self.x2_d[t_off + tt * P: t_off + (tt + 1) * P, :]),
                    waits=[(self.s_yst, y_total), (self.s_x2st, self.attn_x2)],
                    inc=self.s_yi, inc_by=16)
            yi_all = self.s_yi.n
            for ei in range(c.E):
                ib_list = chunks(c.DI, c.GIB)
                gl_marks = {}

                def gu_load(bi, wait_mark):
                    i0, iw = ib_list[bi]
                    slot = bi % 2
                    sg = [self.s_g0, self.s_g1][slot]
                    sc.op("gpsimd", lambda e, ei=ei, i0=i0, iw=iw, slot=slot: e.dma_start(
                        gubv[:, 2 * slot, :, 0:iw],
                        self.gate_d[ei][:, i0:i0 + iw].rearrange("(cc p) w -> p cc w", p=P)),
                        waits=[(self.s_gu, wait_mark)], inc=sg, inc_by=16)
                    gl = sc.op("gpsimd", lambda e, ei=ei, i0=i0, iw=iw, slot=slot: e.dma_start(
                        gubv[:, 2 * slot + 1, :, 0:iw],
                        self.up_d[ei][:, i0:i0 + iw].rearrange("(cc p) w -> p cc w", p=P)),
                        waits=[], inc=sg, inc_by=16)
                    gl_marks[bi] = (sg, gl)

                for bi in range(min(2, len(ib_list))):
                    gu_load(bi, gu_war[bi % 2])
                dl_next = [0]
                dl_last = [0]

                def down_load_some(k):
                    while dl_next[0] < c.IC and k > 0:
                        ic = dl_next[0]
                        dl_next[0] += 1
                        k -= 1
                        dl_last[0] = sc.op("gpsimd", lambda e, ei=ei, ic=ic: e.dma_start(
                            dwv[:, ic, :], self.down_d[ei][ic * P:(ic + 1) * P, :]),
                            waits=[(self.s_dn, dwn_rd)], inc=self.s_dw, inc_by=16)
                pi = 0
                for bi, (i0, iw) in enumerate(ib_list):
                    slot = bi % 2
                    for (ic0, icw) in chunks(iw, P):
                        ic = (i0 + ic0) // P
                        for (t0, tw) in chunks(c.TH, 512):
                            bank = pi % 2
                            gslot = pi % 2
                            pi += 1
                            gm = um = 0
                            for hc in range(c.HC):
                                gm = sc.op("tensor", lambda e, slot=slot, hc=hc, ic0=ic0, t0=t0, tw=tw, bank=bank: e.matmul(
                                    self.pg[bank][:, 0:tw],
                                    gubv[:, 2 * slot, hc, ic0:ic0 + P],
                                    xnv[:, hc, t0:t0 + tw],
                                    start=(hc == 0), stop=(hc == c.HC - 1)),
                                    waits=([gl_marks[bi], (self.s_xl, xl),
                                            (self.s_si, pg_war[bank])] if hc == 0 else []),
                                    inc=self.s_gu)
                            for hc in range(c.HC):
                                um = sc.op("tensor", lambda e, slot=slot, hc=hc, ic0=ic0, t0=t0, tw=tw, bank=bank: e.matmul(
                                    self.pu[bank][:, 0:tw],
                                    gubv[:, 2 * slot + 1, hc, ic0:ic0 + P],
                                    xnv[:, hc, t0:t0 + tw],
                                    start=(hc == 0), stop=(hc == c.HC - 1)),
                                    waits=([(self.s_ae, pu_war[bank])] if hc == 0 else []),
                                    inc=self.s_gu)
                            sv = sc.op("scalar", lambda e, bank=bank, tw=tw, gslot=gslot: e.activation(
                                gsv[:, gslot, 0:tw], self.pg[bank][:, 0:tw], AF.Silu),
                                waits=[(self.s_gu, gm), (self.s_ae, gs_war[gslot]),
                                       (self.s_sc, self.s_sc.n)],
                                inc=self.s_si)
                            pg_war[bank] = sv
                            av = sc.op("vector", lambda e, bank=bank, tw=tw, gslot=gslot, ic=ic, t0=t0: e.tensor_mul(
                                actv[:, ic, t0:t0 + tw], gsv[:, gslot, 0:tw],
                                self.pu[bank][:, 0:tw]),
                                waits=[(self.s_gu, um), (self.s_si, sv),
                                       (self.s_dn, act_rd), (self.s_vc, self.s_vc.n)],
                                inc=self.s_ae)
                            pu_war[bank] = av
                            gs_war[gslot] = av
                    gu_war[slot] = self.s_gu.n
                    if bi + 2 < len(ib_list):
                        gu_load(bi + 2, gu_war[slot])
                    if bi >= 1:
                        down_load_some(2)
                down_load_some(c.IC)
                dl_all = dl_last[0]
                act_done = self.s_ae.n
                bank_i = 0
                for tt in range(c.TTH):
                    for (d0, dw) in chunks(c.D, 512):
                        bank = bank_i % 3
                        bank_i += 1
                        dm = 0
                        for ic in range(c.IC):
                            w = ([(self.s_ae, act_done), (self.s_de, pd_war[bank]),
                                  (self.s_dw, dl_all)] if ic == 0 else [])
                            dm = sc.op("tensor", lambda e, ic=ic, tt=tt, d0=d0, dw=dw, bank=bank: e.matmul(
                                self.pd[bank][:, 0:dw],
                                actv[:, ic, tt * P:(tt + 1) * P],
                                dwv[:, ic, d0:d0 + dw],
                                start=(ic == 0), stop=(ic == c.IC - 1)),
                                waits=w, inc=self.s_dn)
                        wt = half * c.TTH + tt
                        pd_war[bank] = sc.op("vector", lambda e, tt=tt, d0=d0, dw=dw, bank=bank, wt=wt, ei=ei: e.scalar_tensor_tensor(
                            accv[:, tt, d0:d0 + dw], self.pd[bank][:, 0:dw],
                            self.w_all[:, wt * c.E + ei: wt * c.E + ei + 1],
                            accv[:, tt, d0:d0 + dw], ALU.mult, ALU.add),
                            waits=[(self.s_dn, dm), (self.s_yi, yi_all),
                                   (self.s_l2, self.w_done), (self.s_vc, self.s_vc.n)],
                            inc=self.s_de)
                act_rd = self.s_dn.n
                dwn_rd = self.s_dn.n
            xl_prev = self.s_gu.n
            for tt in range(c.TTH):
                sc.op("sync", lambda e, tt=tt, t_off=t_off: e.dma_start(
                    self.y_d[t_off + tt * P: t_off + (tt + 1) * P, :], accv[:, tt, :]),
                    waits=[(self.s_de, self.s_de.n)], inc=self.s_yst, inc_by=16)
            y_total = self.s_yst.n


    def enter_moe_scope_sparse(self):
        nc, c = self.nc, self.c
        gub_w = 2 * 2 * c.HC * c.GIB
        gs = [nc.sbuf_tensor("xngT", [P, c.HC * c.CAP], BF16),
              nc.sbuf_tensor("acts", [P, c.IC * c.CAP], BF16),
              nc.sbuf_tensor("gub", [P, gub_w], BF16),
              nc.sbuf_tensor("dwn", [P, c.IC * c.D], BF16),
              nc.sbuf_tensor("gssb", [P, 2 * 512], BF16),
              nc.sbuf_tensor("xstg", [P, 4 * c.D], BF16),
              nc.sbuf_tensor("sst", [P, 2 * min(1024, c.D)], F32),
              nc.sbuf_tensor("kpall", [P, c.TT * c.E], F32),
              nc.sbuf_tensor("rksc", [P, 2 * c.E], F32),
              nc.sbuf_tensor("peib", [P, c.TT * c.E], mybir.dt.int32),
              nc.sbuf_tensor("wrtb", [P, 2 * c.E], F32),
              nc.sbuf_tensor("idxc", [P, 2 * c.CT], mybir.dt.int32),
              nc.sbuf_tensor("wrg", [P, 2 * c.CT * c.E], F32),
              nc.sbuf_tensor("m01f_sb", [P, P], F32),
              nc.sbuf_tensor("s01f_sb", [P, P], F32),
              nc.sbuf_tensor("iota_sb", [P, c.TT], mybir.dt.int32),
              nc.sbuf_tensor("ecr_sb", [1, c.E], F32),
              nc.sbuf_tensor("thr_sb", [P, c.E], F32),
              nc.sbuf_tensor("trow", [1, c.TT * c.E], F32),
              nc.sbuf_tensor("orow", [1, c.TT * c.E], F32),
              nc.sbuf_tensor("tot16", [c.TT, c.E], F32),
              nc.psum_tensor("pg0", [P, 512], F32),
              nc.psum_tensor("pg1", [P, 512], F32),
              nc.psum_tensor("pu0", [P, 512], F32),
              nc.psum_tensor("pu1", [P, 512], F32),
              nc.psum_tensor("ptp", [P, 512], BF16),
              nc.psum_tensor("ptq", [P, 512], BF16)]
        (self.xngT, self.acts, self.gub, self.dwn, self.gsb, self.xstg,
         self.sst, self.kpall, self.rksc, self.peib, self.wrtb, self.idxc,
         self.wrg, self.m01f_sb, self.s01f_sb,
         self.iota_sb, self.ecr_sb, self.thr_sb, self.trow, self.orow, self.tot16,
         pg0, pg1, pu0, pu1, ptp, ptq) = [g.__enter__() for g in gs]
        self.pg = [pg0, pg1]
        self.pu = [pu0, pu1]
        self.pd = self.pg
        self.ptp = [ptp, ptq]
        self.tp_grp = 0
        self.tp_bank_war = [0, 0]
        return gs

    def phase_router_build(self):
        sc, c = self.sc, self.c
        BIG = 1.0e7
        kv = self.kpall[:].rearrange("p (t e) -> p t e", t=c.TT)

        def _mkregs(e):
            self.bnd_ix = e.alloc_register("bnd_ix")
            e.reg_mov(self.bnd_ix, c.E * c.CAP - 1)
            self.bnd_t = e.alloc_register("bnd_t")
            return e.reg_mov(self.bnd_t, c.T - 1)
        sc.op("gpsimd", _mkregs)
        sc.op("sync", lambda e: e.dma_start(self.m01f_sb[:], self.m01f_d.ap()), inc=self.s_rb, inc_by=16)
        sc.op("sync", lambda e: e.dma_start(self.s01f_sb[:], self.s01f_d.ap()), inc=self.s_rb, inc_by=16)
        sc.op("sync", lambda e: e.dma_start(self.iota_sb[:], self.iota_d.ap()), inc=self.s_rb, inc_by=16)
        sc.op("sync", lambda e: e.dma_start(self.ecr_sb[:], self.ecr_d.ap()), inc=self.s_rb, inc_by=16)
        sc.op("sync", lambda e: e.dma_start(self.thr_sb[:], self.thr_d.ap()), inc=self.s_rb, inc_by=16)
        sc.op("sync", lambda e: e.dma_start(
            self.idx_d.tensor[:].rearrange("(p w) one -> p (w one)", p=P),
            self.sent_d.ap()), inc=self.s_rb, inc_by=16)
        self.sent_mark = cst = self.s_rb.n
        kmarks = []
        for tt in range(c.TT):
            kmarks.append(sc.op("vector", lambda e, tt=tt: e.tensor_scalar(
                kv[:, tt, :], self.w_all[:, tt * c.E:(tt + 1) * c.E], 0.0, None, ALU.is_gt),
                waits=[(self.s_l2, self.w_done), (self.s_vc, self.s_vc.n)], inc=self.s_kp))
        tmm = 0
        for tt in range(c.TT):
            tmm = sc.op("tensor", lambda e, tt=tt: e.matmul(
                self.pw[0:1, tt * c.E:(tt + 1) * c.E], self.ones_cf[:], kv[:, tt, :],
                start=(tt == 0), stop=(tt == c.TT - 1)),
                waits=[(self.s_kp, kmarks[tt])], inc=self.s_ps)
        tev = sc.op("scalar", lambda e: e.copy(self.trow[:], self.pw[0:1, 0:c.TT * c.E]),
                    waits=[(self.s_ps, tmm), (self.s_sc, self.s_sc.n)], inc=self.s_ps)
        t1 = sc.op("sync", lambda e: e.dma_start(self.tot_d, self.trow[:]),
                   waits=[(self.s_ps, tev)], inc=self.s_bc, inc_by=16)
        t2 = sc.op("sync", lambda e: e.dma_start(self.tot16[:], self.tot_d),
                   waits=[(self.s_bc, t1)], inc=self.s_bc, inc_by=16)
        omm = sc.op("tensor", lambda e: e.matmul(
            self.pw[0:c.TT, 256:256 + c.E], self.s01f_sb[0:c.TT, 0:c.TT], self.tot16[:],
            start=True, stop=False),
            waits=[(self.s_bc, t2), (self.s_rb, cst), (self.s_ps, self.s_ps.n)],
            inc=self.s_ps)
        omm = sc.op("tensor", lambda e: e.matmul(
            self.pw[0:c.TT, 256:256 + c.E], self.ones_rf[0:1, 0:c.TT], self.ecr_sb[:],
            start=False, stop=True),
            waits=[(self.s_rb, cst)], inc=self.s_ps)
        oev = sc.op("scalar", lambda e: e.copy(self.tot16[:], self.pw[0:c.TT, 256:256 + c.E]),
                    waits=[(self.s_ps, omm), (self.s_sc, self.s_sc.n)], inc=self.s_ps)
        o1 = sc.op("sync", lambda e: e.dma_start(self.off_d, self.tot16[:]),
                   waits=[(self.s_ps, oev)], inc=self.s_bc, inc_by=16)
        o2 = sc.op("sync", lambda e: e.dma_start(self.orow[:], self.off_d),
                   waits=[(self.s_bc, o1)], inc=self.s_bc, inc_by=16)
        wrt_war = [0, 0]
        pd_rw = [0, 0]
        pcast_marks = []
        for tt in range(c.TT):
            bank = self.pd[tt % 2]
            cmm = sc.op("tensor", lambda e, tt=tt, bank=bank: e.matmul(
                bank[:, 0:c.E], self.m01f_sb[:], kv[:, tt, :],
                start=True, stop=False),
                waits=[(self.s_kp, kmarks[tt]), (self.s_bc, o2),
                       (self.s_kp, pd_rw[tt % 2])],
                inc=self.s_ps)
            cmm = sc.op("tensor", lambda e, tt=tt, bank=bank: e.matmul(
                bank[:, 0:c.E], self.ones_rf[:], self.orow[0:1, tt * c.E:(tt + 1) * c.E],
                start=False, stop=True),
                waits=[], inc=self.s_ps)
            px = sc.op("vector", lambda e, tt=tt, bank=bank: e.tensor_sub(
                self.rksc[:, 0:c.E], bank[:, 0:c.E], kv[:, tt, :]),
                waits=[(self.s_ps, cmm), (self.s_vc, self.s_vc.n),
                       (self.s_kp, self.s_kp.n)], inc=self.s_kp)
            pd_rw[tt % 2] = px
            ws = tt % 2
            wv2 = sc.op("vector", lambda e, tt=tt, ws=ws: e.tensor_copy(
                self.wrtb[:, ws * c.E:(ws + 1) * c.E],
                self.w_all[:, tt * c.E:(tt + 1) * c.E]),
                waits=[([self.s_wa, self.s_wb][ws], wrt_war[ws]),
                       (self.s_vc, self.s_vc.n)], inc=self.s_vc)
            wrt_war[ws] = sc.op("sync", lambda e, tt=tt, ws=ws: e.dma_start(
                self.wr_d[tt * P:(tt + 1) * P, :],
                self.wrtb[:, ws * c.E:(ws + 1) * c.E]),
                waits=[(self.s_vc, wv2)], inc=[self.s_wa, self.s_wb][ws], inc_by=16)
            p1 = sc.op("vector", lambda e, tt=tt: e.tensor_scalar(
                self.rksc[:, a:a + c.E] if False else self.rksc[:, 1 * c.E:2 * c.E],
                kv[:, tt, :], -BIG, BIG, ALU.mult, ALU.add),
                waits=[(self.s_vc, self.s_vc.n)], inc=self.s_vc)
            p2 = sc.op("vector", lambda e: e.tensor_add(
                self.rksc[:, 0:c.E], self.rksc[:, 0:c.E], self.rksc[:, 1 * c.E:2 * c.E]),
                waits=[(self.s_vc, p1)], inc=self.s_vc)
            ov1 = sc.op("vector", lambda e: e.tensor_tensor(
                self.rksc[:, 1 * c.E:2 * c.E], self.rksc[:, 0:c.E], self.thr_sb[:],
                ALU.is_ge),
                waits=[(self.s_vc, p2), (self.s_rb, self.sent_mark)], inc=self.s_vc)
            p3 = sc.op("vector", lambda e: e.scalar_tensor_tensor(
                self.rksc[:, 0:c.E], self.rksc[:, 1 * c.E:2 * c.E], BIG,
                self.rksc[:, 0:c.E], ALU.mult, ALU.add),
                waits=[(self.s_vc, ov1)], inc=self.s_vc)
            pcast = sc.op("vector", lambda e, tt=tt: e.tensor_copy(
                self.peib[:, tt * c.E:(tt + 1) * c.E], self.rksc[:, 0:c.E]),
                waits=[(self.s_vc, p3), (self.s_vc, self.s_vc.n)],
                inc=self.s_kp)
            pcast_marks.append(pcast)
        self.sx_marks = [None] * c.E
        sx_prev = [0, 0]

        def emit_sx(ei):
            sxs = [self.s_sx0, self.s_sx1][ei % 2]
            for tt in range(c.TT):
                sc.op("gpsimd", lambda e, tt=tt, ei=ei: e.indirect_dma_start(
                    self.idx_d,
                    bass.IndirectOffsetOnAxis(self.peib[:, tt * c.E + ei: tt * c.E + ei + 1], 0),
                    self.iota_sb[:, tt:tt + 1], None,
                    bounds_check=self.bnd_ix, oob_is_err=False),
                    waits=[(self.s_kp, pcast_marks[tt]), (self.s_rb, self.sent_mark)] +
                          ([(sxs, sx_prev[ei % 2])] if tt == 0 else []),
                    inc=sxs, inc_by=16)
            self.sx_marks[ei] = (sxs, sxs.n)
            sx_prev[ei % 2] = sxs.n

        for ei in range(min(2, c.E)):
            emit_sx(ei)
        self.emit_sx = emit_sx
        self.pending_sx = list(range(2, c.E))
        self.wr_done = (self.s_wa.n, self.s_wb.n)

    def phase_moe_sparse(self):
        sc, c = self.sc, self.c
        xnv = self.xngT[:].rearrange("p (cc t) -> p cc t", cc=c.HC)
        actv = self.acts[:].rearrange("p (ic t) -> p ic t", ic=c.IC)
        gubv = self.gub[:].rearrange("p (s cc w) -> p s cc w", s=4, cc=c.HC)
        dwv = self.dwn[:].rearrange("p (ic d) -> p ic d", ic=c.IC)
        gsv = self.gsb[:].rearrange("p (s w) -> p s w", s=2)
        xsv = self.xstg[:].rearrange("p (s w) -> p s w", s=4)
        ssv = self.sst[:].rearrange("p (s w) -> p s w", s=2)
        wgv = self.wrg[:].rearrange("p (s ct w) -> p s ct w", s=2, ct=c.CT)

        self.join_all([(self.s_l2, self.w_done), (self.s_sc, self.s_sc.n),
                       (self.s_tr, self.s_tr.n), (self.s_te, self.s_te.n),
                       (self.s_vc, self.s_vc.n), (self.s_ce, self.s_ce.n),
                       (self.s_x2, self.s_x2.n), (self.s_x2st, self.attn_x2),
                       (self.s_xbst, self.attn_xb), (self.s_tm, self.s_tm.n),
                       (self.s_qs0, self.s_qs0.n), (self.s_qs1, self.s_qs1.n),
                       (self.s_x0, self.s_x0.n), (self.s_x1, self.s_x1.n),
                       (self.s_wol, self.wo_ld)])
        self.phase_router_build()

        sc.op("gpsimd", lambda e: e.memset(self.wrg[:], 0.0), inc=self.s_ms)

        gu_war = [0, 0]
        pg_war = [0, 0]
        pu_war = [0, 0]
        pd_war = [0, 0]
        gs_war = [0, 0]
        xstg_war = [0, 0, 0, 0]
        sst_war = [0, 0]
        idx_war = [[], []]
        wrg_war = [0, 0]
        act_rd = 0
        dwn_rd = 0
        prep_done = {}
        self.y_marks = {}

        gxsems = [self.s_gx0, self.s_gx1, self.s_gx2, self.s_gx3]
        prep_st = {}

        def gather_x(ei, ct, il):
            es = ei % 2
            xs = (ei * c.CT + ct) % 4
            return sc.op("gpsimd", lambda e, ct=ct, es=es, xs=xs: e.indirect_dma_start(
                xsv[:, xs, :], None, self.xntm_d,
                bass.IndirectOffsetOnAxis(self.idxc[:, es * c.CT + ct: es * c.CT + ct + 1], 0),
                bounds_check=self.bnd_t, oob_is_err=False),
                waits=[(self.s_ix, il), (self.s_tp, xstg_war[xs])],
                inc=gxsems[xs], inc_by=16)

        def prep_loads(ei):
            es = ei % 2
            il = 0
            for ct in range(c.CT):
                il = sc.op("gpsimd", lambda e, ei=ei, es=es, ct=ct: e.dma_start(
                    self.idxc[:, es * c.CT + ct: es * c.CT + ct + 1],
                    self.idx_d.tensor[ei * c.CAP + ct * P: ei * c.CAP + (ct + 1) * P, :]),
                    waits=([self.sx_marks[ei]] + idx_war[es] if ct == 0 else []),
                    inc=self.s_ix, inc_by=16)
            wg = 0
            for ct in range(c.CT):
                wg = sc.op("gpsimd", lambda e, ct=ct, es=es: e.indirect_dma_start(
                    wgv[:, es, ct, :], None, self.wr_d,
                    bass.IndirectOffsetOnAxis(self.idxc[:, es * c.CT + ct: es * c.CT + ct + 1], 0),
                    bounds_check=self.bnd_t, oob_is_err=False),
                    waits=([(self.s_wa, self.wr_done[0]), (self.s_wb, self.wr_done[1]),
                            (self.s_de, wrg_war[es]), (self.s_ix, il)] if ct == 0 else []),
                    inc=[self.s_gw0, self.s_gw1][es], inc_by=16)
            gxm = {}
            for ct in range(min(4, c.CT)):
                gxm[ct] = gather_x(ei, ct, il)
            prep_st[ei] = dict(il=il, wg=wg, gxm=gxm)

        def prep_rest(ei, agate):
            es = ei % 2
            stt = prep_st[ei]
            il = stt["il"]
            for ct in range(c.CT):
                xs = (ei * c.CT + ct) % 4
                sgx = gxsems[xs]
                gx = stt["gxm"].get(ct)
                if gx is None:
                    gx = gather_x(ei, ct, il)
                mmv = 0
                for hc in range(c.HC):
                    tb = self.tp_grp % 2
                    mmv = sc.op("tensor", lambda e, hc=hc, xs=xs, tb=tb: e.transpose(
                        self.ptp[tb][:, (hc % 4) * P:(hc % 4 + 1) * P],
                        xsv[:, xs, hc * P:(hc + 1) * P], self.ident[:]),
                        waits=(([(sgx, gx), agate] if hc == 0 else []) +
                               ([(self.s_tp, self.tp_bank_war[tb])] if hc % 4 == 0 else [])),
                        inc=self.s_ps)
                    if hc % 4 == 3 or hc == c.HC - 1:
                        nb = hc % 4 + 1
                        self.tp_bank_war[tb] = sc.op(
                            "scalar", lambda e, hc=hc, ct=ct, nb=nb, tb=tb: e.copy(
                                xnv[:, hc - nb + 1:hc + 1, ct * P:(ct + 1) * P],
                                self.ptp[tb][:, 0:nb * P].rearrange("p (f w) -> p f w", f=nb)),
                            waits=[(self.s_ps, mmv), (self.s_sc, self.s_sc.n)],
                            inc=self.s_tp)
                        self.tp_grp += 1
                xstg_war[xs] = self.s_tp.n
            prep_done[ei] = (self.s_tp.n, il, ([self.s_gw0, self.s_gw1][es], stt["wg"]))

        prep_loads(0)
        prep_rest(0, self.sx_marks[0])
        ib_list = chunks(c.DI, c.GIB)
        emit_sx_rest_pending = True
        gl_marks = {}

        def gu_load(lei, bi, wait_mark):
            i0, iw = ib_list[bi]
            slot = bi % 2
            sg = [self.s_g0, self.s_g1][slot]
            sc.op("gpsimd", lambda e, lei=lei, i0=i0, iw=iw, slot=slot: e.dma_start(
                gubv[:, 2 * slot, :, 0:iw],
                self.gate_d[lei][:, i0:i0 + iw].rearrange("(cc p) w -> p cc w", p=P)),
                waits=[(self.s_gu, wait_mark)], inc=sg, inc_by=16)
            gl = sc.op("gpsimd", lambda e, lei=lei, i0=i0, iw=iw, slot=slot: e.dma_start(
                gubv[:, 2 * slot + 1, :, 0:iw],
                self.up_d[lei][:, i0:i0 + iw].rearrange("(cc p) w -> p cc w", p=P)),
                waits=[], inc=sg, inc_by=16)
            gl_marks[(lei, bi)] = (sg, gl)

        for bi in range(min(2, len(ib_list))):
            gu_load(0, bi, 0)

        def drain_sx(n):
            while self.pending_sx and n > 0:
                self.emit_sx(self.pending_sx.pop(0))
                n -= 1
        a_end_prev = 0
        prev_ss = (0, 0)
        for ei in range(c.E):
            es = ei % 2
            tp_mark, il_mark, gw_mark = prep_done[ei]
            dl_next = [0]
            dl_last = [0]

            def down_load_some(k):
                while dl_next[0] < c.IC and k > 0:
                    ic = dl_next[0]
                    dl_next[0] += 1
                    k -= 1
                    dl_last[0] = sc.op("gpsimd", lambda e, ei=ei, ic=ic: e.dma_start(
                        dwv[:, ic, :], self.down_d[ei][ic * P:(ic + 1) * P, :]),
                        waits=[(self.s_dn, dwn_rd)], inc=self.s_dw, inc_by=16)

            pi = 0
            for bi, (i0, iw) in enumerate(ib_list):
                slot = bi % 2
                for (ic0, icw) in chunks(iw, P):
                    ic = (i0 + ic0) // P
                    for (t0, tw) in chunks(c.CAP, 512):
                        bank = pi % 2
                        gslot = pi % 2
                        pi += 1
                        gm = um = 0
                        for hc in range(c.HC):
                            gm = sc.op("tensor", lambda e, slot=slot, hc=hc, ic0=ic0, t0=t0, tw=tw, bank=bank: e.matmul(
                                self.pg[bank][:, 0:tw],
                                gubv[:, 2 * slot, hc, ic0:ic0 + P],
                                xnv[:, hc, t0:t0 + tw],
                                start=(hc == 0), stop=(hc == c.HC - 1)),
                                waits=([gl_marks[(ei, bi)], (self.s_tp, tp_mark),
                                        (self.s_si, pg_war[bank]),
                                        (self.s_de, pd_war[bank])] if hc == 0 else []),
                                inc=self.s_gu)
                        for hc in range(c.HC):
                            um = sc.op("tensor", lambda e, slot=slot, hc=hc, ic0=ic0, t0=t0, tw=tw, bank=bank: e.matmul(
                                self.pu[bank][:, 0:tw],
                                gubv[:, 2 * slot + 1, hc, ic0:ic0 + P],
                                xnv[:, hc, t0:t0 + tw],
                                start=(hc == 0), stop=(hc == c.HC - 1)),
                                waits=([(self.s_ae, pu_war[bank])] if hc == 0 else []),
                                inc=self.s_gu)
                        sv = sc.op("scalar", lambda e, bank=bank, tw=tw, gslot=gslot: e.activation(
                            gsv[:, gslot, 0:tw], self.pg[bank][:, 0:tw], AF.Silu),
                            waits=[(self.s_gu, gm), (self.s_ae, gs_war[gslot]),
                                   (self.s_sc, self.s_sc.n)],
                            inc=self.s_si)
                        pg_war[bank] = sv
                        av = sc.op("vector", lambda e, bank=bank, tw=tw, gslot=gslot, ic=ic, t0=t0: e.tensor_mul(
                            actv[:, ic, t0:t0 + tw], gsv[:, gslot, 0:tw],
                            self.pu[bank][:, 0:tw]),
                            waits=[(self.s_gu, um), (self.s_si, sv),
                                   (self.s_dn, act_rd), (self.s_vc, self.s_vc.n)],
                            inc=self.s_ae)
                        pu_war[bank] = av
                        gs_war[gslot] = av
                gu_war[slot] = self.s_gu.n
                nbi = bi + 2
                if nbi < len(ib_list):
                    gu_load(ei, nbi, gu_war[slot])
                elif ei + 1 < c.E:
                    gu_load(ei + 1, nbi - len(ib_list), gu_war[slot])
                if bi >= 1:
                    down_load_some(2)
                drain_sx(1)
                if bi == 1 and ei + 1 < c.E and len(ib_list) >= 2:
                    prep_loads(ei + 1)
            down_load_some(c.IC)
            dl_all = dl_last[0]
            act_done = self.s_ae.n
            a_end = self.s_gu.n
            if ei + 1 < c.E:
                if len(ib_list) < 2:
                    prep_loads(ei + 1)
                prep_rest(ei + 1, (self.s_gu, a_end))
            bank_i = 0
            DHW = min(1024, c.D)
            for stt_i in range(c.CT):
                for dh, (d0, dhw) in enumerate(chunks(c.D, DHW)):
                    dmm = 0
                    for (dc0, dcw) in chunks(dhw, 512):
                        bank = bank_i % 2
                        bank_i += 1
                        dm = 0
                        for ic in range(c.IC):
                            w = ([(self.s_ae, act_done), (self.s_de, pd_war[bank]),
                                  (self.s_dw, dl_all)] if ic == 0 else [])
                            dm = sc.op("tensor", lambda e, ic=ic, stt_i=stt_i, d0=d0, dc0=dc0, dcw=dcw, bank=bank: e.matmul(
                                self.pd[bank][:, 0:dcw],
                                actv[:, ic, stt_i * P:(stt_i + 1) * P],
                                dwv[:, ic, d0 + dc0:d0 + dc0 + dcw],
                                start=(ic == 0), stop=(ic == c.IC - 1)),
                                waits=w, inc=self.s_dn)
                        ss = stt_i % 2
                        pd_war[bank] = sc.op("vector", lambda e, ss=ss, dc0=dc0, dcw=dcw, bank=bank, es=es, stt_i=stt_i, ei=ei: e.tensor_scalar(
                            ssv[:, ss, dc0:dc0 + dcw], self.pd[bank][:, 0:dcw],
                            wgv[:, es, stt_i, ei:ei + 1], None, ALU.mult),
                            waits=[(self.s_dn, dm), ([self.s_ss0, self.s_ss1][ss], sst_war[ss]),
                                   gw_mark, (self.s_vc, self.s_vc.n)],
                            inc=self.s_de)
                        dmm = self.s_de.n
                    ss = stt_i % 2
                    first_sc = stt_i == 0 and dh == 0
                    sst_war[ss] = sc.op("gpsimd", lambda e, ss=ss, d0=d0, dhw=dhw, es=es, stt_i=stt_i: e.indirect_dma_start(
                        self.y_d,
                        bass.IndirectOffsetOnAxis(self.idxc[:, es * c.CT + stt_i: es * c.CT + stt_i + 1], 0),
                        ssv[:, ss, 0:dhw], None,
                        element_offset=d0,
                        bounds_check=self.bnd_t, oob_is_err=False,
                        compute_op=ALU.add),
                        waits=[(self.s_de, dmm), (self.s_ix, il_mark)] +
                              ([(self.s_ss0, prev_ss[0]), (self.s_ss1, prev_ss[1])]
                               if first_sc else []),
                        inc=[self.s_ss0, self.s_ss1][ss], inc_by=16)
            drain_sx(c.E)
            act_rd = self.s_dn.n
            dwn_rd = self.s_dn.n
            wrg_war[es] = self.s_de.n
            prev_ss = (self.s_ss0.n, self.s_ss1.n)
            idx_war[es] = [(self.s_ss0, prev_ss[0]), (self.s_ss1, prev_ss[1])]
        self.moe_sst = (self.s_ss0.n, self.s_ss1.n)
        self.moe_gu = self.s_gu.n


def build(cfg: Cfg, stop_after="moe", qt_limit=None, head_limit=None, no_stage4=False):
    return Builder(cfg, stop_after, qt_limit, head_limit, no_stage4).build()


_CFG = Cfg(T=2048, D=2048, NH=16, NG=4, E=16, DI=1792, N_HALVES=2, WQB=256,
           GIB=128, CAP=1408)
_NCORES = 8


def kernel(L0_attn_x, L0_attn_Wq, L0_attn_Wk, L0_attn_Wv, L0_attn_Wo, L0_attn_bo,
           L0_ffn_router_w, L0_ffn_expert_bias, L0_ffn_gate_proj, L0_ffn_up_proj,
           L0_ffn_down_proj):
    from concourse.bass_utils import run_bass_kernel_spmd
    c = _CFG
    nc = build(c)
    x = np.ascontiguousarray(np.asarray(L0_attn_x, dtype=np.float32))
    shared = {
        "wq": np.ascontiguousarray(np.asarray(L0_attn_Wq, dtype=np.float32)),
        "wk": np.ascontiguousarray(np.asarray(L0_attn_Wk, dtype=np.float32)),
        "wv": np.ascontiguousarray(np.asarray(L0_attn_Wv, dtype=np.float32)),
        "wo": np.ascontiguousarray(np.asarray(L0_attn_Wo, dtype=np.float32)),
        "bo": np.ascontiguousarray(np.asarray(L0_attn_bo, dtype=np.float32).reshape(1, c.D)),
        "rw": np.ascontiguousarray(np.asarray(L0_ffn_router_w, dtype=np.float32)),
        "eb": np.ascontiguousarray(np.asarray(L0_ffn_expert_bias, dtype=np.float32).reshape(1, c.E)),
        "gate": np.ascontiguousarray(np.asarray(L0_ffn_gate_proj, dtype=np.float32)),
        "up": np.ascontiguousarray(np.asarray(L0_ffn_up_proj, dtype=np.float32)),
        "down": np.ascontiguousarray(np.asarray(L0_ffn_down_proj, dtype=np.float32)),
    }
    in_maps = [dict(shared, x=np.ascontiguousarray(x[i])) for i in range(_NCORES)]
    res = run_bass_kernel_spmd(nc, in_maps, core_ids=list(range(_NCORES)))
    return np.stack([r["y"] for r in res.results]).astype(np.float32)
